# version 03
# baseline (speedup 1.0000x reference)
import math
import numpy as np
import ml_dtypes

import concourse.bass as bass
import concourse.mybir as mybir
from concourse.bass_utils import run_bass_kernel_spmd

F32 = mybir.dt.float32
BF16 = mybir.dt.bfloat16
AF = mybir.ActivationFunctionType
ALU = mybir.AluOpType
AXL = mybir.AxisListType

BF = np.dtype(ml_dtypes.bfloat16)

D = 2048
NCH = 16
TOK = 1024
BLK = 512
NCORES = 8
EPS = 1e-5
LAMBDA_INIT = 0.8 - 0.6 * math.exp(-0.3 * 12)
L_GUARD = 1e-30

FUSED = False
DBG = None


ENGINES = ("pe", "dve", "act", "pool", "sp")
QUEUES = ("qsp", "qact", "qpool", "qcc")
QUEUE_ENGINE = {"qsp": "sp", "qact": "act", "qpool": "pool", "qcc": "pool"}


class Rect:
    __slots__ = ("key", "p0", "p1", "b0", "b1")

    def __init__(self, key, p0, p1, b0, b1):
        self.key, self.p0, self.p1, self.b0, self.b1 = key, p0, p1, b0, b1

    def overlaps(self, o):
        return (self.key == o.key and self.p0 < o.p1 and o.p0 < self.p1
                and self.b0 < o.b1 and o.b0 < self.b1)


class Op:
    __slots__ = ("engine", "fn", "reads", "writes", "queue", "eidx", "qidx",
                 "need_inc", "inc_count", "waits", "tag", "deps")

    def __init__(self, engine, fn, reads, writes, queue, tag=""):
        self.engine, self.fn = engine, fn
        self.reads, self.writes = reads, writes
        self.queue, self.tag = queue, tag
        self.eidx = -1
        self.qidx = -1
        self.need_inc = False
        self.inc_count = -1
        self.waits = {}
        self.deps = []


class Tile:
    __slots__ = ("handle", "key", "is_psum", "p0", "parts", "b0", "free",
                 "dt", "esz")

    def __init__(self, handle, key, is_psum, p0, parts, b0, free, dt):
        self.handle, self.key, self.is_psum = handle, key, is_psum
        self.p0, self.parts, self.b0, self.free, self.dt = p0, parts, b0, free, dt
        self.esz = 4 if dt == F32 else 2

    def sub(self, f0, f1):
        return Tile(self.handle, self.key, self.is_psum, self.p0, self.parts,
                    self.b0 + f0 * self.esz, f1 - f0, self.dt)

    def ap(self, p0=0, p1=None, f0=0, f1=None):
        if p1 is None:
            p1 = self.parts
        if f1 is None:
            f1 = self.free
        if self.is_psum:
            return self.handle[self.p0 + p0:self.p0 + p1, f0:f1]
        e0 = self.b0 // self.esz
        base = self.handle.ap()
        if self.dt != F32:
            base = base.bitcast(self.dt)
        return base[self.p0 + p0:self.p0 + p1, e0 + f0:e0 + f1]

    def rect(self, p0=0, p1=None, f0=0, f1=None):
        if p1 is None:
            p1 = self.parts
        if f1 is None:
            f1 = self.free
        return Rect(self.key, self.p0 + p0, self.p0 + p1,
                    self.b0 + f0 * self.esz, self.b0 + f1 * self.esz)

    def rv(self, p0=0, p1=None, f0=0, f1=None):
        return (self.ap(p0, p1, f0, f1), self.rect(p0, p1, f0, f1))


class Arena:
    def __init__(self, handle, key, nbytes):
        self.handle, self.key, self.nbytes = handle, key, nbytes
        self.off = 0
        self.marks = []
        self.peak = 0

    def alloc(self, parts, free, dt, align=64):
        esz = 4 if dt == F32 else 2
        sz = free * esz
        self.off = (self.off + align - 1) // align * align
        assert self.off + sz <= self.nbytes, \
            f"arena overflow: {self.off}+{sz} > {self.nbytes}"
        t = Tile(self.handle, self.key, False, 0, parts, self.off, free, dt)
        self.off += sz
        self.peak = max(self.peak, self.off)
        return t

    def mark(self):
        self.marks.append(self.off)

    def release(self):
        self.off = self.marks.pop()


def drect(name, r0=0, r1=1 << 30):
    return Rect("D:" + name, 0, 1, r0, r1)


class Prog:
    def __init__(self):
        self.ops = {e: [] for e in ENGINES}
        self.qcount = {q: 0 for q in QUEUES}
        self.order = []
        self.regions = {}
        self.last_q = {}

    def _dep_scan(self, op):
        deps = []
        for r in op.reads:
            for ent in self.regions.get(r.key, ()):
                if ent[1] is not None and ent[0].overlaps(r):
                    deps.append(ent[1])
        for w in op.writes:
            for ent in self.regions.get(w.key, ()):
                if ent[0].overlaps(w):
                    if ent[1] is not None:
                        deps.append(ent[1])
                    deps.extend(ent[2])
        return deps

    def _record(self, op):
        for w in op.writes:
            lst = self.regions.setdefault(w.key, [])
            keep = [ent for ent in lst
                    if not (ent[0].p0 >= w.p0 and ent[0].p1 <= w.p1
                            and ent[0].b0 >= w.b0 and ent[0].b1 <= w.b1)]
            keep.append((w, op, []))
            self.regions[w.key] = keep
        for r in op.reads:
            lst = self.regions.setdefault(r.key, [])
            for ent in lst:
                if ent[0].overlaps(r):
                    ent[2].append(op)

    def add(self, engine, fn, reads=(), writes=(), queue=None, tag=""):
        op = Op(engine, fn, list(reads), list(writes), queue, tag)
        op.deps = self._dep_scan(op)
        self._record(op)
        op.eidx = len(self.ops[engine])
        self.ops[engine].append(op)
        if queue is not None:
            op.qidx = self.qcount[queue]
            self.qcount[queue] += 1
            prev = self.last_q.get(queue)
            if prev is not None:
                op.deps.append(prev)
            self.last_q[queue] = op
        self.order.append(op)
        return op

    def finalize(self):
        for op in self.order:
            for d in op.deps:
                if d is op or d.queue is not None:
                    continue
                if d.engine == "pe" and op.engine == "pe":
                    continue
                d.need_inc = True
        counts = {e: 0 for e in ENGINES}
        for e in ENGINES:
            for op in self.ops[e]:
                if op.queue is None and op.need_inc:
                    counts[e] += 1
                    op.inc_count = counts[e]
        for op in self.order:
            waits = {}
            for d in op.deps:
                if d is op:
                    continue
                if d.queue is not None:
                    sem, val = d.queue, 16 * (d.qidx + 1)
                elif d.engine == "pe" and op.engine == "pe":
                    continue
                else:
                    if d.inc_count < 0:
                        continue
                    sem, val = d.engine, d.inc_count
                if waits.get(sem, 0) < val:
                    waits[sem] = val
            op.waits = waits
        final = {}
        for e in ENGINES:
            seen = {}
            lst = []
            for op in self.ops[e]:
                ws = []
                for sem, val in sorted(op.waits.items()):
                    if seen.get(sem, 0) >= val:
                        continue
                    seen[sem] = val
                    ws.append((sem, val))
                lst.append((op, ws))
            final[e] = lst
        return final



class Ctx:
    pass


def mm(k, out_rv, lhs_rv, rhs_rv, start=True, stop=True, tag=""):
    (oap, orect), (lap, lrect), (rap, rrect) = out_rv, lhs_rv, rhs_rv
    k.P.add("pe", lambda e: e.matmul(oap, lap, rap, start=start, stop=stop),
            reads=[lrect, rrect], writes=[orect], tag=tag)


def transpose(k, out_rv, in_rv, id_rv, tag=""):
    (oap, orect), (iap, irect), (dap, dr) = out_rv, in_rv, id_rv
    k.P.add("pe", lambda e: e.transpose(oap, iap, dap),
            reads=[irect, dr], writes=[orect], tag=tag)


def act(k, out_rv, in_rv, func, bias=None, scale=None, tag=""):
    (oap, orect), (iap, irect) = out_rv, in_rv
    reads = [irect]
    kw = {}
    if bias is not None:
        if isinstance(bias, tuple):
            kw["bias"] = bias[0]
            reads.append(bias[1])
        else:
            kw["bias"] = bias
    if scale is not None:
        kw["scale"] = scale
    k.P.add("act", lambda e: e.activation(oap, iap, func, **kw),
            reads=reads, writes=[orect], tag=tag)


def tt(k, out_rv, a_rv, b_rv, op, tag=""):
    (oap, orect), (aap, ar), (bap, br) = out_rv, a_rv, b_rv
    k.P.add("dve", lambda e: e.tensor_tensor(oap, aap, bap, op),
            reads=[ar, br], writes=[orect], tag=tag)


def ts(k, out_rv, in_rv, s1, op0, s2=None, op1=None, tag=""):
    (oap, orect), (iap, irect) = out_rv, in_rv
    reads = [irect]
    if isinstance(s1, tuple):
        s1v = s1[0]
        reads.append(s1[1])
    else:
        s1v = s1
    s2v = None
    if s2 is not None:
        if isinstance(s2, tuple):
            s2v = s2[0]
            reads.append(s2[1])
        else:
            s2v = s2

    def fn(e):
        if op1 is None:
            return e.tensor_scalar(oap, iap, s1v, None, op0)
        return e.tensor_scalar(oap, iap, s1v, s2v, op0, op1)
    k.P.add("dve", fn, reads=reads, writes=[orect], tag=tag)


def copy(k, out_rv, in_rv, engine="dve", tag=""):
    (oap, orect), (iap, irect) = out_rv, in_rv
    if engine == "dve":
        k.P.add("dve", lambda e: e.tensor_copy(oap, iap),
                reads=[irect], writes=[orect], tag=tag)
    else:
        k.P.add("act", lambda e: e.activation(oap, iap, AF.Copy),
                reads=[irect], writes=[orect], tag=tag)


def reduce(k, out_rv, in_rv, op, negate=False, tag=""):
    (oap, orect), (iap, irect) = out_rv, in_rv
    k.P.add("dve",
            lambda e: e.tensor_reduce(oap, iap, axis=AXL.X, op=op,
                                      negate=negate),
            reads=[irect], writes=[orect], tag=tag)


def recip(k, out_rv, in_rv, tag=""):
    (oap, orect), (iap, irect) = in_rv if False else (out_rv, in_rv)
    k.P.add("dve", lambda e: e.reciprocal(oap, iap),
            reads=[irect], writes=[orect], tag=tag)


def memset(k, out_rv, val, tag=""):
    (oap, orect) = out_rv
    k.P.add("dve", lambda e: e.memset(oap, val), reads=[], writes=[orect],
            tag=tag)


def dma(k, queue, out_rv, in_rv, tag=""):
    (oap, orect), (iap, irect) = out_rv, in_rv
    if iap.ndim == 3 and oap.ndim == 2:
        oap = oap.rearrange("p (c t) -> p c t", c=iap.shape[1])
    elif oap.ndim == 3 and iap.ndim == 2:
        iap = iap.rearrange("p (c t) -> p c t", c=oap.shape[1])
    k.P.add(QUEUE_ENGINE[queue], lambda e: e.dma_start(out=oap, in_=iap),
            reads=[irect], writes=[orect], queue=queue, tag=tag)


def collective(k, kind, op, in_rv, out_rv, tag=""):
    (iap, irect), (oap, orect) = in_rv, out_rv
    rg = [list(range(NCORES))]
    k.P.add("pool",
            lambda e: e.collective_compute(kind, op, replica_groups=rg,
                                           ins=[iap.opt()], outs=[oap.opt()]),
            reads=[irect], writes=[orect], queue="qcc", tag=tag)


def bcast_row(k, psum_tile, row_rv, n, sb_out, tag=""):
    mm(k, psum_tile.rv(0, 128, 0, n), k.ones_row_f32.rv(), row_rv, tag=tag)
    copy(k, sb_out.rv(0, 128, 0, n), psum_tile.rv(0, 128, 0, n), tag=tag)
    return sb_out



def emit_ln(k, gate=False):
    for b in range(2):
        c0 = b * BLK
        p_sx, p_sq = k.pb[4], k.pb[5]
        for c in range(NCH):
            xc = k.x_res.rv(0, 128, c * TOK + c0, c * TOK + c0 + BLK)
            sq = k.scr[c % 2]
            act(k, sq.rv(0, 128, 0, BLK), xc, AF.Square, tag="ln_sq")
            mm(k, p_sx.rv(0, 1, 0, BLK), k.ones_col_f32.rv(), xc,
               start=(c == 0), stop=(c == NCH - 1), tag="ln_sx")
            mm(k, p_sq.rv(0, 1, 0, BLK), k.ones_col_f32.rv(),
               sq.rv(0, 128, 0, BLK),
               start=(c == 0), stop=(c == NCH - 1), tag="ln_sq2")
        mu, ssn, var, rstd = k.rows[0], k.rows[1], k.rows[2], k.rows[3]
        act(k, mu.rv(0, 1, 0, BLK), p_sx.rv(0, 1, 0, BLK), AF.Copy,
            scale=1.0 / D, tag="ln")
        act(k, ssn.rv(0, 1, 0, BLK), p_sq.rv(0, 1, 0, BLK), AF.Copy,
            scale=1.0 / D, tag="ln")
        act(k, var.rv(0, 1, 0, BLK), mu.rv(0, 1, 0, BLK), AF.Square,
            tag="ln")
        tt(k, var.rv(0, 1, 0, BLK), ssn.rv(0, 1, 0, BLK),
           var.rv(0, 1, 0, BLK), ALU.subtract, tag="ln")
        act(k, rstd.rv(0, 1, 0, BLK), var.rv(0, 1, 0, BLK), AF.Sqrt,
            bias=(k.eps_col.ap(0, 1, 0, 1), k.eps_col.rect(0, 1, 0, 1)),
            tag="ln")
        recip(k, rstd.rv(0, 1, 0, BLK), rstd.rv(0, 1, 0, BLK), tag="ln")
        tt(k, mu.rv(0, 1, 0, BLK), mu.rv(0, 1, 0, BLK),
           rstd.rv(0, 1, 0, BLK), ALU.mult, tag="ln")
        bc_r = bcast_row(k, k.pb[4], rstd.rv(0, 1, 0, BLK), BLK, k.bc1)
        bc_m = bcast_row(k, k.pb[5], mu.rv(0, 1, 0, BLK), BLK, k.bc2)
        for c in range(NCH):
            xc = k.x_res.rv(0, 128, c * TOK + c0, c * TOK + c0 + BLK)
            t = k.scr[c % 2]
            tt(k, t.rv(0, 128, 0, BLK), xc, bc_r.rv(0, 128, 0, BLK),
               ALU.mult, tag="ln_ap1")
            lnc = k.ln_x.rv(0, 128, c * TOK + c0, c * TOK + c0 + BLK)
            if not gate:
                tt(k, lnc, t.rv(0, 128, 0, BLK), bc_m.rv(0, 128, 0, BLK),
                   ALU.subtract, tag="ln_ap2")
            else:
                lf = k.scr[2]
                tt(k, lf.rv(0, 128, 0, BLK), t.rv(0, 128, 0, BLK),
                   bc_m.rv(0, 128, 0, BLK), ALU.subtract, tag="ln_ap2")
                copy(k, lnc, lf.rv(0, 128, 0, BLK), tag="ln_cp")
                mm(k, k.pb[6].rv(0, 8, 0, BLK),
                   k.gate_sb.rv(0, 128, c * 8, (c + 1) * 8),
                   lf.rv(0, 128, 0, BLK),
                   start=(c == 0), stop=(c == NCH - 1), tag="gate_mm")
        if gate:
            act(k, k.sc_fm[b].rv(0, 8, 0, BLK), k.pb[6].rv(0, 8, 0, BLK),
                AF.Sigmoid, tag="gate_sig")


def emit_ln_blk(k, xsrc, dst):
    p_sx, p_sq = k.pb[4], k.pb[5]
    for c in range(NCH):
        xc = xsrc.rv(0, 128, c * BLK, (c + 1) * BLK)
        sq = k.scr[c % 2]
        act(k, sq.rv(0, 128, 0, BLK), xc, AF.Square, tag="lnb_sq")
        mm(k, p_sx.rv(0, 1, 0, BLK), k.ones_col_f32.rv(), xc,
           start=(c == 0), stop=(c == NCH - 1), tag="lnb_sx")
        mm(k, p_sq.rv(0, 1, 0, BLK), k.ones_col_f32.rv(),
           sq.rv(0, 128, 0, BLK),
           start=(c == 0), stop=(c == NCH - 1), tag="lnb_sq2")
    mu, ssn, var, rstd = k.rows[0], k.rows[1], k.rows[2], k.rows[3]
    act(k, mu.rv(0, 1, 0, BLK), p_sx.rv(0, 1, 0, BLK), AF.Copy,
        scale=1.0 / D, tag="lnb")
    act(k, ssn.rv(0, 1, 0, BLK), p_sq.rv(0, 1, 0, BLK), AF.Copy,
        scale=1.0 / D, tag="lnb")
    act(k, var.rv(0, 1, 0, BLK), mu.rv(0, 1, 0, BLK), AF.Square, tag="lnb")
    tt(k, var.rv(0, 1, 0, BLK), ssn.rv(0, 1, 0, BLK), var.rv(0, 1, 0, BLK),
       ALU.subtract, tag="lnb")
    act(k, rstd.rv(0, 1, 0, BLK), var.rv(0, 1, 0, BLK), AF.Sqrt,
        bias=(k.eps_col.ap(0, 1, 0, 1), k.eps_col.rect(0, 1, 0, 1)),
        tag="lnb")
    recip(k, rstd.rv(0, 1, 0, BLK), rstd.rv(0, 1, 0, BLK), tag="lnb")
    tt(k, mu.rv(0, 1, 0, BLK), mu.rv(0, 1, 0, BLK), rstd.rv(0, 1, 0, BLK),
       ALU.mult, tag="lnb")
    bc_r = bcast_row(k, k.pb[4], rstd.rv(0, 1, 0, BLK), BLK, k.bc1)
    bc_m = bcast_row(k, k.pb[5], mu.rv(0, 1, 0, BLK), BLK, k.bc2)
    for c in range(NCH):
        xc = xsrc.rv(0, 128, c * BLK, (c + 1) * BLK)
        t = k.scr[c % 2]
        tt(k, t.rv(0, 128, 0, BLK), xc, bc_r.rv(0, 128, 0, BLK), ALU.mult,
           tag="lnb_ap1")
        tt(k, dst.rv(0, 128, c * BLK, (c + 1) * BLK), t.rv(0, 128, 0, BLK),
           bc_m.rv(0, 128, 0, BLK), ALU.subtract, tag="lnb_ap2")


def emit_moe(k, W):
    A = k.arena
    A.mark()
    k.gate_sb = A.alloc(128, NCH * 8, F32)
    dma(k, "qsp", k.gate_sb.rv(), W["gw"])
    ebr = A.alloc(1, 8, F32)
    dma(k, "qsp", ebr.rv(), W["eb"])
    k.sel8 = A.alloc(8, 1024, F32)
    dma(k, "qsp", k.sel8.rv(), W["sel8"])
    k.ebias_bc = A.alloc(128, 8, F32)
    bcast_row(k, k.pb[6], ebr.rv(0, 1, 0, 8), 8, k.ebias_bc)
    k.ln_x = A.alloc(128, NCH * TOK, BF16)
    k.sc_fm = [A.alloc(8, BLK, F32) for _ in range(2)]
    emit_ln(k, gate=True)

    scale_fm = [A.alloc(8, BLK, F32) for _ in range(2)]
    sc_tm = A.alloc(128, 4 * 8, F32)
    bt = A.alloc(128, 8, F32)
    selt = A.alloc(128, 8, F32)
    m1 = A.alloc(128, 1, F32)
    m2 = A.alloc(128, 1, F32)
    for b in range(2):
        for j in range(4):
            transpose(k, k.pb[6].rv(0, 128, j * 8, (j + 1) * 8),
                      k.sc_fm[b].rv(0, 8, j * 128, (j + 1) * 128),
                      k.ident.rv(0, 8, 0, 8), tag="sc_T")
        copy(k, sc_tm.rv(), k.pb[6].rv(0, 128, 0, 32), tag="sc_T_ev")
        for j in range(4):
            stj = sc_tm.rv(0, 128, j * 8, (j + 1) * 8)
            tt(k, bt.rv(), stj, k.ebias_bc.rv(), ALU.add, tag="top2")
            reduce(k, m1.rv(), bt.rv(), ALU.max, tag="top2")
            ts(k, selt.rv(), bt.rv(), (m1.ap(), m1.rect()), ALU.is_ge,
               1e30, ALU.mult, tag="top2")
            tt(k, selt.rv(), bt.rv(), selt.rv(), ALU.subtract, tag="top2")
            reduce(k, m2.rv(), selt.rv(), ALU.max, tag="top2")
            ts(k, selt.rv(), bt.rv(), (m2.ap(), m2.rect()), ALU.is_ge,
               tag="top2")
            tt(k, selt.rv(), selt.rv(), stj, ALU.mult, tag="top2")
            transpose(k, k.pb[6].rv(0, 8, j * 128, (j + 1) * 128),
                      selt.rv(), k.ident.rv(), tag="sc_Tb")
        copy(k, scale_fm[b].rv(), k.pb[6].rv(0, 8, 0, BLK), tag="sc_Tb_ev")

    he = A.alloc(128, 8 * TOK, BF16)
    w1t = [A.alloc(128, NCH * 128, BF16) for _ in range(2)]
    w3t = [A.alloc(128, NCH * 128, BF16) for _ in range(2)]
    w2t = [A.alloc(128, 8 * 128, BF16) for _ in range(2)]
    sbc = [A.alloc(128, BLK, F32) for _ in range(2)]
    t1 = A.alloc(128, BLK, BF16)
    t3 = A.alloc(128, BLK, BF16)
    s1t = A.alloc(128, BLK, BF16)

    def expert(e, w1_rv, w3_rv, w2_rv, routed):
        if routed:
            for b in range(2):
                mm(k, k.pb[6].rv(0, 128, 0, BLK),
                   k.sel8.rv(0, 8, e * 128, (e + 1) * 128),
                   scale_fm[b].rv(0, 8, 0, BLK), tag="sbc_mm")
                copy(k, sbc[b].rv(), k.pb[6].rv(0, 128, 0, BLK),
                     tag="sbc_ev")
        for ht in range(8):
            dma(k, "qpool", w1t[ht % 2].rv(), w1_rv[ht], tag="w1dma")
            dma(k, "qpool", w3t[ht % 2].rv(), w3_rv[ht], tag="w3dma")
            for b in range(2):
                c0 = b * BLK
                ph1 = k.pb[0 + (b % 2)]
                ph3 = k.pb[2 + (b % 2)]
                for dc in range(NCH):
                    rhs = k.ln_x.rv(0, 128, dc * TOK + c0,
                                    dc * TOK + c0 + BLK)
                    mm(k, ph1.rv(),
                       w1t[ht % 2].rv(0, 128, dc * 128, (dc + 1) * 128),
                       rhs, start=(dc == 0), stop=(dc == NCH - 1), tag="h1")
                    mm(k, ph3.rv(),
                       w3t[ht % 2].rv(0, 128, dc * 128, (dc + 1) * 128),
                       rhs, start=(dc == 0), stop=(dc == NCH - 1), tag="h3")
                hslot = he.rv(0, 128, ht * TOK + c0, ht * TOK + c0 + BLK)
                if routed:
                    tt(k, t1.rv(), ph1.rv(), sbc[b].rv(), ALU.mult,
                       tag="h1ev")
                    tt(k, t3.rv(), ph3.rv(), sbc[b].rv(), ALU.mult,
                       tag="h3ev")
                    act(k, s1t.rv(), t1.rv(), AF.Silu, tag="silu")
                    tt(k, hslot, s1t.rv(), t3.rv(), ALU.mult, tag="he")
                else:
                    act(k, s1t.rv(), ph1.rv(), AF.Silu, tag="silu")
                    copy(k, t3.rv(), ph3.rv(), tag="h3ev")
                    tt(k, hslot, s1t.rv(), t3.rv(), ALU.mult, tag="he")
        for ot in range(NCH):
            dma(k, "qpool", w2t[ot % 2].rv(), w2_rv[ot], tag="w2dma")
            for b in range(2):
                c0 = b * BLK
                po = k.pb[4 + (b % 2)]
                for hc in range(8):
                    mm(k, po.rv(),
                       w2t[ot % 2].rv(0, 128, hc * 128, (hc + 1) * 128),
                       he.rv(0, 128, hc * TOK + c0, hc * TOK + c0 + BLK),
                       start=(hc == 0), stop=(hc == 7), tag="w2")
                tmp = k.scr[b % 2]
                act(k, tmp.rv(0, 128, 0, BLK), po.rv(), AF.Copy, scale=0.5,
                    tag="oev")
                xc = k.x_res.rv(0, 128, ot * TOK + c0, ot * TOK + c0 + BLK)
                tt(k, xc, tmp.rv(0, 128, 0, BLK), xc, ALU.add, tag="oadd")

    for e in range(8):
        expert(e, W["w1"][e], W["w3"][e], W["w2"][e], True)
    expert(1, W["sw1"], W["sw3"], W["sw2"], False)
    A.release()


def _rope(k, dst_rv, psum, rot, ctab, stab):
    act(k, k.sbqr.rv(), psum.rv(), AF.Copy, tag="rope_ev")
    mm(k, k.pb[7].rv(), rot.rv(), k.sbqr.rv(), tag="rope_mm")
    tt(k, k.sbta.rv(), k.sbqr.rv(), ctab.rv(0, 128, 0, BLK), ALU.mult,
       tag="rope_c")
    tt(k, k.sbtb.rv(), k.pb[7].rv(), stab.rv(0, 128, 0, BLK), ALU.mult,
       tag="rope_s")
    tt(k, dst_rv, k.sbta.rv(), k.sbtb.rv(), ALU.add, tag="rope_a")


def emit_l0attn(k, W):
    A = k.arena
    A.mark()
    q_sb = A.alloc(128, NCH * TOK, BF16)
    qpos = A.alloc(1, BLK, F32)
    kpos = A.alloc(128, 32, F32)
    dma(k, "qsp", qpos.rv(), W["qpos"])
    dma(k, "qsp", kpos.rv(), W["kpos"])

    A.mark()
    xtmp = A.alloc(128, NCH * BLK, F32)
    lnt = A.alloc(128, NCH * BLK, BF16)
    for s in range(2):
        for cp in range(NCORES):
            dma(k, k.xf_q, xtmp.rv(), k.xf_block(s, cp), tag="xfln")
            emit_ln_blk(k, xtmp, lnt)
            dma(k, "qsp", k.lnf_block(s, cp), lnt.rv(), tag="lnstore")
    A.release()

    A.mark()
    rot0 = A.alloc(128, 128, BF16)
    bk_sb = A.alloc(1, 512, BF16)
    bv_sb = A.alloc(1, 512, BF16)
    dma(k, "qsp", rot0.rv(), W["rot0"])
    dma(k, "qsp", bk_sb.rv(), W["bk"])
    dma(k, "qsp", bv_sb.rv(), W["bv"])
    r0 = {}
    for n in ("r0kca", "r0ksa"):
        r0[n] = A.alloc(128, 8 * BLK, BF16)
        dma(k, "qsp", r0[n].rv(), W[n])
    lnt = A.alloc(128, NCH * BLK, BF16)
    wkv = A.alloc(128, NCH * 512, BF16)
    cst = A.alloc(128, BLK, BF16)
    dma(k, "qsp", wkv.rv(), W["wk"], tag="wk")
    for s in range(2):
        for cp in range(NCORES):
            dma(k, "qsp", lnt.rv(), k.lnf_block(s, cp), tag="lnload")
            for kvh in range(4):
                pq = k.pb[0 + (kvh % 2)]
                for dc in range(NCH):
                    mm(k, pq.rv(),
                       wkv.rv(0, 128, dc * 512 + kvh * 128,
                              dc * 512 + (kvh + 1) * 128),
                       lnt.rv(0, 128, dc * BLK, (dc + 1) * BLK),
                       start=(dc == 0), stop=False, tag="kproj")
                mm(k, pq.rv(), bk_sb.rv(0, 1, kvh * 128, (kvh + 1) * 128),
                   k.ones512_bf.rv(), start=False, stop=True, tag="kbias")
                _rope(k, cst.rv(), pq, rot0,
                      r0["r0kca"].sub(cp * BLK, (cp + 1) * BLK),
                      r0["r0ksa"].sub(cp * BLK, (cp + 1) * BLK))
                rr = cp * 2048 + s * 1024 + kvh * 128
                dma(k, "qsp", (k.kv0_g[rr:rr + 128, :],
                               drect("kv0_g", rr, rr + 128)),
                    cst.rv(), tag="kstore")
    dma(k, "qsp", wkv.rv(), W["wv"], tag="wv")
    for s in range(2):
        for cp in range(NCORES):
            dma(k, "qsp", lnt.rv(), k.lnf_block(s, cp), tag="lnloadv")
            for tt_i in range(4):
                pv = k.pb[0 + (tt_i % 2)]
                for dc in range(NCH):
                    mm(k, pv.rv(),
                       lnt.rv(0, 128, dc * BLK + tt_i * 128,
                              dc * BLK + (tt_i + 1) * 128),
                       wkv.rv(0, 128, dc * 512, (dc + 1) * 512),
                       start=(dc == 0), stop=False, tag="vproj"),
                mm(k, pv.rv(), k.ones_row_bf.rv(), bv_sb.rv(0, 1, 0, 512),
                   start=False, stop=True, tag="vbias")
                act(k, cst.rv(), pv.rv(), AF.Copy, tag="vev")
                rr = cp * 2048 + s * 1024 + 512 + tt_i * 128
                dma(k, "qsp", (k.kv0_g[rr:rr + 128, :],
                               drect("kv0_g", rr, rr + 128)),
                    cst.rv(), tag="vstore")
    A.release()
    A.mark()
    rot0 = A.alloc(128, 128, BF16)
    dma(k, "qsp", rot0.rv(), W["rot0"])
    r0 = {}
    k.ln_x = A.alloc(128, NCH * TOK, BF16)
    emit_ln(k)
    ln_own = k.ln_x
    bq_sb = A.alloc(1, D, BF16)
    dma(k, "qsp", bq_sb.rv(), W["bq"])
    for n in ("r0qc", "r0qs"):
        r0[n] = A.alloc(128, BLK, BF16)
        dma(k, "qsp", r0[n].rv(), W[n])
    wst = [A.alloc(128, NCH * 128, BF16) for _ in range(2)]
    for oc in range(NCH):
        dma(k, "qsp", wst[oc % 2].rv(), W["wq"][oc], tag="wqdma")
        for s in range(2):
            pq = k.pb[0 + (s % 2)]
            for dc in range(NCH):
                mm(k, pq.rv(),
                   wst[oc % 2].rv(0, 128, dc * 128, (dc + 1) * 128),
                   ln_own.rv(0, 128, dc * TOK + s * BLK,
                             dc * TOK + s * BLK + BLK),
                   start=(dc == 0), stop=False, tag="qproj")
            mm(k, pq.rv(), bq_sb.rv(0, 1, oc * 128, (oc + 1) * 128),
               k.ones512_bf.rv(), start=False, stop=True, tag="qbias")
            _rope(k, q_sb.rv(0, 128, oc * TOK + s * BLK,
                             oc * TOK + s * BLK + BLK),
                  pq, rot0, r0["r0qc"], r0["r0qs"])
    A.release()

    A.mark()
    attn_fm = A.alloc(128, NCH * TOK, BF16)
    A.mark()
    masks = A.alloc(128, 32 * BLK, BF16)
    qpb = k.scr[2]
    bcast_row(k, k.pb[6], qpos.rv(0, 1, 0, BLK), BLK, qpb, tag="qp_bc")
    for kt in range(32):
        ts(k, masks.rv(0, 128, kt * BLK, (kt + 1) * BLK),
           qpb.rv(0, 128, 0, BLK),
           (kpos.ap(0, 128, kt, kt + 1), kpos.rect(0, 128, kt, kt + 1)),
           ALU.is_ge, tag="mask")
    ksb = A.alloc(128, 4096, BF16)
    vsb = A.alloc(128, 4096, BF16)
    pt = [A.alloc(128, BLK, BF16) for _ in range(2)]
    for s in range(2):
        for kvh in range(4):
            for r in range(NCORES):
                rr = r * 2048 + s * 1024 + kvh * 128
                dma(k, "qsp", ksb.rv(0, 128, r * 512, (r + 1) * 512),
                    (k.kv0_g[rr:rr + 128, :], drect("kv0_g", rr, rr + 128)),
                    tag="kload")
                r1 = r * 2048 + s * 1024 + 512
                dma(k, "qsp", vsb.rv(0, 128, r * 512, (r + 1) * 512),
                    (k.kv0_g[r1:r1 + 512, kvh * 128:(kvh + 1) * 128]
                     .rearrange("(a p) v -> p a v", p=128),
                     drect("kv0_g", r1, r1 + 512)),
                    tag="vload")
            _max_k2(k, ksb, 0, 128)
            for g in range(4):
                qh = kvh * 4 + g
                qrv = q_sb.rv(0, 128, qh * TOK + s * BLK,
                              qh * TOK + s * BLK + BLK)
                _strip(k, qrv, ksb, vsb, 0, 128, masks, pt,
                       attn_fm.rv(0, 128, qh * TOK + s * BLK,
                                  qh * TOK + s * BLK + BLK),
                       k.pb[4], k.pb[5])
    A.release()
    wst2 = [A.alloc(128, NCH * 128, BF16) for _ in range(2)]
    for oc in range(NCH):
        dma(k, "qsp", wst2[oc % 2].rv(), W["wo"][oc], tag="wodma")
        for b in range(2):
            po = k.pb[0 + (b % 2)]
            for ac_ in range(NCH):
                mm(k, po.rv(),
                   wst2[oc % 2].rv(0, 128, ac_ * 128, (ac_ + 1) * 128),
                   attn_fm.rv(0, 128, ac_ * TOK + b * BLK,
                              ac_ * TOK + b * BLK + BLK),
                   start=(ac_ == 0), stop=(ac_ == NCH - 1), tag="wo")
            xc = k.x_res.rv(0, 128, oc * TOK + b * BLK,
                            oc * TOK + b * BLK + BLK)
            tt(k, xc, po.rv(), xc, ALU.add, tag="wo_add")
    A.release()
    A.release()


def _max_k2(k, ksb, p0, p1):
    mk2 = k.rows[4]
    cm = k.rows[3]
    for kc in range(8):
        sq = k.scr[0]
        act(k, sq.rv(p0, p1, 0, BLK),
            ksb.rv(p0, p1, kc * 512, (kc + 1) * 512), AF.Square, tag="nk_sq")
        mm(k, k.pb[6].rv(0, 1, 0, BLK), k.ones_col_f32.rv(p0, p1, 0, 1),
           sq.rv(p0, p1, 0, BLK), tag="nk_mm")
        if kc == 0:
            reduce(k, mk2.rv(0, 1, 0, 1), k.pb[6].rv(0, 1, 0, BLK),
                   ALU.max, tag="nk_red")
        else:
            reduce(k, cm.rv(0, 1, 0, 1), k.pb[6].rv(0, 1, 0, BLK),
                   ALU.max, tag="nk_red")
            tt(k, mk2.rv(0, 1, 0, 1), mk2.rv(0, 1, 0, 1), cm.rv(0, 1, 0, 1),
               ALU.max, tag="nk_max")


def _strip(k, q_rv, ksb, vsb, p_off, kdim, masks, pt, out_rv, p_o, p_l):
    p1 = p_off + kdim
    sqs = k.scr[0]
    act(k, sqs.rv(p_off, p1, 0, BLK), q_rv, AF.Square, tag="nq_sq")
    mm(k, k.pb[6].rv(0, 1, 0, BLK), k.ones_col_f32.rv(p_off, p1, 0, 1),
       sqs.rv(p_off, p1, 0, BLK), tag="nq_mm")
    nb = k.rows[2]
    ts(k, nb.rv(0, 1, 0, BLK), k.pb[6].rv(0, 1, 0, BLK),
       (k.rows[4].ap(0, 1, 0, 1), k.rows[4].rect(0, 1, 0, 1)), ALU.mult,
       tag="bound")
    act(k, nb.rv(0, 1, 0, BLK), nb.rv(0, 1, 0, BLK), AF.Sqrt, tag="bound")
    reduce(k, nb.rv(0, 1, 0, 1), nb.rv(0, 1, 0, BLK), ALU.max, negate=True,
           tag="bound")
    mm(k, k.pb[6].rv(0, 128, 0, 1), k.ones_row_f32.rv(), nb.rv(0, 1, 0, 1),
       tag="bound_bc")
    copy(k, k.nb_col.rv(), k.pb[6].rv(0, 128, 0, 1), tag="bound_ev")

    def s_mm(kt):
        p_s = k.pb[0 + (kt % 2)]
        mm(k, p_s.rv(),
           ksb.rv(p_off, p1, kt * 128, (kt + 1) * 128), q_rv, tag="s_mm")

    s_mm(0)
    for kt in range(32):
        if kt + 1 < 32:
            s_mm(kt + 1)
        p_s = k.pb[0 + (kt % 2)]
        act(k, pt[kt % len(pt)].rv(), p_s.rv(), AF.Exp,
            bias=(k.nb_col.ap(), k.nb_col.rect()), tag="exp")
        if masks is not None:
            tt(k, pt[kt % len(pt)].rv(), pt[kt % len(pt)].rv(),
               masks.rv(0, 128, kt * BLK, (kt + 1) * BLK), ALU.mult,
               tag="pmask")
        mm(k, p_l.rv(0, 1, 0, BLK), k.ones_col_bf.rv(), pt[kt % len(pt)].rv(),
           start=(kt == 0), stop=(kt == 31), tag="l_mm")
        mm(k, p_o.rv(), vsb.rv(0, 128, kt * 128, (kt + 1) * 128),
           pt[kt % len(pt)].rv(), start=(kt == 0), stop=(kt == 31), tag="pv")
    if out_rv is None:
        return
    rl = k.rows[2]
    ts(k, rl.rv(0, 1, 0, BLK), p_l.rv(0, 1, 0, BLK), L_GUARD, ALU.add,
       tag="lg")
    recip(k, rl.rv(0, 1, 0, BLK), rl.rv(0, 1, 0, BLK), tag="rl")
    bcast_row(k, k.pb[6], rl.rv(0, 1, 0, BLK), BLK, k.bc1, tag="rl_bc")
    tt(k, out_rv, p_o.rv(), k.bc1.rv(0, 128, 0, BLK), ALU.mult, tag="onorm")


def emit_l1ffn2(k, W):
    A = k.arena
    A.mark()
    k.ln_x = A.alloc(128, NCH * TOK, BF16)
    emit_ln(k)
    g1 = A.alloc(128, BLK, BF16)
    u1 = A.alloc(128, BLK, BF16)
    for half in range(2):
        A.mark()
        gated = A.alloc(128, 32 * TOK, BF16)
        A.mark()
        wg = [A.alloc(128, NCH * 128, BF16) for _ in range(2)]
        wu = [A.alloc(128, NCH * 128, BF16) for _ in range(2)]
        for sl in range(32):
            gsl = half * 32 + sl
            dma(k, "qpool", wg[sl % 2].rv(), W["wg"][gsl], tag="wgdma")
            dma(k, "qpool", wu[sl % 2].rv(), W["wu"][gsl], tag="wudma")
            for b in range(2):
                pg = k.pb[0 + (b % 2)]
                pu = k.pb[2 + (b % 2)]
                for dc in range(NCH):
                    rhs = k.ln_x.rv(0, 128, dc * TOK + b * BLK,
                                    dc * TOK + b * BLK + BLK)
                    mm(k, pg.rv(),
                       wg[sl % 2].rv(0, 128, dc * 128, (dc + 1) * 128),
                       rhs, start=(dc == 0), stop=(dc == NCH - 1), tag="g")
                    mm(k, pu.rv(),
                       wu[sl % 2].rv(0, 128, dc * 128, (dc + 1) * 128),
                       rhs, start=(dc == 0), stop=(dc == NCH - 1), tag="u")
                act(k, g1.rv(), pg.rv(), AF.Silu, tag="silu")
                copy(k, u1.rv(), pu.rv(), tag="uev")
                tt(k, gated.rv(0, 128, sl * TOK + b * BLK,
                               sl * TOK + b * BLK + BLK),
                   g1.rv(), u1.rv(), ALU.mult, tag="gu")
        A.release()
        wd = [A.alloc(128, 32 * 128, BF16) for _ in range(2)]
        for ot in range(NCH):
            dma(k, "qpool", wd[ot % 2].rv(), W["wd"][half][ot], tag="wddma")
            for b in range(2):
                po = k.pb[4 + (b % 2)]
                for hc in range(32):
                    mm(k, po.rv(),
                       wd[ot % 2].rv(0, 128, hc * 128, (hc + 1) * 128),
                       gated.rv(0, 128, hc * TOK + b * BLK,
                                hc * TOK + b * BLK + BLK),
                       start=(hc == 0), stop=(hc == 31), tag="wd")
                tmp = k.scr[b % 2]
                act(k, tmp.rv(0, 128, 0, BLK), po.rv(), AF.Copy, scale=0.5,
                    tag="dev")
                xc = k.x_res.rv(0, 128, ot * TOK + b * BLK,
                                ot * TOK + b * BLK + BLK)
                tt(k, xc, tmp.rv(0, 128, 0, BLK), xc, ALU.add, tag="dadd")
        A.release()
    A.release()


def emit_l1attn(k, W):
    A = k.arena
    A.mark()
    rot1 = A.alloc(128, 128, BF16)
    dma(k, "qsp", rot1.rv(), W["rot1"])
    lamv = {}
    for n in ("lq1", "lk1", "lq2", "lk2"):
        lamv[n] = A.alloc(1, 64, F32)
        dma(k, "qsp", lamv[n].rv(), W[n])
    gam = A.alloc(128, 1, F32)
    dma(k, "qsp", gam.rv(), W["gamc"])
    lam = A.alloc(1, 1, F32)
    g1l = A.alloc(128, 1, F32)
    r = k.rows
    tt(k, r[0].rv(0, 1, 0, 64), lamv["lq1"].rv(), lamv["lk1"].rv(),
       ALU.mult, tag="lam")
    reduce(k, r[1].rv(0, 1, 0, 1), r[0].rv(0, 1, 0, 64), ALU.add, tag="lam")
    act(k, r[2].rv(0, 1, 0, 1), r[1].rv(0, 1, 0, 1), AF.Exp, tag="lam")
    tt(k, r[0].rv(0, 1, 0, 64), lamv["lq2"].rv(), lamv["lk2"].rv(),
       ALU.mult, tag="lam")
    reduce(k, r[1].rv(0, 1, 0, 1), r[0].rv(0, 1, 0, 64), ALU.add, tag="lam")
    act(k, r[3].rv(0, 1, 0, 1), r[1].rv(0, 1, 0, 1), AF.Exp, tag="lam")
    tt(k, lam.rv(0, 1, 0, 1), r[2].rv(0, 1, 0, 1), r[3].rv(0, 1, 0, 1),
       ALU.subtract, tag="lam")
    ts(k, lam.rv(0, 1, 0, 1), lam.rv(0, 1, 0, 1), LAMBDA_INIT, ALU.add,
       tag="lam")
    ts(k, g1l.rv(), gam.rv(), 1.0 - LAMBDA_INIT, ALU.mult, tag="gam")

    A.mark()
    xtmp = A.alloc(128, NCH * BLK, F32)
    lnt0 = A.alloc(128, NCH * BLK, BF16)
    for s in range(2):
        for cp in range(NCORES):
            dma(k, k.xf_q, xtmp.rv(), k.xf_block(s, cp), tag="xfln1")
            emit_ln_blk(k, xtmp, lnt0)
            dma(k, "qsp", k.lnf_block(s, cp), lnt0.rv(), tag="lnstore1")
    A.release()

    A.mark()
    r1 = {}
    for n in ("r1kca", "r1ksa"):
        r1[n] = A.alloc(128, 8 * BLK, BF16)
        dma(k, "qsp", r1[n].rv(), W[n])
    lnt = A.alloc(128, NCH * BLK, BF16)
    wfull = A.alloc(128, NCH * D, BF16)
    cst = A.alloc(128, BLK, BF16)
    dma(k, "qpool", wfull.rv(), W["l1wk_full"], tag="l1wk")
    for s in range(2):
        for cp in range(NCORES):
            dma(k, "qsp", lnt.rv(), k.lnf_block(s, cp), tag="lnload1")
            for oc in range(NCH):
                pq = k.pb[0 + (oc % 2)]
                for dc in range(NCH):
                    mm(k, pq.rv(),
                       wfull.rv(0, 128, dc * D + oc * 128,
                                dc * D + (oc + 1) * 128),
                       lnt.rv(0, 128, dc * BLK, (dc + 1) * BLK),
                       start=(dc == 0), stop=(dc == NCH - 1), tag="l1k"),
                _rope(k, cst.rv(), pq, rot1,
                      r1["r1kca"].sub(cp * BLK, (cp + 1) * BLK),
                      r1["r1ksa"].sub(cp * BLK, (cp + 1) * BLK))
                rr = cp * 8192 + s * 4096 + oc * 128
                dma(k, "qsp", (k.kv1_g[rr:rr + 128, :],
                               drect("kv1_g", rr, rr + 128)), cst.rv(),
                    tag="l1kstore")
    dma(k, "qpool", wfull.rv(), W["l1wv_full"], tag="l1wv")
    for s in range(2):
        for cp in range(NCORES):
            dma(k, "qsp", lnt.rv(), k.lnf_block(s, cp), tag="lnloadv1")
            for vg in range(4):
                for tt_i in range(4):
                    pv = k.pb[0 + (tt_i % 2)]
                    for dc in range(NCH):
                        mm(k, pv.rv(),
                           lnt.rv(0, 128, dc * BLK + tt_i * 128,
                                  dc * BLK + (tt_i + 1) * 128),
                           wfull.rv(0, 128, dc * D + vg * 512,
                                    dc * D + (vg + 1) * 512),
                           start=(dc == 0), stop=(dc == NCH - 1), tag="l1v")
                    act(k, cst.rv(), pv.rv(), AF.Copy, tag="l1vev")
                    rr = cp * 8192 + s * 4096 + 2048 + vg * 512 + tt_i * 128
                    dma(k, "qsp", (k.kv1_g[rr:rr + 128, :],
                                   drect("kv1_g", rr, rr + 128)), cst.rv(),
                        tag="l1vstore")
    A.release()
    q_sb = A.alloc(128, NCH * TOK, BF16)
    A.mark()
    k.ln_x = A.alloc(128, NCH * TOK, BF16)
    emit_ln(k)
    ln_own = k.ln_x
    r1q = {}
    for n in ("r1qc", "r1qs"):
        r1q[n] = A.alloc(128, BLK, BF16)
        dma(k, "qsp", r1q[n].rv(), W[n])
    wst = [A.alloc(128, NCH * 128, BF16) for _ in range(2)]
    for oc in range(NCH):
        dma(k, "qpool", wst[oc % 2].rv(), W["l1wq"][oc], tag="l1wqdma")
        for s in range(2):
            pq = k.pb[0 + (s % 2)]
            for dc in range(NCH):
                mm(k, pq.rv(),
                   wst[oc % 2].rv(0, 128, dc * 128, (dc + 1) * 128),
                   ln_own.rv(0, 128, dc * TOK + s * BLK,
                             dc * TOK + s * BLK + BLK),
                   start=(dc == 0), stop=(dc == NCH - 1), tag="l1q")
            _rope(k, q_sb.rv(0, 128, oc * TOK + s * BLK,
                             oc * TOK + s * BLK + BLK),
                  pq, rot1, r1q["r1qc"], r1q["r1qs"])
    A.release()

    A.mark()
    attn_fm = A.alloc(128, NCH * TOK, BF16)
    A.mark()
    ksb = A.alloc(128, 4096, BF16)
    vsb = A.alloc(128, 4096, BF16)
    pt = [A.alloc(128, BLK, BF16) for _ in range(3)]
    a1 = A.alloc(128, BLK, F32)
    a2 = A.alloc(128, BLK, F32)
    for s in range(2):
        for h in range(16):
            for rr in range(NCORES):
                r0_ = rr * 8192 + s * 4096 + h * 128
                dma(k, "qsp", ksb.rv(0, 128, rr * 512, (rr + 1) * 512),
                    (k.kv1_g[r0_:r0_ + 128, :],
                     drect("kv1_g", r0_, r0_ + 128)), tag="l1kload")
                g = h // 4
                co = (h % 4) * 128
                r1_ = rr * 8192 + s * 4096 + 2048 + g * 512
                dma(k, "qsp", vsb.rv(0, 128, rr * 512, (rr + 1) * 512),
                    (k.kv1_g[r1_:r1_ + 512, co:co + 128]
                     .rearrange("(a p) v -> p a v", p=128),
                     drect("kv1_g", r1_, r1_ + 512)), tag="l1vload")
            for comp in range(2):
                pof = comp * 64
                _max_k2(k, ksb, pof, pof + 64)
                qrv = q_sb.rv(pof, pof + 64, h * TOK + s * BLK,
                              h * TOK + s * BLK + BLK)
                p_o = k.pb[4] if comp == 0 else k.pb[3]
                p_l = k.pb[5] if comp == 0 else k.pb[7]
                _strip(k, qrv, ksb, vsb, pof, 64, None, pt, None, p_o, p_l)
            rl1, rl2, sd = k.rows[0], k.rows[1], k.rows[2]
            ts(k, rl1.rv(0, 1, 0, BLK), k.pb[5].rv(0, 1, 0, BLK), L_GUARD,
               ALU.add, tag="lg1")
            recip(k, rl1.rv(0, 1, 0, BLK), rl1.rv(0, 1, 0, BLK), tag="l1rl")
            ts(k, rl2.rv(0, 1, 0, BLK), k.pb[7].rv(0, 1, 0, BLK), L_GUARD,
               ALU.add, tag="lg2")
            recip(k, rl2.rv(0, 1, 0, BLK), rl2.rv(0, 1, 0, BLK), tag="l1rl")
            ts(k, rl2.rv(0, 1, 0, BLK), rl2.rv(0, 1, 0, BLK),
               (lam.ap(0, 1, 0, 1), lam.rect(0, 1, 0, 1)), ALU.mult,
               tag="l1rl2")
            bcast_row(k, k.pb[6], rl1.rv(0, 1, 0, BLK), BLK, k.bc1)
            bcast_row(k, k.pb[6], rl2.rv(0, 1, 0, BLK), BLK, k.bc2)
            tt(k, a1.rv(), k.pb[4].rv(), k.bc1.rv(0, 128, 0, BLK), ALU.mult,
               tag="l1o1")
            tt(k, a2.rv(), k.pb[3].rv(), k.bc2.rv(0, 128, 0, BLK), ALU.mult,
               tag="l1o2")
            tt(k, a1.rv(), a1.rv(), a2.rv(), ALU.subtract, tag="l1sub")
            act(k, a2.rv(), a1.rv(), AF.Square, tag="l1rms")
            mm(k, k.pb[6].rv(0, 1, 0, BLK), k.ones_col_f32.rv(), a2.rv(),
               tag="l1rms")
            act(k, sd.rv(0, 1, 0, BLK), k.pb[6].rv(0, 1, 0, BLK), AF.Sqrt,
                scale=1.0 / 128.0,
                bias=(k.eps_col.ap(0, 1, 0, 1),
                      k.eps_col.rect(0, 1, 0, 1)), tag="l1rms")
            recip(k, sd.rv(0, 1, 0, BLK), sd.rv(0, 1, 0, BLK), tag="l1rms")
            bcast_row(k, k.pb[6], sd.rv(0, 1, 0, BLK), BLK, k.bc1)
            tt(k, a1.rv(), a1.rv(), k.bc1.rv(0, 128, 0, BLK), ALU.mult,
               tag="l1rms2")
            ts(k, attn_fm.rv(0, 128, h * TOK + s * BLK,
                             h * TOK + s * BLK + BLK),
               a1.rv(), (g1l.ap(), g1l.rect()), ALU.mult, tag="l1gam")
    A.release()
    wst2 = [A.alloc(128, NCH * 128, BF16) for _ in range(2)]
    for oc in range(NCH):
        dma(k, "qpool", wst2[oc % 2].rv(), W["l1wo"][oc], tag="l1wodma")
        for b in range(2):
            po = k.pb[0 + (b % 2)]
            for ac_ in range(NCH):
                mm(k, po.rv(),
                   wst2[oc % 2].rv(0, 128, ac_ * 128, (ac_ + 1) * 128),
                   attn_fm.rv(0, 128, ac_ * TOK + b * BLK,
                              ac_ * TOK + b * BLK + BLK),
                   start=(ac_ == 0), stop=(ac_ == NCH - 1), tag="l1wo")
            xc = k.x_res.rv(0, 128, oc * TOK + b * BLK,
                            oc * TOK + b * BLK + BLK)
            tt(k, xc, po.rv(), xc, ALU.add, tag="l1wo_add")
    A.release()
    A.release()



SUBLAYERS = ["moe", "l0attn", "moe", "l1ffn", "l1attn", "l1ffn"]
ARENA_BYTES = 206 * 1024


def build_program(schedule, in_dtype):
    nc = bass.Bass("TRN2", target_bir_lowering=False, debug=False)
    need = set(schedule)

    x_in = nc.dram_tensor("x", [D, TOK], in_dtype, kind="ExternalInput")
    y_out = nc.dram_tensor("y", [D, TOK], F32, kind="ExternalOutput")
    k_dbg = (nc.dram_tensor("dbg", [128, 32768], F32, kind="ExternalOutput")
             if DBG is not None else None)

    k = Ctx()
    k.P = Prog()
    k.dbg = k_dbg
    k.dbg_used = [0]
    din = {}

    def inp(name, shape, dt):
        din[name] = nc.dram_tensor(name, shape, dt, kind="ExternalInput")
        return din[name]

    inp("ident", [128, 128], F32)
    need_xf = ("l0attn" in need) or ("l1attn" in need)
    if need_xf:
        xf_in = nc.dram_tensor("xf", [D, NCORES * TOK], in_dtype,
                               kind="ExternalInput")
        k.xf_q = "qsp" if in_dtype == F32 else "qpool"

        def xf_block(s, cp):
            c0 = s * 4096 + cp * BLK
            return (xf_in.ap().rearrange("(c p) t -> p c t", p=128)
                    [:, :, c0:c0 + BLK], drect("xf"))
        k.xf_block = xf_block
        lnf = nc.dram_tensor("lnf", [D, NCORES * TOK], BF16)
        def lnf_block(s, cp):
            c0 = s * 4096 + cp * BLK
            return (lnf.ap().rearrange("(c p) t -> p c t", p=128)
                    [:, :, c0:c0 + BLK],
                    Rect("D:lnf", 0, 1, c0, c0 + BLK))
        k.lnf_block = lnf_block
    if "l0attn" in need:
        inp("qpos", [1, BLK], F32)
        inp("kpos", [128, 32], F32)
        inp("rot0", [128, 128], BF16)
        for n in ("r0qc", "r0qs"):
            inp(n, [128, BLK], BF16)
        for n in ("r0kca", "r0ksa"):
            inp(n, [128, 8 * BLK], BF16)
        inp("wq", [D, D], BF16)
        inp("wk", [D, 512], BF16)
        inp("wv", [D, 512], BF16)
        inp("wo", [D, D], BF16)
        inp("bq", [1, D], BF16)
        inp("bk", [1, 512], BF16)
        inp("bv", [1, 512], BF16)
        k.kv0_g = nc.dram_tensor("kv0_g", [16384, 512], BF16)
    if "moe" in need:
        inp("gw", [D, 8], F32)
        inp("sel8", [8, 1024], F32)
        inp("eb", [1, 8], F32)
        inp("w1", [8, D, 1024], F32)
        inp("w3", [8, D, 1024], F32)
        inp("w2", [8, 1024, D], F32)
        inp("sw1", [D, 1024], F32)
        inp("sw3", [D, 1024], F32)
        inp("sw2", [1024, D], F32)
    if "l1ffn" in need:
        inp("wgt", [D, 8192], F32)
        inp("wut", [D, 8192], F32)
        inp("wdt", [8192, D], F32)
    if "l1attn" in need:
        inp("rot1", [128, 128], BF16)
        for n in ("r1qc", "r1qs"):
            inp(n, [128, BLK], BF16)
        for n in ("r1kca", "r1ksa"):
            inp(n, [128, 8 * BLK], BF16)
        inp("l1wq", [D, D], F32)
        inp("l1wk", [D, D], F32)
        inp("l1wv", [D, D], F32)
        inp("l1wo", [D, D], F32)
        for n in ("lq1", "lk1", "lq2", "lk2"):
            inp(n, [1, 64], F32)
        inp("gamc", [128, 1], F32)
        k.kv1_g = nc.dram_tensor("kv1_g", [65536, 512], BF16)

    with (
        nc.sbuf_tensor("arena", [128, ARENA_BYTES // 4], F32) as arena_h,
        nc.psum_tensor("pb0", [128, 512], F32) as pb0,
        nc.psum_tensor("pb1", [128, 512], F32) as pb1,
        nc.psum_tensor("pb2", [128, 512], F32) as pb2,
        nc.psum_tensor("pb3", [128, 512], F32) as pb3,
        nc.psum_tensor("pb4", [128, 512], F32) as pb4,
        nc.psum_tensor("pb5", [128, 512], F32) as pb5,
        nc.psum_tensor("pb6", [128, 512], F32) as pb6,
        nc.psum_tensor("pb7", [128, 512], F32) as pb7,
        nc.semaphore("s_pe") as s_pe,
        nc.semaphore("s_dve") as s_dve,
        nc.semaphore("s_act") as s_act,
        nc.semaphore("s_pool") as s_pool,
        nc.semaphore("s_sp") as s_sp,
        nc.semaphore("q_sp") as q_sp,
        nc.semaphore("q_act") as q_act,
        nc.semaphore("q_pool") as q_pool,
        nc.semaphore("q_cc") as q_cc,
        nc.Block() as block,
    ):
        sems = {"pe": s_pe, "dve": s_dve, "act": s_act, "pool": s_pool,
                "sp": s_sp, "qsp": q_sp, "qact": q_act, "qpool": q_pool,
                "qcc": q_cc}
        A = Arena(arena_h, "S:arena", ARENA_BYTES)
        k.arena = A
        k.pb = [Tile(h, f"P:pb{i}", True, 0, 128, 0, 512, F32)
                for i, h in enumerate([pb0, pb1, pb2, pb3, pb4, pb5, pb6,
                                       pb7])]
        k.x_res = A.alloc(128, NCH * TOK, F32)
        k.scr = [A.alloc(128, BLK, F32) for _ in range(3)]
        k.rows = [A.alloc(1, BLK, F32) for _ in range(5)]
        k.bc1 = A.alloc(128, BLK, F32)
        k.bc2 = A.alloc(128, BLK, F32)
        k.sbta = A.alloc(128, BLK, BF16)
        k.sbtb = A.alloc(128, BLK, BF16)
        k.sbqr = A.alloc(128, BLK, BF16)
        k.nb_col = A.alloc(128, 1, F32)
        k.eps_col = A.alloc(128, 1, F32)
        k.ones_col_f32 = A.alloc(128, 1, F32)
        k.ones_col_bf = A.alloc(128, 1, BF16)
        k.ones_row_f32 = A.alloc(1, 128, F32)
        k.ones_row_bf = A.alloc(1, 128, BF16)
        k.ones512_bf = A.alloc(1, BLK, BF16)
        k.ident = A.alloc(128, 128, F32)

        memset(k, k.eps_col.rv(), EPS)
        memset(k, k.ones_col_f32.rv(), 1.0)
        memset(k, k.ones_col_bf.rv(), 1.0)
        memset(k, k.ones_row_f32.rv(), 1.0)
        memset(k, k.ones_row_bf.rv(), 1.0)
        memset(k, k.ones512_bf.rv(), 1.0)
        dma(k, "qsp", k.ident.rv(), (din["ident"].ap(), drect("ident")))

        x_src = (x_in.ap().rearrange("(c p) t -> p c t", p=128),
                 drect("x"))
        if in_dtype == F32:
            dma(k, "qsp", k.x_res.rv(), x_src, tag="xload")
        else:
            dma(k, "qpool", k.x_res.rv(), x_src, tag="xload")

        W = {}
        for n in din:
            pass
        if "moe" in need:
            W["gw"] = (din["gw"].ap().rearrange("(c p) e -> p c e", p=128),
                       drect("gw"))
            W["eb"] = (din["eb"].ap(), drect("eb"))
            W["sel8"] = (din["sel8"].ap(), drect("sel8"))
            W["w1"] = [[(din["w1"][e][:, ht * 128:(ht + 1) * 128]
                         .rearrange("(c p) h -> p c h", p=128), drect("w1"))
                        for ht in range(8)] for e in range(8)]
            W["w3"] = [[(din["w3"][e][:, ht * 128:(ht + 1) * 128]
                         .rearrange("(c p) h -> p c h", p=128), drect("w3"))
                        for ht in range(8)] for e in range(8)]
            W["w2"] = [[(din["w2"][e][:, ot * 128:(ot + 1) * 128]
                         .rearrange("(c p) o -> p c o", p=128), drect("w2"))
                        for ot in range(NCH)] for e in range(8)]
            W["sw1"] = [(din["sw1"][:, ht * 128:(ht + 1) * 128]
                         .rearrange("(c p) h -> p c h", p=128), drect("sw1"))
                        for ht in range(8)]
            W["sw3"] = [(din["sw3"][:, ht * 128:(ht + 1) * 128]
                         .rearrange("(c p) h -> p c h", p=128), drect("sw3"))
                        for ht in range(8)]
            W["sw2"] = [(din["sw2"][:, ot * 128:(ot + 1) * 128]
                         .rearrange("(c p) o -> p c o", p=128), drect("sw2"))
                        for ot in range(NCH)]
        if "l0attn" in need:
            for n in ("qpos", "kpos", "rot0", "r0qc", "r0qs", "r0kca",
                      "r0ksa", "bq", "bk", "bv"):
                W[n] = (din[n].ap(), drect(n))
            W["wk"] = (din["wk"].ap().rearrange("(c p) o -> p c o", p=128),
                       drect("wk"))
            W["wv"] = (din["wv"].ap().rearrange("(c p) o -> p c o", p=128),
                       drect("wv"))
            W["wq"] = [(din["wq"][:, oc * 128:(oc + 1) * 128]
                        .rearrange("(c p) o -> p c o", p=128), drect("wq"))
                       for oc in range(NCH)]
            W["wo"] = [(din["wo"][:, oc * 128:(oc + 1) * 128]
                        .rearrange("(c p) o -> p c o", p=128), drect("wo"))
                       for oc in range(NCH)]
        if "l1ffn" in need:
            W["wg"] = [(din["wgt"][:, sl * 128:(sl + 1) * 128]
                        .rearrange("(c p) h -> p c h", p=128),
                        drect("wgt")) for sl in range(64)]
            W["wu"] = [(din["wut"][:, sl * 128:(sl + 1) * 128]
                        .rearrange("(c p) h -> p c h", p=128),
                        drect("wut")) for sl in range(64)]
            W["wd"] = [[(din["wdt"][half * 4096:(half + 1) * 4096,
                                    ot * 128:(ot + 1) * 128]
                         .rearrange("(c p) o -> p c o", p=128),
                         drect("wdt")) for ot in range(NCH)]
                       for half in range(2)]
        if "l1attn" in need:
            for n in ("rot1", "r1qc", "r1qs", "r1kca", "r1ksa", "lq1",
                      "lk1", "lq2", "lk2", "gamc"):
                W[n] = (din[n].ap(), drect(n))
            W["l1wq"] = [(din["l1wq"][:, oc * 128:(oc + 1) * 128]
                          .rearrange("(c p) o -> p c o", p=128),
                          drect("l1wq")) for oc in range(NCH)]
            W["l1wk_full"] = (din["l1wk"].ap()
                              .rearrange("(c p) o -> p c o", p=128),
                              drect("l1wk"))
            W["l1wv_full"] = (din["l1wv"].ap()
                              .rearrange("(c p) o -> p c o", p=128),
                              drect("l1wv"))
            W["l1wo"] = [(din["l1wo"][:, oc * 128:(oc + 1) * 128]
                          .rearrange("(c p) o -> p c o", p=128),
                          drect("l1wo")) for oc in range(NCH)]

        for name in schedule:
            if name == "moe":
                emit_moe(k, W)
            elif name == "l0attn":
                emit_l0attn(k, W)
            elif name == "l1ffn":
                emit_l1ffn2(k, W)
            elif name == "l1attn":
                emit_l1attn(k, W)

        dma(k, "qsp", (y_out.ap().rearrange("(c p) t -> p c t", p=128),
                       drect("y")), k.x_res.rv(), tag="ystore")

        final = k.P.finalize()
        nq = dict(k.P.qcount)

        def emit_engine(eng_obj, ename, tail=False):
            for op, ws in final[ename]:
                for sem, val in ws:
                    eng_obj.wait_ge(sems[sem], val)
                instr = op.fn(eng_obj)
                if op.queue is not None:
                    instr.then_inc(sems[op.queue], 16)
                elif op.need_inc:
                    instr.then_inc(sems[op.engine], 1)
            if tail:
                eng_obj.wait_ge(sems["qsp"], 16 * nq["qsp"])

        @block.sync
        def _(eng):
            emit_engine(eng, "sp", tail=True)

        @block.tensor
        def _(eng):
            emit_engine(eng, "pe")

        @block.vector
        def _(eng):
            emit_engine(eng, "dve")

        @block.scalar
        def _(eng):
            emit_engine(eng, "act")

        @block.gpsimd
        def _(eng):
            emit_engine(eng, "pool")

    return nc



def _rot_matrices():
    R0 = np.zeros((128, 128), dtype=np.float32)
    for d in range(64):
        R0[d, d + 64] = -1.0
        R0[d + 64, d] = 1.0
    R1 = np.zeros((128, 128), dtype=np.float32)
    for base in (0, 64):
        for d in range(32):
            R1[base + d, base + 32 + d] = -1.0
            R1[base + 32 + d, base + d] = 1.0
    return (np.ascontiguousarray(R0.T).astype(BF),
            np.ascontiguousarray(R1.T).astype(BF))


def _l0_cossin(pos):
    inv = 1.0 / (1e6 ** (np.arange(64, dtype=np.float64) / 64.0))
    ang = np.outer(pos, inv)
    ct = np.cos(ang).T
    st = np.sin(ang).T
    return np.vstack([ct, ct]), np.vstack([st, st])


def _rope_tables_l0(core):
    cos, sin = _l0_cossin(core * BLK + np.arange(BLK, dtype=np.float64))
    s = 1.0 / math.sqrt(128.0)
    return {"r0qc": (cos * s).astype(BF), "r0qs": (sin * s).astype(BF)}


def _l1_cossin(pos):
    angle = 1.0 / (10000.0 ** np.linspace(0.0, 1.0, 32))
    fr = pos[:, None] * angle[None, :]
    return np.tile(np.cos(fr).T, (4, 1)), np.tile(np.sin(fr).T, (4, 1))


def _rope_tables_l1(core):
    cos, sin = _l1_cossin(core * BLK + np.arange(BLK, dtype=np.float64))
    s = 1.0 / 8.0
    return {"r1qc": (cos * s).astype(BF), "r1qs": (sin * s).astype(BF)}


_PERM64 = np.concatenate([np.arange(0, 64, 2), np.arange(1, 64, 2)])


def _perm_cols(wt):
    out = wt.reshape(wt.shape[0], 32, 64)[:, :, _PERM64]
    return np.ascontiguousarray(out.reshape(wt.shape[0], 2048))


def _prep_weights(inputs):
    f32 = np.float32
    Wall = {}
    Wall["gw"] = np.ascontiguousarray(
        np.asarray(inputs["L0_ffn_gate_w"], f32).T)
    Wall["eb"] = np.asarray(inputs["L0_ffn_expert_bias"], f32)[None, :]
    Wall["w1"] = np.ascontiguousarray(
        np.asarray(inputs["L0_ffn_w1"], f32).transpose(0, 2, 1))
    Wall["w3"] = np.ascontiguousarray(
        np.asarray(inputs["L0_ffn_w3"], f32).transpose(0, 2, 1))
    Wall["w2"] = np.ascontiguousarray(
        np.asarray(inputs["L0_ffn_w2"], f32).transpose(0, 2, 1))
    Wall["sw1"] = np.ascontiguousarray(np.asarray(inputs["L0_ffn_sw1"],
                                                  f32).T)
    Wall["sw3"] = np.ascontiguousarray(np.asarray(inputs["L0_ffn_sw3"],
                                                  f32).T)
    Wall["sw2"] = np.ascontiguousarray(np.asarray(inputs["L0_ffn_sw2"],
                                                  f32).T)
    Wall["wq"] = np.ascontiguousarray(np.asarray(inputs["L0_attn_Wq"], BF))
    Wall["wk"] = np.ascontiguousarray(np.asarray(inputs["L0_attn_Wk"], BF))
    Wall["wv"] = np.ascontiguousarray(np.asarray(inputs["L0_attn_Wv"], BF))
    Wall["wo"] = np.ascontiguousarray(np.asarray(inputs["L0_attn_Wo"], BF))
    Wall["bq"] = np.asarray(inputs["L0_attn_bq"], BF)[None, :]
    Wall["bk"] = np.asarray(inputs["L0_attn_bk"], BF)[None, :]
    Wall["bv"] = np.asarray(inputs["L0_attn_bv"], BF)[None, :]
    Wall["wgt"] = np.ascontiguousarray(np.asarray(inputs["L1_ffn_w_gate"],
                                                  f32).T)
    Wall["wut"] = np.ascontiguousarray(np.asarray(inputs["L1_ffn_w_up"],
                                                  f32).T)
    Wall["wdt"] = np.ascontiguousarray(np.asarray(inputs["L1_ffn_w_down"],
                                                  f32).T)
    Wall["l1wq"] = _perm_cols(np.asarray(inputs["L1_attn_Wq"], f32).T)
    Wall["l1wk"] = _perm_cols(np.asarray(inputs["L1_attn_Wk"], f32).T)
    Wall["l1wv"] = np.ascontiguousarray(np.asarray(inputs["L1_attn_Wv"],
                                                   f32).T)
    Wall["l1wo"] = np.ascontiguousarray(np.asarray(inputs["L1_attn_Wo"],
                                                   f32).T)
    Wall["lq1"] = np.asarray(inputs["L1_attn_lambda_q1"], f32)[None, :]
    Wall["lk1"] = np.asarray(inputs["L1_attn_lambda_k1"], f32)[None, :]
    Wall["lq2"] = np.asarray(inputs["L1_attn_lambda_q2"], f32)[None, :]
    Wall["lk2"] = np.asarray(inputs["L1_attn_lambda_k2"], f32)[None, :]
    Wall["gamc"] = np.ascontiguousarray(
        np.asarray(inputs["L1_attn_gamma"], f32)[:, None])
    rot0, rot1 = _rot_matrices()
    Wall["rot0"] = rot0
    Wall["rot1"] = rot1
    allpos = np.arange(8 * BLK, dtype=np.float64)
    c0, s0 = _l0_cossin(allpos)
    Wall["r0kca"] = c0.astype(BF)
    Wall["r0ksa"] = s0.astype(BF)
    c1, s1 = _l1_cossin(allpos)
    Wall["r1kca"] = c1.astype(BF)
    Wall["r1ksa"] = s1.astype(BF)
    Wall["ident"] = np.eye(128, dtype=np.float32)
    Wall["kpos"] = (np.arange(32, dtype=np.float32)[None, :] * 128
                    + np.arange(128, dtype=np.float32)[:, None])
    sel8 = np.zeros((8, 1024), dtype=np.float32)
    for e in range(8):
        sel8[e, e * 128:(e + 1) * 128] = 1.0
    Wall["sel8"] = sel8
    return Wall


_WNAMES = {
    "moe": ("gw", "eb", "w1", "w3", "w2", "sw1", "sw3", "sw2", "sel8"),
    "l0attn": ("wq", "wk", "wv", "wo", "bq", "bk", "bv", "rot0", "kpos",
               "r0kca", "r0ksa"),
    "l1ffn": ("wgt", "wut", "wdt"),
    "l1attn": ("l1wq", "l1wk", "l1wv", "l1wo", "lq1", "lk1", "lq2", "lk2",
               "gamc", "rot1", "r1kca", "r1ksa"),
}


_PROG_CACHE = {}


def _get_prog(group, dt_tag):
    key = (group, dt_tag)
    if key not in _PROG_CACHE:
        _PROG_CACHE[key] = build_program(list(group),
                                         BF16 if dt_tag == "bf16" else F32)
    return _PROG_CACHE[key]


def kernel(**inputs):
    x = np.asarray(inputs["L0_attn_x"], BF)
    Wall = _prep_weights(inputs)

    x_fm = []
    for c in range(NCORES):
        xc = np.concatenate([x[0, c * BLK:(c + 1) * BLK, :],
                             x[1, c * BLK:(c + 1) * BLK, :]], axis=0)
        x_fm.append(np.ascontiguousarray(xc.T))

    def in_map_for(core, schedule, x_arr, xf_arr=None):
        m = {"x": x_arr, "ident": Wall["ident"]}
        need = set(schedule)
        if ("l0attn" in need) or ("l1attn" in need):
            m["xf"] = xf_arr
        if "l0attn" in need:
            m["qpos"] = (core * BLK
                         + np.arange(BLK, dtype=np.float32))[None, :]
            m.update(_rope_tables_l0(core))
        if "l1attn" in need:
            m.update(_rope_tables_l1(core))
        for s in need:
            for n in _WNAMES[s]:
                m[n] = Wall[n]
        return m

    def full_fm(cur):
        xf = np.empty((D, NCORES * TOK), dtype=cur[0].dtype)
        for c in range(NCORES):
            for s in range(2):
                xf[:, s * 4096 + c * BLK:s * 4096 + (c + 1) * BLK] = \
                    cur[c][:, s * BLK:(s + 1) * BLK]
        return xf

    GROUPS = [["moe"], ["l0attn", "moe", "l1ffn"], ["l1attn", "l1ffn"]]
    cur = x_fm
    for gi, group in enumerate(GROUPS):
        nc = _get_prog(tuple(group), "bf16" if gi == 0 else "f32")
        xf_arr = (full_fm(cur)
                  if ("l0attn" in group or "l1attn" in group) else None)
        in_maps = [in_map_for(c, group, cur[c], xf_arr)
                   for c in range(NCORES)]
        res = run_bass_kernel_spmd(nc, in_maps, core_ids=list(range(NCORES)))
        cur = [np.ascontiguousarray(r["y"]) for r in res.results]
    outs = cur

    out = np.empty((2, 4096, 2048), dtype=np.float32)
    for c in range(NCORES):
        yt = outs[c].T
        out[0, c * BLK:(c + 1) * BLK, :] = yt[:BLK]
        out[1, c * BLK:(c + 1) * BLK, :] = yt[BLK:]
    return out


# version 04
# speedup vs baseline: 1.4494x; 1.4494x over previous
import math
import numpy as np
import ml_dtypes

import concourse.bass as bass
import concourse.mybir as mybir
from concourse.bass_utils import run_bass_kernel_spmd

F32 = mybir.dt.float32
BF16 = mybir.dt.bfloat16
AF = mybir.ActivationFunctionType
ALU = mybir.AluOpType
AXL = mybir.AxisListType

BF = np.dtype(ml_dtypes.bfloat16)

D = 2048
NCH = 16
TOK = 1024
BLK = 512
NCORES = 8
EPS = 1e-5
LAMBDA_INIT = 0.8 - 0.6 * math.exp(-0.3 * 12)
L_GUARD = 1e-30

FUSED = False
DBG = None


ENGINES = ("pe", "dve", "act", "pool", "sp")
QUEUES = ("qsp", "qact", "qpool", "qcc")
QUEUE_ENGINE = {"qsp": "sp", "qact": "act", "qpool": "pool", "qcc": "pool"}


class Rect:
    __slots__ = ("key", "p0", "p1", "b0", "b1")

    def __init__(self, key, p0, p1, b0, b1):
        self.key, self.p0, self.p1, self.b0, self.b1 = key, p0, p1, b0, b1

    def overlaps(self, o):
        return (self.key == o.key and self.p0 < o.p1 and o.p0 < self.p1
                and self.b0 < o.b1 and o.b0 < self.b1)


class Op:
    __slots__ = ("engine", "fn", "reads", "writes", "queue", "eidx", "qidx",
                 "need_inc", "inc_count", "waits", "tag", "deps")

    def __init__(self, engine, fn, reads, writes, queue, tag=""):
        self.engine, self.fn = engine, fn
        self.reads, self.writes = reads, writes
        self.queue, self.tag = queue, tag
        self.eidx = -1
        self.qidx = -1
        self.need_inc = False
        self.inc_count = -1
        self.waits = {}
        self.deps = []


class Tile:
    __slots__ = ("handle", "key", "is_psum", "p0", "parts", "b0", "free",
                 "dt", "esz")

    def __init__(self, handle, key, is_psum, p0, parts, b0, free, dt):
        self.handle, self.key, self.is_psum = handle, key, is_psum
        self.p0, self.parts, self.b0, self.free, self.dt = p0, parts, b0, free, dt
        self.esz = 4 if dt == F32 else 2

    def sub(self, f0, f1):
        return Tile(self.handle, self.key, self.is_psum, self.p0, self.parts,
                    self.b0 + f0 * self.esz, f1 - f0, self.dt)

    def ap(self, p0=0, p1=None, f0=0, f1=None):
        if p1 is None:
            p1 = self.parts
        if f1 is None:
            f1 = self.free
        if self.is_psum:
            return self.handle[self.p0 + p0:self.p0 + p1, f0:f1]
        e0 = self.b0 // self.esz
        base = self.handle.ap()
        if self.dt != F32:
            base = base.bitcast(self.dt)
        return base[self.p0 + p0:self.p0 + p1, e0 + f0:e0 + f1]

    def rect(self, p0=0, p1=None, f0=0, f1=None):
        if p1 is None:
            p1 = self.parts
        if f1 is None:
            f1 = self.free
        return Rect(self.key, self.p0 + p0, self.p0 + p1,
                    self.b0 + f0 * self.esz, self.b0 + f1 * self.esz)

    def rv(self, p0=0, p1=None, f0=0, f1=None):
        return (self.ap(p0, p1, f0, f1), self.rect(p0, p1, f0, f1))


class Arena:
    def __init__(self, handle, key, nbytes):
        self.handle, self.key, self.nbytes = handle, key, nbytes
        self.off = 0
        self.marks = []
        self.peak = 0

    def alloc(self, parts, free, dt, align=64):
        esz = 4 if dt == F32 else 2
        sz = free * esz
        self.off = (self.off + align - 1) // align * align
        assert self.off + sz <= self.nbytes, \
            f"arena overflow: {self.off}+{sz} > {self.nbytes}"
        t = Tile(self.handle, self.key, False, 0, parts, self.off, free, dt)
        self.off += sz
        self.peak = max(self.peak, self.off)
        return t

    def mark(self):
        self.marks.append(self.off)

    def release(self):
        self.off = self.marks.pop()


def drect(name, r0=0, r1=1 << 30):
    return Rect("D:" + name, 0, 1, r0, r1)


class Prog:
    def __init__(self):
        self.ops = {e: [] for e in ENGINES}
        self.qcount = {q: 0 for q in QUEUES}
        self.order = []
        self.regions = {}
        self.last_q = {}

    def _dep_scan(self, op):
        deps = []
        for r in op.reads:
            for ent in self.regions.get(r.key, ()):
                if ent[1] is not None and ent[0].overlaps(r):
                    deps.append(ent[1])
        for w in op.writes:
            for ent in self.regions.get(w.key, ()):
                if ent[0].overlaps(w):
                    if ent[1] is not None:
                        deps.append(ent[1])
                    deps.extend(ent[2])
        return deps

    def _record(self, op):
        for w in op.writes:
            lst = self.regions.setdefault(w.key, [])
            keep = [ent for ent in lst
                    if not (ent[0].p0 >= w.p0 and ent[0].p1 <= w.p1
                            and ent[0].b0 >= w.b0 and ent[0].b1 <= w.b1)]
            keep.append((w, op, []))
            self.regions[w.key] = keep
        for r in op.reads:
            lst = self.regions.setdefault(r.key, [])
            for ent in lst:
                if ent[0].overlaps(r):
                    ent[2].append(op)

    def add(self, engine, fn, reads=(), writes=(), queue=None, tag=""):
        op = Op(engine, fn, list(reads), list(writes), queue, tag)
        op.deps = self._dep_scan(op)
        self._record(op)
        op.eidx = len(self.ops[engine])
        self.ops[engine].append(op)
        if queue is not None:
            op.qidx = self.qcount[queue]
            self.qcount[queue] += 1
            prev = self.last_q.get(queue)
            if prev is not None:
                op.deps.append(prev)
            self.last_q[queue] = op
        self.order.append(op)
        return op

    def finalize(self):
        for op in self.order:
            for d in op.deps:
                if d is op or d.queue is not None:
                    continue
                if d.engine == "pe" and op.engine == "pe":
                    continue
                d.need_inc = True
        counts = {e: 0 for e in ENGINES}
        for e in ENGINES:
            for op in self.ops[e]:
                if op.queue is None and op.need_inc:
                    counts[e] += 1
                    op.inc_count = counts[e]
        for op in self.order:
            waits = {}
            for d in op.deps:
                if d is op:
                    continue
                if d.queue is not None:
                    sem, val = d.queue, 16 * (d.qidx + 1)
                elif d.engine == "pe" and op.engine == "pe":
                    continue
                else:
                    if d.inc_count < 0:
                        continue
                    sem, val = d.engine, d.inc_count
                if waits.get(sem, 0) < val:
                    waits[sem] = val
            op.waits = waits
        final = {}
        for e in ENGINES:
            seen = {}
            lst = []
            for op in self.ops[e]:
                ws = []
                for sem, val in sorted(op.waits.items()):
                    if seen.get(sem, 0) >= val:
                        continue
                    seen[sem] = val
                    ws.append((sem, val))
                lst.append((op, ws))
            final[e] = lst
        return final



class Ctx:
    pass


def mm(k, out_rv, lhs_rv, rhs_rv, start=True, stop=True, tag=""):
    (oap, orect), (lap, lrect), (rap, rrect) = out_rv, lhs_rv, rhs_rv
    k.P.add("pe", lambda e: e.matmul(oap, lap, rap, start=start, stop=stop),
            reads=[lrect, rrect], writes=[orect], tag=tag)


def transpose(k, out_rv, in_rv, id_rv, tag=""):
    (oap, orect), (iap, irect), (dap, dr) = out_rv, in_rv, id_rv
    k.P.add("pe", lambda e: e.transpose(oap, iap, dap),
            reads=[irect, dr], writes=[orect], tag=tag)


def act(k, out_rv, in_rv, func, bias=None, scale=None, tag=""):
    (oap, orect), (iap, irect) = out_rv, in_rv
    reads = [irect]
    kw = {}
    if bias is not None:
        if isinstance(bias, tuple):
            kw["bias"] = bias[0]
            reads.append(bias[1])
        else:
            kw["bias"] = bias
    if scale is not None:
        kw["scale"] = scale
    k.P.add("act", lambda e: e.activation(oap, iap, func, **kw),
            reads=reads, writes=[orect], tag=tag)


def tt(k, out_rv, a_rv, b_rv, op, tag=""):
    (oap, orect), (aap, ar), (bap, br) = out_rv, a_rv, b_rv
    k.P.add("dve", lambda e: e.tensor_tensor(oap, aap, bap, op),
            reads=[ar, br], writes=[orect], tag=tag)


def ts(k, out_rv, in_rv, s1, op0, s2=None, op1=None, tag=""):
    (oap, orect), (iap, irect) = out_rv, in_rv
    reads = [irect]
    if isinstance(s1, tuple):
        s1v = s1[0]
        reads.append(s1[1])
    else:
        s1v = s1
    s2v = None
    if s2 is not None:
        if isinstance(s2, tuple):
            s2v = s2[0]
            reads.append(s2[1])
        else:
            s2v = s2

    def fn(e):
        if op1 is None:
            return e.tensor_scalar(oap, iap, s1v, None, op0)
        return e.tensor_scalar(oap, iap, s1v, s2v, op0, op1)
    k.P.add("dve", fn, reads=reads, writes=[orect], tag=tag)


def copy(k, out_rv, in_rv, engine="dve", tag=""):
    (oap, orect), (iap, irect) = out_rv, in_rv
    if engine == "dve":
        k.P.add("dve", lambda e: e.tensor_copy(oap, iap),
                reads=[irect], writes=[orect], tag=tag)
    else:
        k.P.add("act", lambda e: e.activation(oap, iap, AF.Copy),
                reads=[irect], writes=[orect], tag=tag)


def reduce(k, out_rv, in_rv, op, negate=False, tag=""):
    (oap, orect), (iap, irect) = out_rv, in_rv
    k.P.add("dve",
            lambda e: e.tensor_reduce(oap, iap, axis=AXL.X, op=op,
                                      negate=negate),
            reads=[irect], writes=[orect], tag=tag)


def recip(k, out_rv, in_rv, tag=""):
    (oap, orect), (iap, irect) = in_rv if False else (out_rv, in_rv)
    k.P.add("dve", lambda e: e.reciprocal(oap, iap),
            reads=[irect], writes=[orect], tag=tag)


def memset(k, out_rv, val, tag=""):
    (oap, orect) = out_rv
    k.P.add("dve", lambda e: e.memset(oap, val), reads=[], writes=[orect],
            tag=tag)


def dma(k, queue, out_rv, in_rv, tag=""):
    (oap, orect), (iap, irect) = out_rv, in_rv
    if iap.ndim == 3 and oap.ndim == 2:
        oap = oap.rearrange("p (c t) -> p c t", c=iap.shape[1])
    elif oap.ndim == 3 and iap.ndim == 2:
        iap = iap.rearrange("p (c t) -> p c t", c=oap.shape[1])
    k.P.add(QUEUE_ENGINE[queue], lambda e: e.dma_start(out=oap, in_=iap),
            reads=[irect], writes=[orect], queue=queue, tag=tag)


def collective(k, kind, op, in_rv, out_rv, tag=""):
    (iap, irect), (oap, orect) = in_rv, out_rv
    rg = [list(range(NCORES))]
    k.P.add("pool",
            lambda e: e.collective_compute(kind, op, replica_groups=rg,
                                           ins=[iap.opt()], outs=[oap.opt()]),
            reads=[irect], writes=[orect], queue="qcc", tag=tag)


def bcast_row(k, psum_tile, row_rv, n, sb_out, tag=""):
    mm(k, psum_tile.rv(0, 128, 0, n), k.ones_row_f32.rv(), row_rv, tag=tag)
    copy(k, sb_out.rv(0, 128, 0, n), psum_tile.rv(0, 128, 0, n), tag=tag)
    return sb_out



def emit_ln(k, gate=False):
    for b in range(2):
        c0 = b * BLK
        p_sx, p_sq = k.pb[4], k.pb[5]
        for c in range(NCH):
            xc = k.x_res.rv(0, 128, c * TOK + c0, c * TOK + c0 + BLK)
            sq = k.scr[c % 2]
            act(k, sq.rv(0, 128, 0, BLK), xc, AF.Square, tag="ln_sq")
            mm(k, p_sx.rv(0, 1, 0, BLK), k.ones_col_f32.rv(), xc,
               start=(c == 0), stop=(c == NCH - 1), tag="ln_sx")
            mm(k, p_sq.rv(0, 1, 0, BLK), k.ones_col_f32.rv(),
               sq.rv(0, 128, 0, BLK),
               start=(c == 0), stop=(c == NCH - 1), tag="ln_sq2")
        mu, ssn, var, rstd = k.rows[0], k.rows[1], k.rows[2], k.rows[3]
        act(k, mu.rv(0, 1, 0, BLK), p_sx.rv(0, 1, 0, BLK), AF.Copy,
            scale=1.0 / D, tag="ln")
        act(k, ssn.rv(0, 1, 0, BLK), p_sq.rv(0, 1, 0, BLK), AF.Copy,
            scale=1.0 / D, tag="ln")
        act(k, var.rv(0, 1, 0, BLK), mu.rv(0, 1, 0, BLK), AF.Square,
            tag="ln")
        tt(k, var.rv(0, 1, 0, BLK), ssn.rv(0, 1, 0, BLK),
           var.rv(0, 1, 0, BLK), ALU.subtract, tag="ln")
        act(k, rstd.rv(0, 1, 0, BLK), var.rv(0, 1, 0, BLK), AF.Sqrt,
            bias=(k.eps_col.ap(0, 1, 0, 1), k.eps_col.rect(0, 1, 0, 1)),
            tag="ln")
        recip(k, rstd.rv(0, 1, 0, BLK), rstd.rv(0, 1, 0, BLK), tag="ln")
        tt(k, mu.rv(0, 1, 0, BLK), mu.rv(0, 1, 0, BLK),
           rstd.rv(0, 1, 0, BLK), ALU.mult, tag="ln")
        bc_r = bcast_row(k, k.pb[4], rstd.rv(0, 1, 0, BLK), BLK, k.bc1)
        bc_m = bcast_row(k, k.pb[5], mu.rv(0, 1, 0, BLK), BLK, k.bc2)
        for c in range(NCH):
            xc = k.x_res.rv(0, 128, c * TOK + c0, c * TOK + c0 + BLK)
            t = k.scr[c % 2]
            tt(k, t.rv(0, 128, 0, BLK), xc, bc_r.rv(0, 128, 0, BLK),
               ALU.mult, tag="ln_ap1")
            lnc = k.ln_x.rv(0, 128, c * TOK + c0, c * TOK + c0 + BLK)
            if not gate:
                tt(k, lnc, t.rv(0, 128, 0, BLK), bc_m.rv(0, 128, 0, BLK),
                   ALU.subtract, tag="ln_ap2")
            else:
                lf = k.scr[2]
                tt(k, lf.rv(0, 128, 0, BLK), t.rv(0, 128, 0, BLK),
                   bc_m.rv(0, 128, 0, BLK), ALU.subtract, tag="ln_ap2")
                copy(k, lnc, lf.rv(0, 128, 0, BLK), tag="ln_cp")
                mm(k, k.pb[6].rv(0, 8, 0, BLK),
                   k.gate_sb.rv(0, 128, c * 8, (c + 1) * 8),
                   lf.rv(0, 128, 0, BLK),
                   start=(c == 0), stop=(c == NCH - 1), tag="gate_mm")
        if gate:
            act(k, k.sc_fm[b].rv(0, 8, 0, BLK), k.pb[6].rv(0, 8, 0, BLK),
                AF.Sigmoid, tag="gate_sig")


def emit_ln_blk(k, xsrc, dst):
    p_sx, p_sq = k.pb[4], k.pb[5]
    for c in range(NCH):
        xc = xsrc.rv(0, 128, c * BLK, (c + 1) * BLK)
        sq = k.scr[c % 2]
        act(k, sq.rv(0, 128, 0, BLK), xc, AF.Square, tag="lnb_sq")
        mm(k, p_sx.rv(0, 1, 0, BLK), k.ones_col_f32.rv(), xc,
           start=(c == 0), stop=(c == NCH - 1), tag="lnb_sx")
        mm(k, p_sq.rv(0, 1, 0, BLK), k.ones_col_f32.rv(),
           sq.rv(0, 128, 0, BLK),
           start=(c == 0), stop=(c == NCH - 1), tag="lnb_sq2")
    mu, ssn, var, rstd = k.rows[0], k.rows[1], k.rows[2], k.rows[3]
    act(k, mu.rv(0, 1, 0, BLK), p_sx.rv(0, 1, 0, BLK), AF.Copy,
        scale=1.0 / D, tag="lnb")
    act(k, ssn.rv(0, 1, 0, BLK), p_sq.rv(0, 1, 0, BLK), AF.Copy,
        scale=1.0 / D, tag="lnb")
    act(k, var.rv(0, 1, 0, BLK), mu.rv(0, 1, 0, BLK), AF.Square, tag="lnb")
    tt(k, var.rv(0, 1, 0, BLK), ssn.rv(0, 1, 0, BLK), var.rv(0, 1, 0, BLK),
       ALU.subtract, tag="lnb")
    act(k, rstd.rv(0, 1, 0, BLK), var.rv(0, 1, 0, BLK), AF.Sqrt,
        bias=(k.eps_col.ap(0, 1, 0, 1), k.eps_col.rect(0, 1, 0, 1)),
        tag="lnb")
    recip(k, rstd.rv(0, 1, 0, BLK), rstd.rv(0, 1, 0, BLK), tag="lnb")
    tt(k, mu.rv(0, 1, 0, BLK), mu.rv(0, 1, 0, BLK), rstd.rv(0, 1, 0, BLK),
       ALU.mult, tag="lnb")
    bc_r = bcast_row(k, k.pb[4], rstd.rv(0, 1, 0, BLK), BLK, k.bc1)
    bc_m = bcast_row(k, k.pb[5], mu.rv(0, 1, 0, BLK), BLK, k.bc2)
    for c in range(NCH):
        xc = xsrc.rv(0, 128, c * BLK, (c + 1) * BLK)
        t = k.scr[c % 2]
        tt(k, t.rv(0, 128, 0, BLK), xc, bc_r.rv(0, 128, 0, BLK), ALU.mult,
           tag="lnb_ap1")
        tt(k, dst.rv(0, 128, c * BLK, (c + 1) * BLK), t.rv(0, 128, 0, BLK),
           bc_m.rv(0, 128, 0, BLK), ALU.subtract, tag="lnb_ap2")


def emit_moe(k, W):
    A = k.arena
    A.mark()
    k.gate_sb = A.alloc(128, NCH * 8, F32)
    dma(k, "qsp", k.gate_sb.rv(), W["gw"])
    ebr = A.alloc(1, 8, F32)
    dma(k, "qsp", ebr.rv(), W["eb"])
    k.sel8 = A.alloc(8, 1024, F32)
    dma(k, "qsp", k.sel8.rv(), W["sel8"])
    k.ebias_bc = A.alloc(128, 8, F32)
    bcast_row(k, k.pb[6], ebr.rv(0, 1, 0, 8), 8, k.ebias_bc)
    k.ln_x = A.alloc(128, NCH * TOK, BF16)
    k.sc_fm = [A.alloc(8, BLK, F32) for _ in range(2)]
    emit_ln(k, gate=True)

    scale_fm = [A.alloc(8, BLK, F32) for _ in range(2)]
    sc_tm = A.alloc(128, 4 * 8, F32)
    bt = A.alloc(128, 8, F32)
    selt = A.alloc(128, 8, F32)
    m1 = A.alloc(128, 1, F32)
    m2 = A.alloc(128, 1, F32)
    for b in range(2):
        for j in range(4):
            transpose(k, k.pb[6].rv(0, 128, j * 8, (j + 1) * 8),
                      k.sc_fm[b].rv(0, 8, j * 128, (j + 1) * 128),
                      k.ident.rv(0, 8, 0, 8), tag="sc_T")
        copy(k, sc_tm.rv(), k.pb[6].rv(0, 128, 0, 32), tag="sc_T_ev")
        for j in range(4):
            stj = sc_tm.rv(0, 128, j * 8, (j + 1) * 8)
            tt(k, bt.rv(), stj, k.ebias_bc.rv(), ALU.add, tag="top2")
            reduce(k, m1.rv(), bt.rv(), ALU.max, tag="top2")
            ts(k, selt.rv(), bt.rv(), (m1.ap(), m1.rect()), ALU.is_ge,
               1e30, ALU.mult, tag="top2")
            tt(k, selt.rv(), bt.rv(), selt.rv(), ALU.subtract, tag="top2")
            reduce(k, m2.rv(), selt.rv(), ALU.max, tag="top2")
            ts(k, selt.rv(), bt.rv(), (m2.ap(), m2.rect()), ALU.is_ge,
               tag="top2")
            tt(k, selt.rv(), selt.rv(), stj, ALU.mult, tag="top2")
            transpose(k, k.pb[6].rv(0, 8, j * 128, (j + 1) * 128),
                      selt.rv(), k.ident.rv(), tag="sc_Tb")
        copy(k, scale_fm[b].rv(), k.pb[6].rv(0, 8, 0, BLK), tag="sc_Tb_ev")

    he = A.alloc(128, 8 * TOK, BF16)
    w1t = [A.alloc(128, NCH * 128, BF16) for _ in range(2)]
    w3t = [A.alloc(128, NCH * 128, BF16) for _ in range(2)]
    w2t = [A.alloc(128, 8 * 128, BF16) for _ in range(2)]
    sbc = [A.alloc(128, BLK, F32) for _ in range(2)]
    t1 = A.alloc(128, BLK, BF16)
    t3 = A.alloc(128, BLK, BF16)
    s1t = A.alloc(128, BLK, BF16)

    def expert(e, w1_rv, w3_rv, w2_rv, routed):
        if routed:
            for b in range(2):
                mm(k, k.pb[6].rv(0, 128, 0, BLK),
                   k.sel8.rv(0, 8, e * 128, (e + 1) * 128),
                   scale_fm[b].rv(0, 8, 0, BLK), tag="sbc_mm")
                copy(k, sbc[b].rv(), k.pb[6].rv(0, 128, 0, BLK),
                     tag="sbc_ev")
        for ht in range(8):
            dma(k, "qpool", w1t[ht % 2].rv(), w1_rv[ht], tag="w1dma")
            dma(k, "qpool", w3t[ht % 2].rv(), w3_rv[ht], tag="w3dma")
            for b in range(2):
                c0 = b * BLK
                ph1 = k.pb[0 + (b % 2)]
                ph3 = k.pb[2 + (b % 2)]
                for dc in range(NCH):
                    rhs = k.ln_x.rv(0, 128, dc * TOK + c0,
                                    dc * TOK + c0 + BLK)
                    mm(k, ph1.rv(),
                       w1t[ht % 2].rv(0, 128, dc * 128, (dc + 1) * 128),
                       rhs, start=(dc == 0), stop=(dc == NCH - 1), tag="h1")
                    mm(k, ph3.rv(),
                       w3t[ht % 2].rv(0, 128, dc * 128, (dc + 1) * 128),
                       rhs, start=(dc == 0), stop=(dc == NCH - 1), tag="h3")
                hslot = he.rv(0, 128, ht * TOK + c0, ht * TOK + c0 + BLK)
                if routed:
                    tt(k, t1.rv(), ph1.rv(), sbc[b].rv(), ALU.mult,
                       tag="h1ev")
                    tt(k, t3.rv(), ph3.rv(), sbc[b].rv(), ALU.mult,
                       tag="h3ev")
                    act(k, s1t.rv(), t1.rv(), AF.Silu, tag="silu")
                    tt(k, hslot, s1t.rv(), t3.rv(), ALU.mult, tag="he")
                else:
                    act(k, s1t.rv(), ph1.rv(), AF.Silu, tag="silu")
                    copy(k, t3.rv(), ph3.rv(), tag="h3ev")
                    tt(k, hslot, s1t.rv(), t3.rv(), ALU.mult, tag="he")
        for ot in range(NCH):
            dma(k, "qpool", w2t[ot % 2].rv(), w2_rv[ot], tag="w2dma")
            for b in range(2):
                c0 = b * BLK
                po = k.pb[4 + (b % 2)]
                for hc in range(8):
                    mm(k, po.rv(),
                       w2t[ot % 2].rv(0, 128, hc * 128, (hc + 1) * 128),
                       he.rv(0, 128, hc * TOK + c0, hc * TOK + c0 + BLK),
                       start=(hc == 0), stop=(hc == 7), tag="w2")
                tmp = k.scr[b % 2]
                act(k, tmp.rv(0, 128, 0, BLK), po.rv(), AF.Copy, scale=0.5,
                    tag="oev")
                xc = k.x_res.rv(0, 128, ot * TOK + c0, ot * TOK + c0 + BLK)
                tt(k, xc, tmp.rv(0, 128, 0, BLK), xc, ALU.add, tag="oadd")

    for e in range(8):
        expert(e, W["w1"][e], W["w3"][e], W["w2"][e], True)
    expert(1, W["sw1"], W["sw3"], W["sw2"], False)
    A.release()


def _rope(k, dst_rv, psum, rot, ctab, stab):
    act(k, k.sbqr.rv(), psum.rv(), AF.Copy, tag="rope_ev")
    mm(k, k.pb[7].rv(), rot.rv(), k.sbqr.rv(), tag="rope_mm")
    tt(k, k.sbta.rv(), k.sbqr.rv(), ctab.rv(0, 128, 0, BLK), ALU.mult,
       tag="rope_c")
    tt(k, k.sbtb.rv(), k.pb[7].rv(), stab.rv(0, 128, 0, BLK), ALU.mult,
       tag="rope_s")
    tt(k, dst_rv, k.sbta.rv(), k.sbtb.rv(), ALU.add, tag="rope_a")


def emit_kv0(k, W):
    A = k.arena
    A.mark()
    rot0 = A.alloc(128, 128, BF16)
    bk_sb = A.alloc(1, 512, BF16)
    bv_sb = A.alloc(1, 512, BF16)
    dma(k, "qsp", rot0.rv(), W["rot0"])
    dma(k, "qsp", bk_sb.rv(), W["bk"])
    dma(k, "qsp", bv_sb.rv(), W["bv"])
    kc = A.alloc(128, BLK, BF16)
    ks = A.alloc(128, BLK, BF16)
    dma(k, "qsp", kc.rv(), W["r0kc"])
    dma(k, "qsp", ks.rv(), W["r0ks"])
    k.ln_x = A.alloc(128, NCH * TOK, BF16)
    emit_ln(k)
    wkv = A.alloc(128, NCH * 512, BF16)
    cst = A.alloc(128, BLK, BF16)
    dma(k, "qsp", wkv.rv(), W["wk"], tag="wk")
    for s in range(2):
        c0 = s * BLK
        for kvh in range(4):
            pq = k.pb[0 + (kvh % 2)]
            for dc in range(NCH):
                mm(k, pq.rv(),
                   wkv.rv(0, 128, dc * 512 + kvh * 128,
                          dc * 512 + (kvh + 1) * 128),
                   k.ln_x.rv(0, 128, dc * TOK + c0, dc * TOK + c0 + BLK),
                   start=(dc == 0), stop=False, tag="kproj")
            mm(k, pq.rv(), bk_sb.rv(0, 1, kvh * 128, (kvh + 1) * 128),
               k.ones512_bf.rv(), start=False, stop=True, tag="kbias")
            _rope(k, cst.rv(), pq, rot0, kc, ks)
            rr = s * 1024 + kvh * 128
            dma(k, "qsp", (k.kv0_out[rr:rr + 128, :],
                           drect("kv0o", rr, rr + 128)),
                cst.rv(), tag="kstore")
    dma(k, "qsp", wkv.rv(), W["wv"], tag="wv")
    for s in range(2):
        c0 = s * BLK
        for tt_i in range(4):
            pv = k.pb[0 + (tt_i % 2)]
            for dc in range(NCH):
                mm(k, pv.rv(),
                   k.ln_x.rv(0, 128, dc * TOK + c0 + tt_i * 128,
                             dc * TOK + c0 + (tt_i + 1) * 128),
                   wkv.rv(0, 128, dc * 512, (dc + 1) * 512),
                   start=(dc == 0), stop=False, tag="vproj")
            mm(k, pv.rv(), k.ones_row_bf.rv(), bv_sb.rv(0, 1, 0, 512),
               start=False, stop=True, tag="vbias")
            act(k, cst.rv(), pv.rv(), AF.Copy, tag="vev")
            rr = s * 1024 + 512 + tt_i * 128
            dma(k, "qsp", (k.kv0_out[rr:rr + 128, :],
                           drect("kv0o", rr, rr + 128)),
                cst.rv(), tag="vstore")
    A.release()


def emit_kv1(k, W):
    A = k.arena
    A.mark()
    rot1 = A.alloc(128, 128, BF16)
    dma(k, "qsp", rot1.rv(), W["rot1"])
    kc = A.alloc(128, BLK, BF16)
    ks = A.alloc(128, BLK, BF16)
    dma(k, "qsp", kc.rv(), W["r1kc"])
    dma(k, "qsp", ks.rv(), W["r1ks"])
    k.ln_x = A.alloc(128, NCH * TOK, BF16)
    emit_ln(k)
    wfull = A.alloc(128, NCH * D, BF16)
    cst = A.alloc(128, BLK, BF16)
    dma(k, "qpool", wfull.rv(), W["l1wk_full"], tag="l1wk")
    for s in range(2):
        c0 = s * BLK
        for oc in range(NCH):
            pq = k.pb[0 + (oc % 2)]
            for dc in range(NCH):
                mm(k, pq.rv(),
                   wfull.rv(0, 128, dc * D + oc * 128,
                            dc * D + (oc + 1) * 128),
                   k.ln_x.rv(0, 128, dc * TOK + c0, dc * TOK + c0 + BLK),
                   start=(dc == 0), stop=(dc == NCH - 1), tag="l1k")
            _rope(k, cst.rv(), pq, rot1, kc, ks)
            rr = s * 4096 + oc * 128
            dma(k, "qsp", (k.kv1_out[rr:rr + 128, :],
                           drect("kv1o", rr, rr + 128)),
                cst.rv(), tag="l1kstore")
    dma(k, "qpool", wfull.rv(), W["l1wv_full"], tag="l1wv")
    for s in range(2):
        c0 = s * BLK
        for vg in range(4):
            for tt_i in range(4):
                pv = k.pb[0 + (tt_i % 2)]
                for dc in range(NCH):
                    mm(k, pv.rv(),
                       k.ln_x.rv(0, 128, dc * TOK + c0 + tt_i * 128,
                                 dc * TOK + c0 + (tt_i + 1) * 128),
                       wfull.rv(0, 128, dc * D + vg * 512,
                                dc * D + (vg + 1) * 512),
                       start=(dc == 0), stop=(dc == NCH - 1), tag="l1v")
                act(k, cst.rv(), pv.rv(), AF.Copy, tag="l1vev")
                rr = s * 4096 + 2048 + vg * 512 + tt_i * 128
                dma(k, "qsp", (k.kv1_out[rr:rr + 128, :],
                               drect("kv1o", rr, rr + 128)),
                    cst.rv(), tag="l1vstore")
    A.release()


def emit_l0attn(k, W):
    A = k.arena
    A.mark()
    q_sb = A.alloc(128, NCH * TOK, BF16)
    qpos = A.alloc(1, BLK, F32)
    kpos = A.alloc(128, 32, F32)
    dma(k, "qsp", qpos.rv(), W["qpos"])
    dma(k, "qsp", kpos.rv(), W["kpos"])
    A.mark()
    rot0 = A.alloc(128, 128, BF16)
    dma(k, "qsp", rot0.rv(), W["rot0"])
    r0 = {}
    k.ln_x = A.alloc(128, NCH * TOK, BF16)
    emit_ln(k)
    ln_own = k.ln_x
    bq_sb = A.alloc(1, D, BF16)
    dma(k, "qsp", bq_sb.rv(), W["bq"])
    for n in ("r0qc", "r0qs"):
        r0[n] = A.alloc(128, BLK, BF16)
        dma(k, "qsp", r0[n].rv(), W[n])
    wst = [A.alloc(128, NCH * 128, BF16) for _ in range(2)]
    for oc in range(NCH):
        dma(k, "qsp", wst[oc % 2].rv(), W["wq"][oc], tag="wqdma")
        for s in range(2):
            pq = k.pb[0 + (s % 2)]
            for dc in range(NCH):
                mm(k, pq.rv(),
                   wst[oc % 2].rv(0, 128, dc * 128, (dc + 1) * 128),
                   ln_own.rv(0, 128, dc * TOK + s * BLK,
                             dc * TOK + s * BLK + BLK),
                   start=(dc == 0), stop=False, tag="qproj")
            mm(k, pq.rv(), bq_sb.rv(0, 1, oc * 128, (oc + 1) * 128),
               k.ones512_bf.rv(), start=False, stop=True, tag="qbias")
            _rope(k, q_sb.rv(0, 128, oc * TOK + s * BLK,
                             oc * TOK + s * BLK + BLK),
                  pq, rot0, r0["r0qc"], r0["r0qs"])
    A.release()

    A.mark()
    attn_fm = A.alloc(128, NCH * TOK, BF16)
    A.mark()
    masks = A.alloc(128, 32 * BLK, BF16)
    qpb = k.scr[2]
    bcast_row(k, k.pb[6], qpos.rv(0, 1, 0, BLK), BLK, qpb, tag="qp_bc")
    for kt in range(32):
        ts(k, masks.rv(0, 128, kt * BLK, (kt + 1) * BLK),
           qpb.rv(0, 128, 0, BLK),
           (kpos.ap(0, 128, kt, kt + 1), kpos.rect(0, 128, kt, kt + 1)),
           ALU.is_ge, tag="mask")
    ksb = A.alloc(128, 4096, BF16)
    vsb = A.alloc(128, 4096, BF16)
    pt = [A.alloc(128, BLK, BF16) for _ in range(2)]
    for s in range(2):
        for kvh in range(4):
            for r in range(NCORES):
                rr = r * 2048 + s * 1024 + kvh * 128
                dma(k, "qsp", ksb.rv(0, 128, r * 512, (r + 1) * 512),
                    (k.kv0_g[rr:rr + 128, :], drect("kv0_g", rr, rr + 128)),
                    tag="kload")
                r1 = r * 2048 + s * 1024 + 512
                dma(k, "qsp", vsb.rv(0, 128, r * 512, (r + 1) * 512),
                    (k.kv0_g[r1:r1 + 512, kvh * 128:(kvh + 1) * 128]
                     .rearrange("(a p) v -> p a v", p=128),
                     drect("kv0_g", r1, r1 + 512)),
                    tag="vload")
            _max_k2(k, ksb, 0, 128)
            for g in range(4):
                qh = kvh * 4 + g
                qrv = q_sb.rv(0, 128, qh * TOK + s * BLK,
                              qh * TOK + s * BLK + BLK)
                _strip(k, qrv, ksb, vsb, 0, 128, masks, pt,
                       attn_fm.rv(0, 128, qh * TOK + s * BLK,
                                  qh * TOK + s * BLK + BLK),
                       k.pb[4], k.pb[5])
    A.release()
    wst2 = [A.alloc(128, NCH * 128, BF16) for _ in range(2)]
    for oc in range(NCH):
        dma(k, "qsp", wst2[oc % 2].rv(), W["wo"][oc], tag="wodma")
        for b in range(2):
            po = k.pb[0 + (b % 2)]
            for ac_ in range(NCH):
                mm(k, po.rv(),
                   wst2[oc % 2].rv(0, 128, ac_ * 128, (ac_ + 1) * 128),
                   attn_fm.rv(0, 128, ac_ * TOK + b * BLK,
                              ac_ * TOK + b * BLK + BLK),
                   start=(ac_ == 0), stop=(ac_ == NCH - 1), tag="wo")
            xc = k.x_res.rv(0, 128, oc * TOK + b * BLK,
                            oc * TOK + b * BLK + BLK)
            tt(k, xc, po.rv(), xc, ALU.add, tag="wo_add")
    A.release()
    A.release()


def _max_k2(k, ksb, p0, p1):
    mk2 = k.rows[4]
    cm = k.rows[3]
    for kc in range(8):
        sq = k.scr[0]
        act(k, sq.rv(p0, p1, 0, BLK),
            ksb.rv(p0, p1, kc * 512, (kc + 1) * 512), AF.Square, tag="nk_sq")
        mm(k, k.pb[6].rv(0, 1, 0, BLK), k.ones_col_f32.rv(p0, p1, 0, 1),
           sq.rv(p0, p1, 0, BLK), tag="nk_mm")
        if kc == 0:
            reduce(k, mk2.rv(0, 1, 0, 1), k.pb[6].rv(0, 1, 0, BLK),
                   ALU.max, tag="nk_red")
        else:
            reduce(k, cm.rv(0, 1, 0, 1), k.pb[6].rv(0, 1, 0, BLK),
                   ALU.max, tag="nk_red")
            tt(k, mk2.rv(0, 1, 0, 1), mk2.rv(0, 1, 0, 1), cm.rv(0, 1, 0, 1),
               ALU.max, tag="nk_max")


def _strip(k, q_rv, ksb, vsb, p_off, kdim, masks, pt, out_rv, p_o, p_l):
    p1 = p_off + kdim
    sqs = k.scr[0]
    act(k, sqs.rv(p_off, p1, 0, BLK), q_rv, AF.Square, tag="nq_sq")
    mm(k, k.pb[6].rv(0, 1, 0, BLK), k.ones_col_f32.rv(p_off, p1, 0, 1),
       sqs.rv(p_off, p1, 0, BLK), tag="nq_mm")
    nb = k.rows[2]
    ts(k, nb.rv(0, 1, 0, BLK), k.pb[6].rv(0, 1, 0, BLK),
       (k.rows[4].ap(0, 1, 0, 1), k.rows[4].rect(0, 1, 0, 1)), ALU.mult,
       tag="bound")
    act(k, nb.rv(0, 1, 0, BLK), nb.rv(0, 1, 0, BLK), AF.Sqrt, tag="bound")
    reduce(k, nb.rv(0, 1, 0, 1), nb.rv(0, 1, 0, BLK), ALU.max, negate=True,
           tag="bound")
    mm(k, k.pb[6].rv(0, 128, 0, 1), k.ones_row_f32.rv(), nb.rv(0, 1, 0, 1),
       tag="bound_bc")
    copy(k, k.nb_col.rv(), k.pb[6].rv(0, 128, 0, 1), tag="bound_ev")

    def s_mm(kt):
        p_s = k.pb[0 + (kt % 2)]
        mm(k, p_s.rv(),
           ksb.rv(p_off, p1, kt * 128, (kt + 1) * 128), q_rv, tag="s_mm")

    s_mm(0)
    for kt in range(32):
        if kt + 1 < 32:
            s_mm(kt + 1)
        p_s = k.pb[0 + (kt % 2)]
        act(k, pt[kt % len(pt)].rv(), p_s.rv(), AF.Exp,
            bias=(k.nb_col.ap(), k.nb_col.rect()), tag="exp")
        if masks is not None:
            tt(k, pt[kt % len(pt)].rv(), pt[kt % len(pt)].rv(),
               masks.rv(0, 128, kt * BLK, (kt + 1) * BLK), ALU.mult,
               tag="pmask")
        mm(k, p_l.rv(0, 1, 0, BLK), k.ones_col_bf.rv(), pt[kt % len(pt)].rv(),
           start=(kt == 0), stop=(kt == 31), tag="l_mm")
        mm(k, p_o.rv(), vsb.rv(0, 128, kt * 128, (kt + 1) * 128),
           pt[kt % len(pt)].rv(), start=(kt == 0), stop=(kt == 31), tag="pv")
    if out_rv is None:
        return
    rl = k.rows[2]
    ts(k, rl.rv(0, 1, 0, BLK), p_l.rv(0, 1, 0, BLK), L_GUARD, ALU.add,
       tag="lg")
    recip(k, rl.rv(0, 1, 0, BLK), rl.rv(0, 1, 0, BLK), tag="rl")
    bcast_row(k, k.pb[6], rl.rv(0, 1, 0, BLK), BLK, k.bc1, tag="rl_bc")
    tt(k, out_rv, p_o.rv(), k.bc1.rv(0, 128, 0, BLK), ALU.mult, tag="onorm")


def emit_l1ffn2(k, W):
    A = k.arena
    A.mark()
    k.ln_x = A.alloc(128, NCH * TOK, BF16)
    emit_ln(k)
    g1 = A.alloc(128, BLK, BF16)
    u1 = A.alloc(128, BLK, BF16)
    for half in range(2):
        A.mark()
        gated = A.alloc(128, 32 * TOK, BF16)
        A.mark()
        wg = [A.alloc(128, NCH * 128, BF16) for _ in range(2)]
        wu = [A.alloc(128, NCH * 128, BF16) for _ in range(2)]
        for sl in range(32):
            gsl = half * 32 + sl
            dma(k, "qpool", wg[sl % 2].rv(), W["wg"][gsl], tag="wgdma")
            dma(k, "qpool", wu[sl % 2].rv(), W["wu"][gsl], tag="wudma")
            for b in range(2):
                pg = k.pb[0 + (b % 2)]
                pu = k.pb[2 + (b % 2)]
                for dc in range(NCH):
                    rhs = k.ln_x.rv(0, 128, dc * TOK + b * BLK,
                                    dc * TOK + b * BLK + BLK)
                    mm(k, pg.rv(),
                       wg[sl % 2].rv(0, 128, dc * 128, (dc + 1) * 128),
                       rhs, start=(dc == 0), stop=(dc == NCH - 1), tag="g")
                    mm(k, pu.rv(),
                       wu[sl % 2].rv(0, 128, dc * 128, (dc + 1) * 128),
                       rhs, start=(dc == 0), stop=(dc == NCH - 1), tag="u")
                act(k, g1.rv(), pg.rv(), AF.Silu, tag="silu")
                copy(k, u1.rv(), pu.rv(), tag="uev")
                tt(k, gated.rv(0, 128, sl * TOK + b * BLK,
                               sl * TOK + b * BLK + BLK),
                   g1.rv(), u1.rv(), ALU.mult, tag="gu")
        A.release()
        wd = [A.alloc(128, 32 * 128, BF16) for _ in range(2)]
        for ot in range(NCH):
            dma(k, "qpool", wd[ot % 2].rv(), W["wd"][half][ot], tag="wddma")
            for b in range(2):
                po = k.pb[4 + (b % 2)]
                for hc in range(32):
                    mm(k, po.rv(),
                       wd[ot % 2].rv(0, 128, hc * 128, (hc + 1) * 128),
                       gated.rv(0, 128, hc * TOK + b * BLK,
                                hc * TOK + b * BLK + BLK),
                       start=(hc == 0), stop=(hc == 31), tag="wd")
                tmp = k.scr[b % 2]
                act(k, tmp.rv(0, 128, 0, BLK), po.rv(), AF.Copy, scale=0.5,
                    tag="dev")
                xc = k.x_res.rv(0, 128, ot * TOK + b * BLK,
                                ot * TOK + b * BLK + BLK)
                tt(k, xc, tmp.rv(0, 128, 0, BLK), xc, ALU.add, tag="dadd")
        A.release()
    A.release()


def emit_l1attn(k, W):
    A = k.arena
    A.mark()
    rot1 = A.alloc(128, 128, BF16)
    dma(k, "qsp", rot1.rv(), W["rot1"])
    lamv = {}
    for n in ("lq1", "lk1", "lq2", "lk2"):
        lamv[n] = A.alloc(1, 64, F32)
        dma(k, "qsp", lamv[n].rv(), W[n])
    gam = A.alloc(128, 1, F32)
    dma(k, "qsp", gam.rv(), W["gamc"])
    lam = A.alloc(1, 1, F32)
    g1l = A.alloc(128, 1, F32)
    r = k.rows
    tt(k, r[0].rv(0, 1, 0, 64), lamv["lq1"].rv(), lamv["lk1"].rv(),
       ALU.mult, tag="lam")
    reduce(k, r[1].rv(0, 1, 0, 1), r[0].rv(0, 1, 0, 64), ALU.add, tag="lam")
    act(k, r[2].rv(0, 1, 0, 1), r[1].rv(0, 1, 0, 1), AF.Exp, tag="lam")
    tt(k, r[0].rv(0, 1, 0, 64), lamv["lq2"].rv(), lamv["lk2"].rv(),
       ALU.mult, tag="lam")
    reduce(k, r[1].rv(0, 1, 0, 1), r[0].rv(0, 1, 0, 64), ALU.add, tag="lam")
    act(k, r[3].rv(0, 1, 0, 1), r[1].rv(0, 1, 0, 1), AF.Exp, tag="lam")
    tt(k, lam.rv(0, 1, 0, 1), r[2].rv(0, 1, 0, 1), r[3].rv(0, 1, 0, 1),
       ALU.subtract, tag="lam")
    ts(k, lam.rv(0, 1, 0, 1), lam.rv(0, 1, 0, 1), LAMBDA_INIT, ALU.add,
       tag="lam")
    ts(k, g1l.rv(), gam.rv(), 1.0 - LAMBDA_INIT, ALU.mult, tag="gam")

    q_sb = A.alloc(128, NCH * TOK, BF16)
    A.mark()
    k.ln_x = A.alloc(128, NCH * TOK, BF16)
    emit_ln(k)
    ln_own = k.ln_x
    r1q = {}
    for n in ("r1qc", "r1qs"):
        r1q[n] = A.alloc(128, BLK, BF16)
        dma(k, "qsp", r1q[n].rv(), W[n])
    wst = [A.alloc(128, NCH * 128, BF16) for _ in range(2)]
    for oc in range(NCH):
        dma(k, "qpool", wst[oc % 2].rv(), W["l1wq"][oc], tag="l1wqdma")
        for s in range(2):
            pq = k.pb[0 + (s % 2)]
            for dc in range(NCH):
                mm(k, pq.rv(),
                   wst[oc % 2].rv(0, 128, dc * 128, (dc + 1) * 128),
                   ln_own.rv(0, 128, dc * TOK + s * BLK,
                             dc * TOK + s * BLK + BLK),
                   start=(dc == 0), stop=(dc == NCH - 1), tag="l1q")
            _rope(k, q_sb.rv(0, 128, oc * TOK + s * BLK,
                             oc * TOK + s * BLK + BLK),
                  pq, rot1, r1q["r1qc"], r1q["r1qs"])
    A.release()

    A.mark()
    attn_fm = A.alloc(128, NCH * TOK, BF16)
    A.mark()
    ksb = A.alloc(128, 4096, BF16)
    vsb = A.alloc(128, 4096, BF16)
    pt = [A.alloc(128, BLK, BF16) for _ in range(3)]
    a1 = A.alloc(128, BLK, F32)
    a2 = A.alloc(128, BLK, F32)
    for s in range(2):
        for h in range(16):
            for rr in range(NCORES):
                r0_ = rr * 8192 + s * 4096 + h * 128
                dma(k, "qsp", ksb.rv(0, 128, rr * 512, (rr + 1) * 512),
                    (k.kv1_g[r0_:r0_ + 128, :],
                     drect("kv1_g", r0_, r0_ + 128)), tag="l1kload")
                g = h // 4
                co = (h % 4) * 128
                r1_ = rr * 8192 + s * 4096 + 2048 + g * 512
                dma(k, "qsp", vsb.rv(0, 128, rr * 512, (rr + 1) * 512),
                    (k.kv1_g[r1_:r1_ + 512, co:co + 128]
                     .rearrange("(a p) v -> p a v", p=128),
                     drect("kv1_g", r1_, r1_ + 512)), tag="l1vload")
            for comp in range(2):
                pof = comp * 64
                _max_k2(k, ksb, pof, pof + 64)
                qrv = q_sb.rv(pof, pof + 64, h * TOK + s * BLK,
                              h * TOK + s * BLK + BLK)
                p_o = k.pb[4] if comp == 0 else k.pb[3]
                p_l = k.pb[5] if comp == 0 else k.pb[7]
                _strip(k, qrv, ksb, vsb, pof, 64, None, pt, None, p_o, p_l)
            rl1, rl2, sd = k.rows[0], k.rows[1], k.rows[2]
            ts(k, rl1.rv(0, 1, 0, BLK), k.pb[5].rv(0, 1, 0, BLK), L_GUARD,
               ALU.add, tag="lg1")
            recip(k, rl1.rv(0, 1, 0, BLK), rl1.rv(0, 1, 0, BLK), tag="l1rl")
            ts(k, rl2.rv(0, 1, 0, BLK), k.pb[7].rv(0, 1, 0, BLK), L_GUARD,
               ALU.add, tag="lg2")
            recip(k, rl2.rv(0, 1, 0, BLK), rl2.rv(0, 1, 0, BLK), tag="l1rl")
            ts(k, rl2.rv(0, 1, 0, BLK), rl2.rv(0, 1, 0, BLK),
               (lam.ap(0, 1, 0, 1), lam.rect(0, 1, 0, 1)), ALU.mult,
               tag="l1rl2")
            bcast_row(k, k.pb[6], rl1.rv(0, 1, 0, BLK), BLK, k.bc1)
            bcast_row(k, k.pb[6], rl2.rv(0, 1, 0, BLK), BLK, k.bc2)
            tt(k, a1.rv(), k.pb[4].rv(), k.bc1.rv(0, 128, 0, BLK), ALU.mult,
               tag="l1o1")
            tt(k, a2.rv(), k.pb[3].rv(), k.bc2.rv(0, 128, 0, BLK), ALU.mult,
               tag="l1o2")
            tt(k, a1.rv(), a1.rv(), a2.rv(), ALU.subtract, tag="l1sub")
            act(k, a2.rv(), a1.rv(), AF.Square, tag="l1rms")
            mm(k, k.pb[6].rv(0, 1, 0, BLK), k.ones_col_f32.rv(), a2.rv(),
               tag="l1rms")
            act(k, sd.rv(0, 1, 0, BLK), k.pb[6].rv(0, 1, 0, BLK), AF.Sqrt,
                scale=1.0 / 128.0,
                bias=(k.eps_col.ap(0, 1, 0, 1),
                      k.eps_col.rect(0, 1, 0, 1)), tag="l1rms")
            recip(k, sd.rv(0, 1, 0, BLK), sd.rv(0, 1, 0, BLK), tag="l1rms")
            bcast_row(k, k.pb[6], sd.rv(0, 1, 0, BLK), BLK, k.bc1)
            tt(k, a1.rv(), a1.rv(), k.bc1.rv(0, 128, 0, BLK), ALU.mult,
               tag="l1rms2")
            ts(k, attn_fm.rv(0, 128, h * TOK + s * BLK,
                             h * TOK + s * BLK + BLK),
               a1.rv(), (g1l.ap(), g1l.rect()), ALU.mult, tag="l1gam")
    A.release()
    wst2 = [A.alloc(128, NCH * 128, BF16) for _ in range(2)]
    for oc in range(NCH):
        dma(k, "qpool", wst2[oc % 2].rv(), W["l1wo"][oc], tag="l1wodma")
        for b in range(2):
            po = k.pb[0 + (b % 2)]
            for ac_ in range(NCH):
                mm(k, po.rv(),
                   wst2[oc % 2].rv(0, 128, ac_ * 128, (ac_ + 1) * 128),
                   attn_fm.rv(0, 128, ac_ * TOK + b * BLK,
                              ac_ * TOK + b * BLK + BLK),
                   start=(ac_ == 0), stop=(ac_ == NCH - 1), tag="l1wo")
            xc = k.x_res.rv(0, 128, oc * TOK + b * BLK,
                            oc * TOK + b * BLK + BLK)
            tt(k, xc, po.rv(), xc, ALU.add, tag="l1wo_add")
    A.release()
    A.release()



SUBLAYERS = ["moe", "l0attn", "moe", "l1ffn", "l1attn", "l1ffn"]
ARENA_BYTES = 206 * 1024


def build_program(schedule, in_dtype):
    nc = bass.Bass("TRN2", target_bir_lowering=False, debug=False)
    need = set(schedule)

    x_in = nc.dram_tensor("x", [D, TOK], in_dtype, kind="ExternalInput")
    y_out = nc.dram_tensor("y", [D, TOK], F32, kind="ExternalOutput")
    k_dbg = (nc.dram_tensor("dbg", [128, 32768], F32, kind="ExternalOutput")
             if DBG is not None else None)

    k = Ctx()
    k.P = Prog()
    k.dbg = k_dbg
    k.dbg_used = [0]
    din = {}

    def inp(name, shape, dt):
        din[name] = nc.dram_tensor(name, shape, dt, kind="ExternalInput")
        return din[name]

    inp("ident", [128, 128], F32)
    if "kv0" in need:
        inp("rot0", [128, 128], BF16)
        for n in ("r0kc", "r0ks"):
            inp(n, [128, BLK], BF16)
        inp("wk", [D, 512], BF16)
        inp("wv", [D, 512], BF16)
        inp("bk", [1, 512], BF16)
        inp("bv", [1, 512], BF16)
        k.kv0_out = nc.dram_tensor("kv0", [2048, 512], BF16,
                                   kind="ExternalOutput")
    if "kv1" in need:
        if "rot1" not in din:
            inp("rot1", [128, 128], BF16)
        for n in ("r1kc", "r1ks"):
            inp(n, [128, BLK], BF16)
        inp("l1wk", [D, D], F32)
        inp("l1wv", [D, D], F32)
        k.kv1_out = nc.dram_tensor("kv1", [8192, 512], BF16,
                                   kind="ExternalOutput")
    if "l0attn" in need:
        inp("qpos", [1, BLK], F32)
        inp("kpos", [128, 32], F32)
        if "rot0" not in din:
            inp("rot0", [128, 128], BF16)
        for n in ("r0qc", "r0qs"):
            inp(n, [128, BLK], BF16)
        inp("wq", [D, D], BF16)
        inp("wo", [D, D], BF16)
        inp("bq", [1, D], BF16)
        k.kv0_g = nc.dram_tensor("kv0_g", [16384, 512], BF16,
                                 kind="ExternalInput")
    if "moe" in need:
        inp("gw", [D, 8], F32)
        inp("sel8", [8, 1024], F32)
        inp("eb", [1, 8], F32)
        inp("w1", [8, D, 1024], F32)
        inp("w3", [8, D, 1024], F32)
        inp("w2", [8, 1024, D], F32)
        inp("sw1", [D, 1024], F32)
        inp("sw3", [D, 1024], F32)
        inp("sw2", [1024, D], F32)
    if "l1ffn" in need:
        inp("wgt", [D, 8192], F32)
        inp("wut", [D, 8192], F32)
        inp("wdt", [8192, D], F32)
    if "l1attn" in need:
        if "rot1" not in din:
            inp("rot1", [128, 128], BF16)
        for n in ("r1qc", "r1qs"):
            inp(n, [128, BLK], BF16)
        inp("l1wq", [D, D], F32)
        inp("l1wo", [D, D], F32)
        for n in ("lq1", "lk1", "lq2", "lk2"):
            inp(n, [1, 64], F32)
        inp("gamc", [128, 1], F32)
        k.kv1_g = nc.dram_tensor("kv1_g", [65536, 512], BF16,
                                 kind="ExternalInput")

    with (
        nc.sbuf_tensor("arena", [128, ARENA_BYTES // 4], F32) as arena_h,
        nc.psum_tensor("pb0", [128, 512], F32) as pb0,
        nc.psum_tensor("pb1", [128, 512], F32) as pb1,
        nc.psum_tensor("pb2", [128, 512], F32) as pb2,
        nc.psum_tensor("pb3", [128, 512], F32) as pb3,
        nc.psum_tensor("pb4", [128, 512], F32) as pb4,
        nc.psum_tensor("pb5", [128, 512], F32) as pb5,
        nc.psum_tensor("pb6", [128, 512], F32) as pb6,
        nc.psum_tensor("pb7", [128, 512], F32) as pb7,
        nc.semaphore("s_pe") as s_pe,
        nc.semaphore("s_dve") as s_dve,
        nc.semaphore("s_act") as s_act,
        nc.semaphore("s_pool") as s_pool,
        nc.semaphore("s_sp") as s_sp,
        nc.semaphore("q_sp") as q_sp,
        nc.semaphore("q_act") as q_act,
        nc.semaphore("q_pool") as q_pool,
        nc.semaphore("q_cc") as q_cc,
        nc.Block() as block,
    ):
        sems = {"pe": s_pe, "dve": s_dve, "act": s_act, "pool": s_pool,
                "sp": s_sp, "qsp": q_sp, "qact": q_act, "qpool": q_pool,
                "qcc": q_cc}
        A = Arena(arena_h, "S:arena", ARENA_BYTES)
        k.arena = A
        k.pb = [Tile(h, f"P:pb{i}", True, 0, 128, 0, 512, F32)
                for i, h in enumerate([pb0, pb1, pb2, pb3, pb4, pb5, pb6,
                                       pb7])]
        k.x_res = A.alloc(128, NCH * TOK, F32)
        k.scr = [A.alloc(128, BLK, F32) for _ in range(3)]
        k.rows = [A.alloc(1, BLK, F32) for _ in range(5)]
        k.bc1 = A.alloc(128, BLK, F32)
        k.bc2 = A.alloc(128, BLK, F32)
        k.sbta = A.alloc(128, BLK, BF16)
        k.sbtb = A.alloc(128, BLK, BF16)
        k.sbqr = A.alloc(128, BLK, BF16)
        k.nb_col = A.alloc(128, 1, F32)
        k.eps_col = A.alloc(128, 1, F32)
        k.ones_col_f32 = A.alloc(128, 1, F32)
        k.ones_col_bf = A.alloc(128, 1, BF16)
        k.ones_row_f32 = A.alloc(1, 128, F32)
        k.ones_row_bf = A.alloc(1, 128, BF16)
        k.ones512_bf = A.alloc(1, BLK, BF16)
        k.ident = A.alloc(128, 128, F32)

        memset(k, k.eps_col.rv(), EPS)
        memset(k, k.ones_col_f32.rv(), 1.0)
        memset(k, k.ones_col_bf.rv(), 1.0)
        memset(k, k.ones_row_f32.rv(), 1.0)
        memset(k, k.ones_row_bf.rv(), 1.0)
        memset(k, k.ones512_bf.rv(), 1.0)
        dma(k, "qsp", k.ident.rv(), (din["ident"].ap(), drect("ident")))

        x_src = (x_in.ap().rearrange("(c p) t -> p c t", p=128),
                 drect("x"))
        if in_dtype == F32:
            dma(k, "qsp", k.x_res.rv(), x_src, tag="xload")
        else:
            dma(k, "qpool", k.x_res.rv(), x_src, tag="xload")

        W = {}
        for n in din:
            pass
        if "moe" in need:
            W["gw"] = (din["gw"].ap().rearrange("(c p) e -> p c e", p=128),
                       drect("gw"))
            W["eb"] = (din["eb"].ap(), drect("eb"))
            W["sel8"] = (din["sel8"].ap(), drect("sel8"))
            W["w1"] = [[(din["w1"][e][:, ht * 128:(ht + 1) * 128]
                         .rearrange("(c p) h -> p c h", p=128), drect("w1"))
                        for ht in range(8)] for e in range(8)]
            W["w3"] = [[(din["w3"][e][:, ht * 128:(ht + 1) * 128]
                         .rearrange("(c p) h -> p c h", p=128), drect("w3"))
                        for ht in range(8)] for e in range(8)]
            W["w2"] = [[(din["w2"][e][:, ot * 128:(ot + 1) * 128]
                         .rearrange("(c p) o -> p c o", p=128), drect("w2"))
                        for ot in range(NCH)] for e in range(8)]
            W["sw1"] = [(din["sw1"][:, ht * 128:(ht + 1) * 128]
                         .rearrange("(c p) h -> p c h", p=128), drect("sw1"))
                        for ht in range(8)]
            W["sw3"] = [(din["sw3"][:, ht * 128:(ht + 1) * 128]
                         .rearrange("(c p) h -> p c h", p=128), drect("sw3"))
                        for ht in range(8)]
            W["sw2"] = [(din["sw2"][:, ot * 128:(ot + 1) * 128]
                         .rearrange("(c p) o -> p c o", p=128), drect("sw2"))
                        for ot in range(NCH)]
        if "kv0" in need:
            for n in ("rot0", "r0kc", "r0ks", "bk", "bv"):
                W[n] = (din[n].ap(), drect(n))
            W["wk"] = (din["wk"].ap().rearrange("(c p) o -> p c o", p=128),
                       drect("wk"))
            W["wv"] = (din["wv"].ap().rearrange("(c p) o -> p c o", p=128),
                       drect("wv"))
        if "kv1" in need:
            for n in ("rot1", "r1kc", "r1ks"):
                W[n] = (din[n].ap(), drect(n))
            W["l1wk_full"] = (din["l1wk"].ap()
                              .rearrange("(c p) o -> p c o", p=128),
                              drect("l1wk"))
            W["l1wv_full"] = (din["l1wv"].ap()
                              .rearrange("(c p) o -> p c o", p=128),
                              drect("l1wv"))
        if "l0attn" in need:
            for n in ("qpos", "kpos", "rot0", "r0qc", "r0qs", "bq"):
                W[n] = (din[n].ap(), drect(n))
            W["wq"] = [(din["wq"][:, oc * 128:(oc + 1) * 128]
                        .rearrange("(c p) o -> p c o", p=128), drect("wq"))
                       for oc in range(NCH)]
            W["wo"] = [(din["wo"][:, oc * 128:(oc + 1) * 128]
                        .rearrange("(c p) o -> p c o", p=128), drect("wo"))
                       for oc in range(NCH)]
        if "l1ffn" in need:
            W["wg"] = [(din["wgt"][:, sl * 128:(sl + 1) * 128]
                        .rearrange("(c p) h -> p c h", p=128),
                        drect("wgt")) for sl in range(64)]
            W["wu"] = [(din["wut"][:, sl * 128:(sl + 1) * 128]
                        .rearrange("(c p) h -> p c h", p=128),
                        drect("wut")) for sl in range(64)]
            W["wd"] = [[(din["wdt"][half * 4096:(half + 1) * 4096,
                                    ot * 128:(ot + 1) * 128]
                         .rearrange("(c p) o -> p c o", p=128),
                         drect("wdt")) for ot in range(NCH)]
                       for half in range(2)]
        if "l1attn" in need:
            for n in ("rot1", "r1qc", "r1qs", "lq1",
                      "lk1", "lq2", "lk2", "gamc"):
                W[n] = (din[n].ap(), drect(n))
            W["l1wq"] = [(din["l1wq"][:, oc * 128:(oc + 1) * 128]
                          .rearrange("(c p) o -> p c o", p=128),
                          drect("l1wq")) for oc in range(NCH)]
            W["l1wo"] = [(din["l1wo"][:, oc * 128:(oc + 1) * 128]
                          .rearrange("(c p) o -> p c o", p=128),
                          drect("l1wo")) for oc in range(NCH)]

        for name in schedule:
            if name == "moe":
                emit_moe(k, W)
            elif name == "kv0":
                emit_kv0(k, W)
            elif name == "kv1":
                emit_kv1(k, W)
            elif name == "l0attn":
                emit_l0attn(k, W)
            elif name == "l1ffn":
                emit_l1ffn2(k, W)
            elif name == "l1attn":
                emit_l1attn(k, W)

        dma(k, "qsp", (y_out.ap().rearrange("(c p) t -> p c t", p=128),
                       drect("y")), k.x_res.rv(), tag="ystore")

        final = k.P.finalize()
        nq = dict(k.P.qcount)

        def emit_engine(eng_obj, ename, tail=False):
            for op, ws in final[ename]:
                for sem, val in ws:
                    eng_obj.wait_ge(sems[sem], val)
                instr = op.fn(eng_obj)
                if op.queue is not None:
                    instr.then_inc(sems[op.queue], 16)
                elif op.need_inc:
                    instr.then_inc(sems[op.engine], 1)
            if tail:
                eng_obj.wait_ge(sems["qsp"], 16 * nq["qsp"])

        @block.sync
        def _(eng):
            emit_engine(eng, "sp", tail=True)

        @block.tensor
        def _(eng):
            emit_engine(eng, "pe")

        @block.vector
        def _(eng):
            emit_engine(eng, "dve")

        @block.scalar
        def _(eng):
            emit_engine(eng, "act")

        @block.gpsimd
        def _(eng):
            emit_engine(eng, "pool")

    return nc



def _rot_matrices():
    R0 = np.zeros((128, 128), dtype=np.float32)
    for d in range(64):
        R0[d, d + 64] = -1.0
        R0[d + 64, d] = 1.0
    R1 = np.zeros((128, 128), dtype=np.float32)
    for base in (0, 64):
        for d in range(32):
            R1[base + d, base + 32 + d] = -1.0
            R1[base + 32 + d, base + d] = 1.0
    return (np.ascontiguousarray(R0.T).astype(BF),
            np.ascontiguousarray(R1.T).astype(BF))


def _l0_cossin(pos):
    inv = 1.0 / (1e6 ** (np.arange(64, dtype=np.float64) / 64.0))
    ang = np.outer(pos, inv)
    ct = np.cos(ang).T
    st = np.sin(ang).T
    return np.vstack([ct, ct]), np.vstack([st, st])


def _rope_tables_l0(core):
    cos, sin = _l0_cossin(core * BLK + np.arange(BLK, dtype=np.float64))
    s = 1.0 / math.sqrt(128.0)
    return {"r0qc": (cos * s).astype(BF), "r0qs": (sin * s).astype(BF)}


def _ktables_l0(core):
    cos, sin = _l0_cossin(core * BLK + np.arange(BLK, dtype=np.float64))
    return {"r0kc": cos.astype(BF), "r0ks": sin.astype(BF)}


def _ktables_l1(core):
    cos, sin = _l1_cossin(core * BLK + np.arange(BLK, dtype=np.float64))
    return {"r1kc": cos.astype(BF), "r1ks": sin.astype(BF)}


def _l1_cossin(pos):
    angle = 1.0 / (10000.0 ** np.linspace(0.0, 1.0, 32))
    fr = pos[:, None] * angle[None, :]
    return np.tile(np.cos(fr).T, (4, 1)), np.tile(np.sin(fr).T, (4, 1))


def _rope_tables_l1(core):
    cos, sin = _l1_cossin(core * BLK + np.arange(BLK, dtype=np.float64))
    s = 1.0 / 8.0
    return {"r1qc": (cos * s).astype(BF), "r1qs": (sin * s).astype(BF)}


_PERM64 = np.concatenate([np.arange(0, 64, 2), np.arange(1, 64, 2)])


def _perm_cols(wt):
    out = wt.reshape(wt.shape[0], 32, 64)[:, :, _PERM64]
    return np.ascontiguousarray(out.reshape(wt.shape[0], 2048))


def _prep_weights(inputs):
    f32 = np.float32
    Wall = {}
    Wall["gw"] = np.ascontiguousarray(
        np.asarray(inputs["L0_ffn_gate_w"], f32).T)
    Wall["eb"] = np.asarray(inputs["L0_ffn_expert_bias"], f32)[None, :]
    Wall["w1"] = np.ascontiguousarray(
        np.asarray(inputs["L0_ffn_w1"], f32).transpose(0, 2, 1))
    Wall["w3"] = np.ascontiguousarray(
        np.asarray(inputs["L0_ffn_w3"], f32).transpose(0, 2, 1))
    Wall["w2"] = np.ascontiguousarray(
        np.asarray(inputs["L0_ffn_w2"], f32).transpose(0, 2, 1))
    Wall["sw1"] = np.ascontiguousarray(np.asarray(inputs["L0_ffn_sw1"],
                                                  f32).T)
    Wall["sw3"] = np.ascontiguousarray(np.asarray(inputs["L0_ffn_sw3"],
                                                  f32).T)
    Wall["sw2"] = np.ascontiguousarray(np.asarray(inputs["L0_ffn_sw2"],
                                                  f32).T)
    Wall["wq"] = np.ascontiguousarray(np.asarray(inputs["L0_attn_Wq"], BF))
    Wall["wk"] = np.ascontiguousarray(np.asarray(inputs["L0_attn_Wk"], BF))
    Wall["wv"] = np.ascontiguousarray(np.asarray(inputs["L0_attn_Wv"], BF))
    Wall["wo"] = np.ascontiguousarray(np.asarray(inputs["L0_attn_Wo"], BF))
    Wall["bq"] = np.asarray(inputs["L0_attn_bq"], BF)[None, :]
    Wall["bk"] = np.asarray(inputs["L0_attn_bk"], BF)[None, :]
    Wall["bv"] = np.asarray(inputs["L0_attn_bv"], BF)[None, :]
    Wall["wgt"] = np.ascontiguousarray(np.asarray(inputs["L1_ffn_w_gate"],
                                                  f32).T)
    Wall["wut"] = np.ascontiguousarray(np.asarray(inputs["L1_ffn_w_up"],
                                                  f32).T)
    Wall["wdt"] = np.ascontiguousarray(np.asarray(inputs["L1_ffn_w_down"],
                                                  f32).T)
    Wall["l1wq"] = _perm_cols(np.asarray(inputs["L1_attn_Wq"], f32).T)
    Wall["l1wk"] = _perm_cols(np.asarray(inputs["L1_attn_Wk"], f32).T)
    Wall["l1wv"] = np.ascontiguousarray(np.asarray(inputs["L1_attn_Wv"],
                                                   f32).T)
    Wall["l1wo"] = np.ascontiguousarray(np.asarray(inputs["L1_attn_Wo"],
                                                   f32).T)
    Wall["lq1"] = np.asarray(inputs["L1_attn_lambda_q1"], f32)[None, :]
    Wall["lk1"] = np.asarray(inputs["L1_attn_lambda_k1"], f32)[None, :]
    Wall["lq2"] = np.asarray(inputs["L1_attn_lambda_q2"], f32)[None, :]
    Wall["lk2"] = np.asarray(inputs["L1_attn_lambda_k2"], f32)[None, :]
    Wall["gamc"] = np.ascontiguousarray(
        np.asarray(inputs["L1_attn_gamma"], f32)[:, None])
    rot0, rot1 = _rot_matrices()
    Wall["rot0"] = rot0
    Wall["rot1"] = rot1
    Wall["ident"] = np.eye(128, dtype=np.float32)
    Wall["kpos"] = (np.arange(32, dtype=np.float32)[None, :] * 128
                    + np.arange(128, dtype=np.float32)[:, None])
    sel8 = np.zeros((8, 1024), dtype=np.float32)
    for e in range(8):
        sel8[e, e * 128:(e + 1) * 128] = 1.0
    Wall["sel8"] = sel8
    return Wall


_WNAMES = {
    "moe": ("gw", "eb", "w1", "w3", "w2", "sw1", "sw3", "sw2", "sel8"),
    "kv0": ("wk", "wv", "bk", "bv", "rot0"),
    "kv1": ("l1wk", "l1wv", "rot1"),
    "l0attn": ("wq", "wo", "bq", "rot0", "kpos"),
    "l1ffn": ("wgt", "wut", "wdt"),
    "l1attn": ("l1wq", "l1wo", "lq1", "lk1", "lq2", "lk2",
               "gamc", "rot1"),
}


_PROG_CACHE = {}


def _get_prog(group, dt_tag):
    key = (group, dt_tag)
    if key not in _PROG_CACHE:
        _PROG_CACHE[key] = build_program(list(group),
                                         BF16 if dt_tag == "bf16" else F32)
    return _PROG_CACHE[key]


def kernel(**inputs):
    x = np.asarray(inputs["L0_attn_x"], BF)
    Wall = _prep_weights(inputs)

    x_fm = []
    for c in range(NCORES):
        xc = np.concatenate([x[0, c * BLK:(c + 1) * BLK, :],
                             x[1, c * BLK:(c + 1) * BLK, :]], axis=0)
        x_fm.append(np.ascontiguousarray(xc.T))

    def in_map_for(core, schedule, x_arr, extra=None):
        m = {"x": x_arr, "ident": Wall["ident"]}
        need = set(schedule)
        if "l0attn" in need:
            m["qpos"] = (core * BLK
                         + np.arange(BLK, dtype=np.float32))[None, :]
            m.update(_rope_tables_l0(core))
        if "l1attn" in need:
            m.update(_rope_tables_l1(core))
        if "kv0" in need:
            m.update(_ktables_l0(core))
        if "kv1" in need:
            m.update(_ktables_l1(core))
        if extra:
            m.update(extra)
        for s in need:
            for n in _WNAMES[s]:
                m[n] = Wall[n]
        return m

    GROUPS = [["moe", "kv0"], ["l0attn", "moe", "l1ffn", "kv1"],
              ["l1attn", "l1ffn"]]
    cur = x_fm
    extra = None
    for gi, group in enumerate(GROUPS):
        nc = _get_prog(tuple(group), "bf16" if gi == 0 else "f32")
        in_maps = [in_map_for(c, group, cur[c], extra)
                   for c in range(NCORES)]
        res = run_bass_kernel_spmd(nc, in_maps, core_ids=list(range(NCORES)))
        cur = [np.ascontiguousarray(r["y"]) for r in res.results]
        if "kv0" in group:
            extra = {"kv0_g": np.concatenate(
                [np.asarray(r["kv0"]) for r in res.results], axis=0)}
        elif "kv1" in group:
            extra = {"kv1_g": np.concatenate(
                [np.asarray(r["kv1"]) for r in res.results], axis=0)}
        else:
            extra = None
    outs = cur

    out = np.empty((2, 4096, 2048), dtype=np.float32)
    for c in range(NCORES):
        yt = outs[c].T
        out[0, c * BLK:(c + 1) * BLK, :] = yt[:BLK]
        out[1, c * BLK:(c + 1) * BLK, :] = yt[BLK:]
    return out


# version 05 (final)
# speedup vs baseline: 1.6682x; 1.1510x over previous
import math
import numpy as np
import ml_dtypes

import concourse.bass as bass
import concourse.mybir as mybir
from concourse.bass_utils import run_bass_kernel_spmd

F32 = mybir.dt.float32
BF16 = mybir.dt.bfloat16
AF = mybir.ActivationFunctionType
ALU = mybir.AluOpType
AXL = mybir.AxisListType

BF = np.dtype(ml_dtypes.bfloat16)

D = 2048
NCH = 16
TOK = 1024
BLK = 512
NCORES = 8
EPS = 1e-5
LAMBDA_INIT = 0.8 - 0.6 * math.exp(-0.3 * 12)
L_GUARD = 1e-30

FUSED = False
DBG = None


ENGINES = ("pe", "dve", "act", "pool", "sp")
QUEUES = ("qsp", "qact", "qpool", "qcc")
QUEUE_ENGINE = {"qsp": "sp", "qact": "act", "qpool": "pool", "qcc": "pool"}


class Rect:
    __slots__ = ("key", "p0", "p1", "b0", "b1")

    def __init__(self, key, p0, p1, b0, b1):
        self.key, self.p0, self.p1, self.b0, self.b1 = key, p0, p1, b0, b1

    def overlaps(self, o):
        return (self.key == o.key and self.p0 < o.p1 and o.p0 < self.p1
                and self.b0 < o.b1 and o.b0 < self.b1)


class Op:
    __slots__ = ("engine", "fn", "reads", "writes", "queue", "eidx", "qidx",
                 "need_inc", "inc_count", "waits", "tag", "deps")

    def __init__(self, engine, fn, reads, writes, queue, tag=""):
        self.engine, self.fn = engine, fn
        self.reads, self.writes = reads, writes
        self.queue, self.tag = queue, tag
        self.eidx = -1
        self.qidx = -1
        self.need_inc = False
        self.inc_count = -1
        self.waits = {}
        self.deps = []


class Tile:
    __slots__ = ("handle", "key", "is_psum", "p0", "parts", "b0", "free",
                 "dt", "esz")

    def __init__(self, handle, key, is_psum, p0, parts, b0, free, dt):
        self.handle, self.key, self.is_psum = handle, key, is_psum
        self.p0, self.parts, self.b0, self.free, self.dt = p0, parts, b0, free, dt
        self.esz = 4 if dt == F32 else 2

    def sub(self, f0, f1):
        return Tile(self.handle, self.key, self.is_psum, self.p0, self.parts,
                    self.b0 + f0 * self.esz, f1 - f0, self.dt)

    def ap(self, p0=0, p1=None, f0=0, f1=None):
        if p1 is None:
            p1 = self.parts
        if f1 is None:
            f1 = self.free
        if self.is_psum:
            return self.handle[self.p0 + p0:self.p0 + p1, f0:f1]
        e0 = self.b0 // self.esz
        base = self.handle.ap()
        if self.dt != F32:
            base = base.bitcast(self.dt)
        return base[self.p0 + p0:self.p0 + p1, e0 + f0:e0 + f1]

    def rect(self, p0=0, p1=None, f0=0, f1=None):
        if p1 is None:
            p1 = self.parts
        if f1 is None:
            f1 = self.free
        return Rect(self.key, self.p0 + p0, self.p0 + p1,
                    self.b0 + f0 * self.esz, self.b0 + f1 * self.esz)

    def rv(self, p0=0, p1=None, f0=0, f1=None):
        return (self.ap(p0, p1, f0, f1), self.rect(p0, p1, f0, f1))


class Arena:
    def __init__(self, handle, key, nbytes):
        self.handle, self.key, self.nbytes = handle, key, nbytes
        self.off = 0
        self.marks = []
        self.peak = 0

    def alloc(self, parts, free, dt, align=64):
        esz = 4 if dt == F32 else 2
        sz = free * esz
        self.off = (self.off + align - 1) // align * align
        assert self.off + sz <= self.nbytes, \
            f"arena overflow: {self.off}+{sz} > {self.nbytes}"
        t = Tile(self.handle, self.key, False, 0, parts, self.off, free, dt)
        self.off += sz
        self.peak = max(self.peak, self.off)
        return t

    def mark(self):
        self.marks.append(self.off)

    def release(self):
        self.off = self.marks.pop()


def drect(name, r0=0, r1=1 << 30):
    return Rect("D:" + name, 0, 1, r0, r1)


class Prog:
    def __init__(self):
        self.ops = {e: [] for e in ENGINES}
        self.qcount = {q: 0 for q in QUEUES}
        self.order = []
        self.regions = {}
        self.last_q = {}

    def _dep_scan(self, op):
        deps = []
        for r in op.reads:
            for ent in self.regions.get(r.key, ()):
                if ent[1] is not None and ent[0].overlaps(r):
                    deps.append(ent[1])
        for w in op.writes:
            for ent in self.regions.get(w.key, ()):
                if ent[0].overlaps(w):
                    if ent[1] is not None:
                        deps.append(ent[1])
                    deps.extend(ent[2])
        return deps

    def _record(self, op):
        for w in op.writes:
            lst = self.regions.setdefault(w.key, [])
            keep = [ent for ent in lst
                    if not (ent[0].p0 >= w.p0 and ent[0].p1 <= w.p1
                            and ent[0].b0 >= w.b0 and ent[0].b1 <= w.b1)]
            keep.append((w, op, []))
            self.regions[w.key] = keep
        for r in op.reads:
            lst = self.regions.setdefault(r.key, [])
            for ent in lst:
                if ent[0].overlaps(r):
                    ent[2].append(op)

    def add(self, engine, fn, reads=(), writes=(), queue=None, tag=""):
        op = Op(engine, fn, list(reads), list(writes), queue, tag)
        op.deps = self._dep_scan(op)
        self._record(op)
        op.eidx = len(self.ops[engine])
        self.ops[engine].append(op)
        if queue is not None:
            op.qidx = self.qcount[queue]
            self.qcount[queue] += 1
            prev = self.last_q.get(queue)
            if prev is not None:
                op.deps.append(prev)
            self.last_q[queue] = op
        self.order.append(op)
        return op

    def finalize(self):
        for op in self.order:
            for d in op.deps:
                if d is op or d.queue is not None:
                    continue
                if d.engine == "pe" and op.engine == "pe":
                    continue
                d.need_inc = True
        counts = {e: 0 for e in ENGINES}
        for e in ENGINES:
            for op in self.ops[e]:
                if op.queue is None and op.need_inc:
                    counts[e] += 1
                    op.inc_count = counts[e]
        for op in self.order:
            waits = {}
            for d in op.deps:
                if d is op:
                    continue
                if d.queue is not None:
                    sem, val = d.queue, 16 * (d.qidx + 1)
                elif d.engine == "pe" and op.engine == "pe":
                    continue
                else:
                    if d.inc_count < 0:
                        continue
                    sem, val = d.engine, d.inc_count
                if waits.get(sem, 0) < val:
                    waits[sem] = val
            op.waits = waits
        final = {}
        for e in ENGINES:
            seen = {}
            lst = []
            for op in self.ops[e]:
                ws = []
                for sem, val in sorted(op.waits.items()):
                    if seen.get(sem, 0) >= val:
                        continue
                    seen[sem] = val
                    ws.append((sem, val))
                lst.append((op, ws))
            final[e] = lst
        return final



class Ctx:
    pass


def mm(k, out_rv, lhs_rv, rhs_rv, start=True, stop=True, tag=""):
    (oap, orect), (lap, lrect), (rap, rrect) = out_rv, lhs_rv, rhs_rv
    k.P.add("pe", lambda e: e.matmul(oap, lap, rap, start=start, stop=stop),
            reads=[lrect, rrect], writes=[orect], tag=tag)


def transpose(k, out_rv, in_rv, id_rv, tag=""):
    (oap, orect), (iap, irect), (dap, dr) = out_rv, in_rv, id_rv
    k.P.add("pe", lambda e: e.transpose(oap, iap, dap),
            reads=[irect, dr], writes=[orect], tag=tag)


def act(k, out_rv, in_rv, func, bias=None, scale=None, tag=""):
    (oap, orect), (iap, irect) = out_rv, in_rv
    reads = [irect]
    kw = {}
    if bias is not None:
        if isinstance(bias, tuple):
            kw["bias"] = bias[0]
            reads.append(bias[1])
        else:
            kw["bias"] = bias
    if scale is not None:
        kw["scale"] = scale
    k.P.add("act", lambda e: e.activation(oap, iap, func, **kw),
            reads=reads, writes=[orect], tag=tag)


def tt(k, out_rv, a_rv, b_rv, op, tag=""):
    (oap, orect), (aap, ar), (bap, br) = out_rv, a_rv, b_rv
    k.P.add("dve", lambda e: e.tensor_tensor(oap, aap, bap, op),
            reads=[ar, br], writes=[orect], tag=tag)


def ts(k, out_rv, in_rv, s1, op0, s2=None, op1=None, tag=""):
    (oap, orect), (iap, irect) = out_rv, in_rv
    reads = [irect]
    if isinstance(s1, tuple):
        s1v = s1[0]
        reads.append(s1[1])
    else:
        s1v = s1
    s2v = None
    if s2 is not None:
        if isinstance(s2, tuple):
            s2v = s2[0]
            reads.append(s2[1])
        else:
            s2v = s2

    def fn(e):
        if op1 is None:
            return e.tensor_scalar(oap, iap, s1v, None, op0)
        return e.tensor_scalar(oap, iap, s1v, s2v, op0, op1)
    k.P.add("dve", fn, reads=reads, writes=[orect], tag=tag)


def copy(k, out_rv, in_rv, engine="dve", tag=""):
    (oap, orect), (iap, irect) = out_rv, in_rv
    if engine == "dve":
        k.P.add("dve", lambda e: e.tensor_copy(oap, iap),
                reads=[irect], writes=[orect], tag=tag)
    else:
        k.P.add("act", lambda e: e.activation(oap, iap, AF.Copy),
                reads=[irect], writes=[orect], tag=tag)


def reduce(k, out_rv, in_rv, op, negate=False, tag=""):
    (oap, orect), (iap, irect) = out_rv, in_rv
    k.P.add("dve",
            lambda e: e.tensor_reduce(oap, iap, axis=AXL.X, op=op,
                                      negate=negate),
            reads=[irect], writes=[orect], tag=tag)


def recip(k, out_rv, in_rv, tag=""):
    (oap, orect), (iap, irect) = in_rv if False else (out_rv, in_rv)
    k.P.add("dve", lambda e: e.reciprocal(oap, iap),
            reads=[irect], writes=[orect], tag=tag)


def memset(k, out_rv, val, tag=""):
    (oap, orect) = out_rv
    k.P.add("dve", lambda e: e.memset(oap, val), reads=[], writes=[orect],
            tag=tag)


def dma(k, queue, out_rv, in_rv, tag=""):
    (oap, orect), (iap, irect) = out_rv, in_rv
    if iap.ndim == 3 and oap.ndim == 2:
        oap = oap.rearrange("p (c t) -> p c t", c=iap.shape[1])
    elif oap.ndim == 3 and iap.ndim == 2:
        iap = iap.rearrange("p (c t) -> p c t", c=oap.shape[1])
    k.P.add(QUEUE_ENGINE[queue], lambda e: e.dma_start(out=oap, in_=iap),
            reads=[irect], writes=[orect], queue=queue, tag=tag)


def collective(k, kind, op, in_rv, out_rv, tag=""):
    (iap, irect), (oap, orect) = in_rv, out_rv
    rg = [list(range(NCORES))]
    k.P.add("pool",
            lambda e: e.collective_compute(kind, op, replica_groups=rg,
                                           ins=[iap.opt()], outs=[oap.opt()]),
            reads=[irect], writes=[orect], queue="qcc", tag=tag)


def bcast_row(k, psum_tile, row_rv, n, sb_out, tag="", bf=False):
    ones = k.ones_row_bf if bf else k.ones_row_f32
    mm(k, psum_tile.rv(0, 128, 0, n), ones.rv(), row_rv, tag=tag)
    copy(k, sb_out.rv(0, 128, 0, n), psum_tile.rv(0, 128, 0, n), tag=tag)
    return sb_out



def emit_ln(k, gate=False):
    for b in range(2):
        c0 = b * BLK
        p_sx, p_sq = k.pb[4], k.pb[5]
        for c in range(NCH):
            xc = k.x_res.rv(0, 128, c * TOK + c0, c * TOK + c0 + BLK)
            if gate:
                sq = k.scr[c % 2]
                act(k, sq.rv(0, 128, 0, BLK), xc, AF.Square, tag="ln_sq")
                mm(k, p_sx.rv(0, 1, 0, BLK), k.ones_col_f32.rv(), xc,
                   start=(c == 0), stop=(c == NCH - 1), tag="ln_sx")
                mm(k, p_sq.rv(0, 1, 0, BLK), k.ones_col_f32.rv(),
                   sq.rv(0, 128, 0, BLK),
                   start=(c == 0), stop=(c == NCH - 1), tag="ln_sq2")
            else:
                xb = k.scrb[(2 * c) % 4]
                sqb = k.scrb[(2 * c + 1) % 4]
                act(k, xb.rv(), xc, AF.Copy, tag="ln_cpb")
                act(k, sqb.rv(), xc, AF.Square, tag="ln_sqb")
                mm(k, p_sx.rv(0, 1, 0, BLK), k.ones_col_bf.rv(), xb.rv(),
                   start=(c == 0), stop=(c == NCH - 1), tag="ln_sx")
                mm(k, p_sq.rv(0, 1, 0, BLK), k.ones_col_bf.rv(), sqb.rv(),
                   start=(c == 0), stop=(c == NCH - 1), tag="ln_sq2")
        mu, ssn, var, rstd = k.rows[0], k.rows[1], k.rows[2], k.rows[3]
        act(k, mu.rv(0, 1, 0, BLK), p_sx.rv(0, 1, 0, BLK), AF.Copy,
            scale=1.0 / D, tag="ln")
        act(k, ssn.rv(0, 1, 0, BLK), p_sq.rv(0, 1, 0, BLK), AF.Copy,
            scale=1.0 / D, tag="ln")
        act(k, var.rv(0, 1, 0, BLK), mu.rv(0, 1, 0, BLK), AF.Square,
            tag="ln")
        tt(k, var.rv(0, 1, 0, BLK), ssn.rv(0, 1, 0, BLK),
           var.rv(0, 1, 0, BLK), ALU.subtract, tag="ln")
        act(k, rstd.rv(0, 1, 0, BLK), var.rv(0, 1, 0, BLK), AF.Sqrt,
            bias=(k.eps_col.ap(0, 1, 0, 1), k.eps_col.rect(0, 1, 0, 1)),
            tag="ln")
        recip(k, rstd.rv(0, 1, 0, BLK), rstd.rv(0, 1, 0, BLK), tag="ln")
        tt(k, mu.rv(0, 1, 0, BLK), mu.rv(0, 1, 0, BLK),
           rstd.rv(0, 1, 0, BLK), ALU.mult, tag="ln")
        if gate:
            bc_r = bcast_row(k, k.pb[4], rstd.rv(0, 1, 0, BLK), BLK, k.bc1)
            bc_m = bcast_row(k, k.pb[5], mu.rv(0, 1, 0, BLK), BLK, k.bc2)
        else:
            copy(k, k.rowsb[0].rv(), rstd.rv(0, 1, 0, BLK), tag="ln_rb")
            copy(k, k.rowsb[1].rv(), mu.rv(0, 1, 0, BLK), tag="ln_mb")
            bc_r = bcast_row(k, k.pb[4], k.rowsb[0].rv(), BLK, k.bc1,
                             bf=True)
            bc_m = bcast_row(k, k.pb[5], k.rowsb[1].rv(), BLK, k.bc2,
                             bf=True)
        for c in range(NCH):
            xc = k.x_res.rv(0, 128, c * TOK + c0, c * TOK + c0 + BLK)
            t = k.scr[c % 2]
            tt(k, t.rv(0, 128, 0, BLK), xc, bc_r.rv(0, 128, 0, BLK),
               ALU.mult, tag="ln_ap1")
            lnc = k.ln_x.rv(0, 128, c * TOK + c0, c * TOK + c0 + BLK)
            if not gate:
                tt(k, lnc, t.rv(0, 128, 0, BLK), bc_m.rv(0, 128, 0, BLK),
                   ALU.subtract, tag="ln_ap2")
            else:
                lf = k.scr[2]
                tt(k, lf.rv(0, 128, 0, BLK), t.rv(0, 128, 0, BLK),
                   bc_m.rv(0, 128, 0, BLK), ALU.subtract, tag="ln_ap2")
                copy(k, lnc, lf.rv(0, 128, 0, BLK), tag="ln_cp")
                mm(k, k.pb[6].rv(0, 8, 0, BLK),
                   k.gate_sb.rv(0, 128, c * 8, (c + 1) * 8),
                   lf.rv(0, 128, 0, BLK),
                   start=(c == 0), stop=(c == NCH - 1), tag="gate_mm")
        if gate:
            act(k, k.sc_fm[b].rv(0, 8, 0, BLK), k.pb[6].rv(0, 8, 0, BLK),
                AF.Sigmoid, tag="gate_sig")


def emit_moe(k, W):
    A = k.arena
    A.mark()
    k.gate_sb = A.alloc(128, NCH * 8, F32)
    dma(k, "qsp", k.gate_sb.rv(), W["gw"])
    ebr = A.alloc(1, 8, F32)
    dma(k, "qsp", ebr.rv(), W["eb"])
    k.sel8 = A.alloc(8, 1024, BF16)
    dma(k, "qsp", k.sel8.rv(), W["sel8"])
    k.ebias_bc = A.alloc(128, 8, F32)
    bcast_row(k, k.pb[6], ebr.rv(0, 1, 0, 8), 8, k.ebias_bc)
    k.ln_x = A.alloc(128, NCH * TOK, BF16)
    k.sc_fm = [A.alloc(8, BLK, F32) for _ in range(2)]
    emit_ln(k, gate=True)

    scale_fm = [A.alloc(8, BLK, BF16) for _ in range(2)]
    sc_tm = A.alloc(128, 4 * 8, F32)
    bt = A.alloc(128, 8, F32)
    selt = A.alloc(128, 8, F32)
    m1 = A.alloc(128, 1, F32)
    m2 = A.alloc(128, 1, F32)
    for b in range(2):
        for j in range(4):
            transpose(k, k.pb[6].rv(0, 128, j * 8, (j + 1) * 8),
                      k.sc_fm[b].rv(0, 8, j * 128, (j + 1) * 128),
                      k.ident.rv(0, 8, 0, 8), tag="sc_T")
        copy(k, sc_tm.rv(), k.pb[6].rv(0, 128, 0, 32), tag="sc_T_ev")
        for j in range(4):
            stj = sc_tm.rv(0, 128, j * 8, (j + 1) * 8)
            tt(k, bt.rv(), stj, k.ebias_bc.rv(), ALU.add, tag="top2")
            reduce(k, m1.rv(), bt.rv(), ALU.max, tag="top2")
            ts(k, selt.rv(), bt.rv(), (m1.ap(), m1.rect()), ALU.is_ge,
               1e30, ALU.mult, tag="top2")
            tt(k, selt.rv(), bt.rv(), selt.rv(), ALU.subtract, tag="top2")
            reduce(k, m2.rv(), selt.rv(), ALU.max, tag="top2")
            ts(k, selt.rv(), bt.rv(), (m2.ap(), m2.rect()), ALU.is_ge,
               tag="top2")
            tt(k, selt.rv(), selt.rv(), stj, ALU.mult, tag="top2")
            transpose(k, k.pb[6].rv(0, 8, j * 128, (j + 1) * 128),
                      selt.rv(), k.ident.rv(), tag="sc_Tb")
        copy(k, scale_fm[b].rv(), k.pb[6].rv(0, 8, 0, BLK), tag="sc_Tb_ev")

    he = A.alloc(128, 8 * TOK, BF16)
    w1t = [A.alloc(128, NCH * 128, BF16) for _ in range(3)]
    w3t = [A.alloc(128, NCH * 128, BF16) for _ in range(3)]
    w2t = [A.alloc(128, 8 * 128, BF16) for _ in range(3)]
    sbc = [A.alloc(128, BLK, F32) for _ in range(4)]
    t1 = [A.alloc(128, BLK, BF16) for _ in range(2)]
    t3 = [A.alloc(128, BLK, BF16) for _ in range(2)]
    s1t = [A.alloc(128, BLK, BF16) for _ in range(2)]

    def expert(e, w1_rv, w3_rv, w2_rv, routed):
        if routed:
            for b in range(2):
                mm(k, k.pb[6].rv(0, 128, 0, BLK),
                   k.sel8.rv(0, 8, e * 128, (e + 1) * 128),
                   scale_fm[b].rv(0, 8, 0, BLK), tag="sbc_mm")
                copy(k, sbc[(e % 2) * 2 + b].rv(),
                     k.pb[6].rv(0, 128, 0, BLK),
                     tag="sbc_ev")
        for ht in range(8):
            dma(k, "qpool", w1t[ht % 3].rv(), w1_rv[ht], tag="w1dma")
            dma(k, "qpool", w3t[ht % 3].rv(), w3_rv[ht], tag="w3dma")
            for b in range(2):
                c0 = b * BLK
                u = ht * 2 + b
                ph1 = k.pb[u % 3]
                ph3 = k.pb[3 + u % 3]
                for dc in range(NCH):
                    rhs = k.ln_x.rv(0, 128, dc * TOK + c0,
                                    dc * TOK + c0 + BLK)
                    mm(k, ph1.rv(),
                       w1t[ht % 3].rv(0, 128, dc * 128, (dc + 1) * 128),
                       rhs, start=(dc == 0), stop=(dc == NCH - 1), tag="h1")
                    mm(k, ph3.rv(),
                       w3t[ht % 3].rv(0, 128, dc * 128, (dc + 1) * 128),
                       rhs, start=(dc == 0), stop=(dc == NCH - 1), tag="h3")
                hslot = he.rv(0, 128, ht * TOK + c0, ht * TOK + c0 + BLK)
                if routed:
                    tt(k, t1[u % 2].rv(), ph1.rv(),
                       sbc[(e % 2) * 2 + b].rv(), ALU.mult,
                       tag="h1ev")
                    tt(k, t3[u % 2].rv(), ph3.rv(),
                       sbc[(e % 2) * 2 + b].rv(), ALU.mult,
                       tag="h3ev")
                    act(k, s1t[u % 2].rv(), t1[u % 2].rv(), AF.Silu,
                        tag="silu")
                    tt(k, hslot, s1t[u % 2].rv(), t3[u % 2].rv(), ALU.mult,
                       tag="he")
                else:
                    act(k, s1t[u % 2].rv(), ph1.rv(), AF.Silu, tag="silu")
                    copy(k, t3[u % 2].rv(), ph3.rv(), tag="h3ev")
                    tt(k, hslot, s1t[u % 2].rv(), t3[u % 2].rv(), ALU.mult,
                       tag="he")
        for ot in range(NCH):
            dma(k, "qpool", w2t[ot % 3].rv(), w2_rv[ot], tag="w2dma")
            for b in range(2):
                c0 = b * BLK
                v = ot * 2 + b
                po = k.pb[4 + v % 4]
                for hc in range(8):
                    mm(k, po.rv(),
                       w2t[ot % 3].rv(0, 128, hc * 128, (hc + 1) * 128),
                       he.rv(0, 128, hc * TOK + c0, hc * TOK + c0 + BLK),
                       start=(hc == 0), stop=(hc == 7), tag="w2")
                tmp = k.scr[v % 3]
                act(k, tmp.rv(0, 128, 0, BLK), po.rv(), AF.Copy, scale=0.5,
                    tag="oev")
                xc = k.x_res.rv(0, 128, ot * TOK + c0, ot * TOK + c0 + BLK)
                tt(k, xc, tmp.rv(0, 128, 0, BLK), xc, ALU.add, tag="oadd")

    for e in range(8):
        expert(e, W["w1"][e], W["w3"][e], W["w2"][e], True)
    expert(1, W["sw1"], W["sw3"], W["sw2"], False)
    A.release()


def _rope(k, dst_rv, psum, rot, ctab, stab, par=0):
    sbqr, sbta, sbtb = k.ropes[par % 2]
    pbr = k.pb[6 + (par % 2)]
    act(k, sbqr.rv(), psum.rv(), AF.Copy, tag="rope_ev")
    mm(k, pbr.rv(), rot.rv(), sbqr.rv(), tag="rope_mm")
    tt(k, sbta.rv(), sbqr.rv(), ctab.rv(0, 128, 0, BLK), ALU.mult,
       tag="rope_c")
    tt(k, sbtb.rv(), pbr.rv(), stab.rv(0, 128, 0, BLK), ALU.mult,
       tag="rope_s")
    tt(k, dst_rv, sbta.rv(), sbtb.rv(), ALU.add, tag="rope_a")


def emit_kv0(k, W):
    A = k.arena
    A.mark()
    rot0 = A.alloc(128, 128, BF16)
    bk_sb = A.alloc(1, 512, BF16)
    bv_sb = A.alloc(1, 512, BF16)
    dma(k, "qsp", rot0.rv(), W["rot0"])
    dma(k, "qsp", bk_sb.rv(), W["bk"])
    dma(k, "qsp", bv_sb.rv(), W["bv"])
    kc = A.alloc(128, BLK, BF16)
    ks = A.alloc(128, BLK, BF16)
    dma(k, "qsp", kc.rv(), W["r0kc"])
    dma(k, "qsp", ks.rv(), W["r0ks"])
    k.ln_x = A.alloc(128, NCH * TOK, BF16)
    emit_ln(k)
    wkv = [A.alloc(128, NCH * 512, BF16) for _ in range(2)]
    cst = [A.alloc(128, BLK, BF16) for _ in range(2)]
    dma(k, "qpool", wkv[0].rv(), W["wk"], tag="wk")
    dma(k, "qact", wkv[1].rv(), W["wv"], tag="wv")
    u = 0
    for s in range(2):
        c0 = s * BLK
        for kvh in range(4):
            pq = k.pb[u % 3]
            for dc in range(NCH):
                mm(k, pq.rv(),
                   wkv[0].rv(0, 128, dc * 512 + kvh * 128,
                              dc * 512 + (kvh + 1) * 128),
                   k.ln_x.rv(0, 128, dc * TOK + c0, dc * TOK + c0 + BLK),
                   start=(dc == 0), stop=False, tag="kproj")
            mm(k, pq.rv(), bk_sb.rv(0, 1, kvh * 128, (kvh + 1) * 128),
               k.ones512_bf.rv(), start=False, stop=True, tag="kbias")
            _rope(k, cst[u % 2].rv(), pq, rot0, kc, ks, par=u)
            _kmax_tile(k, cst[u % 2].rv(), u)
            rr = s * 1024 + kvh * 128
            dma(k, "qsp", (k.kv0_out[rr:rr + 128, :],
                           drect("kv0o", rr, rr + 128)),
                cst[u % 2].rv(), tag="kstore")
            u += 1
    _kmax_out(k, 8, k.kmax0_out, "kmx0o")
    u = 0
    for s in range(2):
        c0 = s * BLK
        for tt_i in range(4):
            pv = k.pb[u % 3]
            for dc in range(NCH):
                mm(k, pv.rv(),
                   k.ln_x.rv(0, 128, dc * TOK + c0 + tt_i * 128,
                             dc * TOK + c0 + (tt_i + 1) * 128),
                   wkv[1].rv(0, 128, dc * 512, (dc + 1) * 512),
                   start=(dc == 0), stop=False, tag="vproj")
            mm(k, pv.rv(), k.ones_row_bf.rv(), bv_sb.rv(0, 1, 0, 512),
               start=False, stop=True, tag="vbias")
            act(k, cst[u % 2].rv(), pv.rv(), AF.Copy, tag="vev")
            rr = s * 1024 + 512 + tt_i * 128
            dma(k, "qsp", (k.kv0_out[rr:rr + 128, :],
                           drect("kv0o", rr, rr + 128)),
                cst[u % 2].rv(), tag="vstore")
            u += 1
    A.release()


def emit_kv1(k, W):
    A = k.arena
    A.mark()
    rot1 = A.alloc(128, 128, BF16)
    dma(k, "qsp", rot1.rv(), W["rot1"])
    kc = A.alloc(128, BLK, BF16)
    ks = A.alloc(128, BLK, BF16)
    dma(k, "qsp", kc.rv(), W["r1kc"])
    dma(k, "qsp", ks.rv(), W["r1ks"])
    k.ln_x = A.alloc(128, NCH * TOK, BF16)
    emit_ln(k)
    wfull = A.alloc(128, NCH * D, BF16)
    cst = [A.alloc(128, BLK, BF16) for _ in range(2)]
    dma(k, "qpool", wfull.rv(0, 128, 0, NCH * D // 2),
        W["l1wk_h"][0], tag="l1wk")
    dma(k, "qpool", wfull.rv(0, 128, NCH * D // 2, NCH * D),
        W["l1wk_h"][1], tag="l1wk")
    u = 0
    for s in range(2):
        c0 = s * BLK
        for oc in range(NCH):
            pq = k.pb[u % 3]
            for dc in range(NCH):
                mm(k, pq.rv(),
                   wfull.rv(0, 128, dc * D + oc * 128,
                            dc * D + (oc + 1) * 128),
                   k.ln_x.rv(0, 128, dc * TOK + c0, dc * TOK + c0 + BLK),
                   start=(dc == 0), stop=(dc == NCH - 1), tag="l1k")
            _rope(k, cst[u % 2].rv(), pq, rot1, kc, ks, par=u)
            _kmax_tile(k, cst[u % 2].rv(), u)
            rr = s * 4096 + oc * 128
            dma(k, "qsp", (k.kv1_out[rr:rr + 128, :],
                           drect("kv1o", rr, rr + 128)),
                cst[u % 2].rv(), tag="l1kstore")
            u += 1
    _kmax_out(k, 32, k.kmax1_out, "kmx1o")
    dma(k, "qpool", wfull.rv(0, 128, 0, NCH * D // 2),
        W["l1wv_h"][0], tag="l1wv")
    dma(k, "qpool", wfull.rv(0, 128, NCH * D // 2, NCH * D),
        W["l1wv_h"][1], tag="l1wv")
    u = 0
    for s in range(2):
        c0 = s * BLK
        for vg in range(4):
            for tt_i in range(4):
                pv = k.pb[u % 3]
                for dc in range(NCH):
                    mm(k, pv.rv(),
                       k.ln_x.rv(0, 128, dc * TOK + c0 + tt_i * 128,
                                 dc * TOK + c0 + (tt_i + 1) * 128),
                       wfull.rv(0, 128, dc * D + vg * 512,
                                dc * D + (vg + 1) * 512),
                       start=(dc == 0), stop=(dc == NCH - 1), tag="l1v")
                act(k, cst[u % 2].rv(), pv.rv(), AF.Copy, tag="l1vev")
                rr = s * 4096 + 2048 + vg * 512 + tt_i * 128
                dma(k, "qsp", (k.kv1_out[rr:rr + 128, :],
                               drect("kv1o", rr, rr + 128)),
                    cst[u % 2].rv(), tag="l1vstore")
                u += 1
    A.release()


def emit_l0attn(k, W):
    A = k.arena
    A.mark()
    q_sb = A.alloc(128, NCH * TOK, BF16)
    qpos = A.alloc(1, BLK, F32)
    kpos = A.alloc(128, 32, F32)
    dma(k, "qsp", qpos.rv(), W["qpos"])
    dma(k, "qsp", kpos.rv(), W["kpos"])
    A.mark()
    rot0 = A.alloc(128, 128, BF16)
    dma(k, "qsp", rot0.rv(), W["rot0"])
    r0 = {}
    k.ln_x = A.alloc(128, NCH * TOK, BF16)
    emit_ln(k)
    ln_own = k.ln_x
    bq_sb = A.alloc(1, D, BF16)
    dma(k, "qsp", bq_sb.rv(), W["bq"])
    for n in ("r0qc", "r0qs"):
        r0[n] = A.alloc(128, BLK, BF16)
        dma(k, "qsp", r0[n].rv(), W[n])
    wst = [A.alloc(128, NCH * 128, BF16) for _ in range(3)]
    for oc in range(NCH):
        dma(k, "qpool" if oc % 2 == 0 else "qact",
            wst[oc % 3].rv(), W["wq"][oc], tag="wqdma")
        for s in range(2):
            u = oc * 2 + s
            pq = k.pb[u % 3]
            for dc in range(NCH):
                mm(k, pq.rv(),
                   wst[oc % 3].rv(0, 128, dc * 128, (dc + 1) * 128),
                   ln_own.rv(0, 128, dc * TOK + s * BLK,
                             dc * TOK + s * BLK + BLK),
                   start=(dc == 0), stop=False, tag="qproj")
            mm(k, pq.rv(), bq_sb.rv(0, 1, oc * 128, (oc + 1) * 128),
               k.ones512_bf.rv(), start=False, stop=True, tag="qbias")
            _rope(k, q_sb.rv(0, 128, oc * TOK + s * BLK,
                             oc * TOK + s * BLK + BLK),
                  pq, rot0, r0["r0qc"], r0["r0qs"], par=u)
    A.release()

    A.mark()
    kmx = A.alloc(1, 8, F32)
    dma(k, "qsp", kmx.rv(), W["kmx0"])
    emit_qbound(k, q_sb, kmx)
    masks = A.alloc(128, 32 * BLK, BF16)
    qpb = k.scr[2]
    bcast_row(k, k.pb[6], qpos.rv(0, 1, 0, BLK), BLK, qpb, tag="qp_bc")
    for kt in range(32):
        ts(k, masks.rv(0, 128, kt * BLK, (kt + 1) * BLK),
           qpb.rv(0, 128, 0, BLK),
           (kpos.ap(0, 128, kt, kt + 1), kpos.rect(0, 128, kt, kt + 1)),
           ALU.is_ge, tag="mask")
    attn_s = A.alloc(128, NCH * BLK, BF16)
    ksb = A.alloc(128, 4096, BF16)
    vsb = A.alloc(128, 4096, BF16)
    pt = [A.alloc(128, BLK, BF16) for _ in range(3)]
    wst2 = [A.alloc(128, NCH * 128, BF16) for _ in range(2)]
    sbanks = [k.pb[0], k.pb[1], k.pb[2]]
    spar = 0
    for s in range(2):
        for kvh in range(4):
            for r in range(NCORES):
                rr = r * 2048 + s * 1024 + kvh * 128
                dma(k, "qsp", ksb.rv(0, 128, r * 512, (r + 1) * 512),
                    (k.kv0_g[rr:rr + 128, :], drect("kv0_g", rr, rr + 128)),
                    tag="kload")
                r1 = r * 2048 + s * 1024 + 512
                dma(k, "qsp", vsb.rv(0, 128, r * 512, (r + 1) * 512),
                    (k.kv0_g[r1:r1 + 512, kvh * 128:(kvh + 1) * 128]
                     .rearrange("(a p) v -> p a v", p=128),
                     drect("kv0_g", r1, r1 + 512)),
                    tag="vload")
            for g in range(4):
                qh = kvh * 4 + g
                qrv = q_sb.rv(0, 128, qh * TOK + s * BLK,
                              qh * TOK + s * BLK + BLK)
                p_o = k.pb[3] if spar % 2 == 0 else k.pb[5]
                p_l = k.pb[4] if spar % 2 == 0 else k.pb[6]
                _strip(k, qrv, ksb, vsb, 0, 128, masks, pt,
                       attn_s.rv(0, 128, qh * BLK, (qh + 1) * BLK),
                       p_o, p_l, sbanks, par=spar)
                spar += 1
        for oc in range(NCH):
            dma(k, "qpool" if oc % 2 == 0 else "qact",
                wst2[oc % 2].rv(), W["wo"][oc], tag="wodma")
            po = k.pb[oc % 4]
            for ac_ in range(NCH):
                mm(k, po.rv(),
                   wst2[oc % 2].rv(0, 128, ac_ * 128, (ac_ + 1) * 128),
                   attn_s.rv(0, 128, ac_ * BLK, (ac_ + 1) * BLK),
                   start=(ac_ == 0), stop=(ac_ == NCH - 1), tag="wo")
            xc = k.x_res.rv(0, 128, oc * TOK + s * BLK,
                            oc * TOK + s * BLK + BLK)
            tt(k, xc, po.rv(), xc, ALU.add, tag="wo_add")
    A.release()
    A.release()


def _kmax_tile(k, src_rv, slot):
    sqb = k.scrb[slot % 4]
    act(k, sqb.rv(), src_rv, AF.Square, tag="km_sq")
    pbk = k.pb[3 + slot % 3]
    mm(k, pbk.rv(0, 1, 0, BLK), k.ones_col_bf.rv(), sqb.rv(), tag="km_mm")
    reduce(k, k.rows[3].rv(0, 1, slot, slot + 1), pbk.rv(0, 1, 0, BLK),
           ALU.max, tag="km_red")


def _kmax_out(k, nslots, handle, name):
    reduce(k, k.rows[4].rv(0, 1, 0, 1), k.rows[3].rv(0, 1, 0, nslots),
           ALU.max, tag="km_fin")
    dma(k, "qsp", (handle[0:1, :], drect(name, 0, 1)),
        k.rows[4].rv(0, 1, 0, 1), tag="km_st")


def emit_qbound(k, q_sb, kmax_in):
    for c in range(32):
        sqb = k.scrb[c % 4]
        act(k, sqb.rv(), q_sb.rv(0, 128, c * BLK, (c + 1) * BLK),
            AF.Square, tag="qb_sq")
        pbq = k.pb[6 + c % 2]
        mm(k, pbq.rv(0, 1, 0, BLK), k.ones_col_bf.rv(), sqb.rv(),
           tag="qb_mm")
        reduce(k, k.rows[3].rv(0, 1, c, c + 1), pbq.rv(0, 1, 0, BLK),
               ALU.max, tag="qb_red")
    qm = k.rows[4]
    reduce(k, qm.rv(0, 1, 0, 1), k.rows[3].rv(0, 1, 0, 32), ALU.max,
           tag="qb_fin")
    km = k.rows[2]
    reduce(k, km.rv(0, 1, 0, 1), kmax_in.rv(0, 1, 0, 8), ALU.max,
           tag="qb_km")
    tt(k, qm.rv(0, 1, 0, 1), qm.rv(0, 1, 0, 1), km.rv(0, 1, 0, 1),
       ALU.mult, tag="qb_b")
    act(k, qm.rv(0, 1, 0, 1), qm.rv(0, 1, 0, 1), AF.Sqrt, tag="qb_b")
    ts(k, qm.rv(0, 1, 0, 1), qm.rv(0, 1, 0, 1), -1.0, ALU.mult, tag="qb_b")
    mm(k, k.pb[6].rv(0, 128, 0, 1), k.ones_row_f32.rv(), qm.rv(0, 1, 0, 1),
       tag="qb_bc")
    copy(k, k.nb_col.rv(), k.pb[6].rv(0, 128, 0, 1), tag="qb_ev")


def _strip(k, q_rv, ksb, vsb, p_off, kdim, masks, pt, out_rv, p_o, p_l,
           sbanks, par=0):
    p1 = p_off + kdim
    LA = len(sbanks)

    def s_mm(kt):
        mm(k, sbanks[kt % LA].rv(),
           ksb.rv(p_off, p1, kt * 128, (kt + 1) * 128), q_rv, tag="s_mm")

    for kt in range(LA - 1):
        s_mm(kt)
    for kt in range(32):
        if kt + LA - 1 < 32:
            s_mm(kt + LA - 1)
        p_s = sbanks[kt % LA]
        ptile = pt[kt % len(pt)]
        act(k, ptile.rv(), p_s.rv(), AF.Exp,
            bias=(k.nb_col.ap(), k.nb_col.rect()), tag="exp")
        if masks is not None:
            tt(k, ptile.rv(), ptile.rv(),
               masks.rv(0, 128, kt * BLK, (kt + 1) * BLK), ALU.mult,
               tag="pmask")
        mm(k, p_l.rv(0, 1, 0, BLK), k.ones_col_bf.rv(), ptile.rv(),
           start=(kt == 0), stop=(kt == 31), tag="l_mm")
        mm(k, p_o.rv(), vsb.rv(0, 128, kt * 128, (kt + 1) * 128),
           ptile.rv(), start=(kt == 0), stop=(kt == 31), tag="pv")
    if out_rv is None:
        return
    rl = k.rows[2 + (par % 2)]
    ts(k, rl.rv(0, 1, 0, BLK), p_l.rv(0, 1, 0, BLK), L_GUARD, ALU.add,
       tag="lg")
    recip(k, rl.rv(0, 1, 0, BLK), rl.rv(0, 1, 0, BLK), tag="rl")
    copy(k, k.rowsb[par % 2].rv(), rl.rv(0, 1, 0, BLK), tag="rl_b")
    bc = k.bc1 if par % 2 == 0 else k.bc2
    bcast_row(k, k.pb[7], k.rowsb[par % 2].rv(), BLK, bc, bf=True,
              tag="rl_bc")
    tt(k, out_rv, p_o.rv(), bc.rv(0, 128, 0, BLK), ALU.mult, tag="onorm")


def emit_l1ffn2(k, W):
    A = k.arena
    A.mark()
    k.ln_x = A.alloc(128, NCH * TOK, BF16)
    emit_ln(k)
    g1 = [A.alloc(128, BLK, BF16) for _ in range(2)]
    u1 = [A.alloc(128, BLK, BF16) for _ in range(2)]
    gated = A.alloc(128, 16 * TOK, BF16)
    wg = [A.alloc(128, NCH * 128, BF16) for _ in range(2)]
    wu = [A.alloc(128, NCH * 128, BF16) for _ in range(2)]
    wd = [A.alloc(128, 16 * 128, BF16) for _ in range(2)]
    for qtr in range(4):
        for sl in range(16):
            gsl = qtr * 16 + sl
            dma(k, "qpool", wg[sl % 2].rv(), W["wg"][gsl], tag="wgdma")
            dma(k, "qpool", wu[sl % 2].rv(), W["wu"][gsl], tag="wudma")
            for b in range(2):
                u = sl * 2 + b
                pg = k.pb[u % 3]
                pu = k.pb[3 + u % 3]
                for dc in range(NCH):
                    rhs = k.ln_x.rv(0, 128, dc * TOK + b * BLK,
                                    dc * TOK + b * BLK + BLK)
                    mm(k, pg.rv(),
                       wg[sl % 2].rv(0, 128, dc * 128, (dc + 1) * 128),
                       rhs, start=(dc == 0), stop=(dc == NCH - 1), tag="g")
                    mm(k, pu.rv(),
                       wu[sl % 2].rv(0, 128, dc * 128, (dc + 1) * 128),
                       rhs, start=(dc == 0), stop=(dc == NCH - 1), tag="u")
                act(k, g1[u % 2].rv(), pg.rv(), AF.Silu, tag="silu")
                copy(k, u1[u % 2].rv(), pu.rv(), tag="uev")
                tt(k, gated.rv(0, 128, sl * TOK + b * BLK,
                               sl * TOK + b * BLK + BLK),
                   g1[u % 2].rv(), u1[u % 2].rv(), ALU.mult, tag="gu")
        for ot in range(NCH):
            dma(k, "qpool", wd[ot % 2].rv(), W["wd"][qtr][ot],
                tag="wddma")
            for b in range(2):
                v = ot * 2 + b
                po = k.pb[4 + v % 4]
                for hc in range(16):
                    mm(k, po.rv(),
                       wd[ot % 2].rv(0, 128, hc * 128, (hc + 1) * 128),
                       gated.rv(0, 128, hc * TOK + b * BLK,
                                hc * TOK + b * BLK + BLK),
                       start=(hc == 0), stop=(hc == 15), tag="wd")
                tmp = k.scr[v % 3]
                act(k, tmp.rv(0, 128, 0, BLK), po.rv(), AF.Copy, scale=0.5,
                    tag="dev")
                xc = k.x_res.rv(0, 128, ot * TOK + b * BLK,
                                ot * TOK + b * BLK + BLK)
                tt(k, xc, tmp.rv(0, 128, 0, BLK), xc, ALU.add, tag="dadd")
    A.release()


def emit_l1attn(k, W):
    A = k.arena
    A.mark()
    rot1 = A.alloc(128, 128, BF16)
    dma(k, "qsp", rot1.rv(), W["rot1"])
    lamv = {}
    for n in ("lq1", "lk1", "lq2", "lk2"):
        lamv[n] = A.alloc(1, 64, F32)
        dma(k, "qsp", lamv[n].rv(), W[n])
    gam = A.alloc(128, 1, F32)
    dma(k, "qsp", gam.rv(), W["gamc"])
    lam = A.alloc(1, 1, F32)
    g1l = A.alloc(128, 1, F32)
    r = k.rows
    tt(k, r[0].rv(0, 1, 0, 64), lamv["lq1"].rv(), lamv["lk1"].rv(),
       ALU.mult, tag="lam")
    reduce(k, r[1].rv(0, 1, 0, 1), r[0].rv(0, 1, 0, 64), ALU.add, tag="lam")
    act(k, r[2].rv(0, 1, 0, 1), r[1].rv(0, 1, 0, 1), AF.Exp, tag="lam")
    tt(k, r[0].rv(0, 1, 0, 64), lamv["lq2"].rv(), lamv["lk2"].rv(),
       ALU.mult, tag="lam")
    reduce(k, r[1].rv(0, 1, 0, 1), r[0].rv(0, 1, 0, 64), ALU.add, tag="lam")
    act(k, r[3].rv(0, 1, 0, 1), r[1].rv(0, 1, 0, 1), AF.Exp, tag="lam")
    tt(k, lam.rv(0, 1, 0, 1), r[2].rv(0, 1, 0, 1), r[3].rv(0, 1, 0, 1),
       ALU.subtract, tag="lam")
    ts(k, lam.rv(0, 1, 0, 1), lam.rv(0, 1, 0, 1), LAMBDA_INIT, ALU.add,
       tag="lam")
    ts(k, g1l.rv(), gam.rv(), 1.0 - LAMBDA_INIT, ALU.mult, tag="gam")

    q_sb = A.alloc(128, NCH * TOK, BF16)
    A.mark()
    k.ln_x = A.alloc(128, NCH * TOK, BF16)
    emit_ln(k)
    ln_own = k.ln_x
    r1q = {}
    for n in ("r1qc", "r1qs"):
        r1q[n] = A.alloc(128, BLK, BF16)
        dma(k, "qsp", r1q[n].rv(), W[n])
    wst = [A.alloc(128, NCH * 128, BF16) for _ in range(3)]
    for oc in range(NCH):
        dma(k, "qpool", wst[oc % 3].rv(), W["l1wq"][oc], tag="l1wqdma")
        for s in range(2):
            u = oc * 2 + s
            pq = k.pb[u % 3]
            for dc in range(NCH):
                mm(k, pq.rv(),
                   wst[oc % 3].rv(0, 128, dc * 128, (dc + 1) * 128),
                   ln_own.rv(0, 128, dc * TOK + s * BLK,
                             dc * TOK + s * BLK + BLK),
                   start=(dc == 0), stop=(dc == NCH - 1), tag="l1q")
            _rope(k, q_sb.rv(0, 128, oc * TOK + s * BLK,
                             oc * TOK + s * BLK + BLK),
                  pq, rot1, r1q["r1qc"], r1q["r1qs"], par=u)
    A.release()

    A.mark()
    attn_fm = A.alloc(128, NCH * TOK, BF16)
    A.mark()
    kmx = A.alloc(1, 8, F32)
    dma(k, "qsp", kmx.rv(), W["kmx1"])
    emit_qbound(k, q_sb, kmx)
    ksb = A.alloc(128, 4096, BF16)
    vsb = A.alloc(128, 4096, BF16)
    pt = [A.alloc(128, BLK, BF16) for _ in range(3)]
    a1 = A.alloc(128, BLK, F32)
    a2 = A.alloc(128, BLK, F32)
    o1sb = A.alloc(128, BLK, BF16)
    o2sb = A.alloc(128, BLK, BF16)
    sbanks = [k.pb[0], k.pb[1], k.pb[2]]
    for s in range(2):
        for h in range(16):
            for rr in range(NCORES):
                r0_ = rr * 8192 + s * 4096 + h * 128
                dma(k, "qsp", ksb.rv(0, 128, rr * 512, (rr + 1) * 512),
                    (k.kv1_g[r0_:r0_ + 128, :],
                     drect("kv1_g", r0_, r0_ + 128)), tag="l1kload")
                g = h // 4
                co = (h % 4) * 128
                r1_ = rr * 8192 + s * 4096 + 2048 + g * 512
                dma(k, "qsp", vsb.rv(0, 128, rr * 512, (rr + 1) * 512),
                    (k.kv1_g[r1_:r1_ + 512, co:co + 128]
                     .rearrange("(a p) v -> p a v", p=128),
                     drect("kv1_g", r1_, r1_ + 512)), tag="l1vload")
            for comp in range(2):
                pof = comp * 64
                qrv = q_sb.rv(pof, pof + 64, h * TOK + s * BLK,
                              h * TOK + s * BLK + BLK)
                p_o = k.pb[3] if comp == 0 else k.pb[5]
                p_l = k.pb[4] if comp == 0 else k.pb[6]
                _strip(k, qrv, ksb, vsb, pof, 64, None, pt, None, p_o, p_l,
                       sbanks)
            rl1, rl2, sd = k.rows[0], k.rows[1], k.rows[2]
            act(k, o1sb.rv(), k.pb[3].rv(), AF.Copy, tag="l1oev")
            act(k, o2sb.rv(), k.pb[5].rv(), AF.Copy, tag="l1oev")
            ts(k, rl1.rv(0, 1, 0, BLK), k.pb[4].rv(0, 1, 0, BLK), L_GUARD,
               ALU.add, tag="lg1")
            recip(k, rl1.rv(0, 1, 0, BLK), rl1.rv(0, 1, 0, BLK), tag="l1rl")
            ts(k, rl2.rv(0, 1, 0, BLK), k.pb[6].rv(0, 1, 0, BLK), L_GUARD,
               ALU.add, tag="lg2")
            recip(k, rl2.rv(0, 1, 0, BLK), rl2.rv(0, 1, 0, BLK), tag="l1rl")
            ts(k, rl2.rv(0, 1, 0, BLK), rl2.rv(0, 1, 0, BLK),
               (lam.ap(0, 1, 0, 1), lam.rect(0, 1, 0, 1)), ALU.mult,
               tag="l1rl2")
            copy(k, k.rowsb[0].rv(), rl1.rv(0, 1, 0, BLK), tag="l1rb1")
            copy(k, k.rowsb[1].rv(), rl2.rv(0, 1, 0, BLK), tag="l1rb2")
            bcast_row(k, k.pb[7], k.rowsb[0].rv(), BLK, k.bc1, bf=True)
            bcast_row(k, k.pb[7], k.rowsb[1].rv(), BLK, k.bc2, bf=True)
            tt(k, a1.rv(), o1sb.rv(), k.bc1.rv(0, 128, 0, BLK), ALU.mult,
               tag="l1o1")
            tt(k, a2.rv(), o2sb.rv(), k.bc2.rv(0, 128, 0, BLK), ALU.mult,
               tag="l1o2")
            tt(k, a1.rv(), a1.rv(), a2.rv(), ALU.subtract, tag="l1sub")
            sqa = k.scrb[0]
            act(k, sqa.rv(), a1.rv(), AF.Square, tag="l1rms")
            mm(k, k.pb[7].rv(0, 1, 0, BLK), k.ones_col_bf.rv(), sqa.rv(),
               tag="l1rms")
            act(k, sd.rv(0, 1, 0, BLK), k.pb[7].rv(0, 1, 0, BLK), AF.Sqrt,
                scale=1.0 / 128.0,
                bias=(k.eps_col.ap(0, 1, 0, 1),
                      k.eps_col.rect(0, 1, 0, 1)), tag="l1rms")
            recip(k, sd.rv(0, 1, 0, BLK), sd.rv(0, 1, 0, BLK), tag="l1rms")
            copy(k, k.rowsb[0].rv(), sd.rv(0, 1, 0, BLK), tag="l1rb3")
            bcast_row(k, k.pb[7], k.rowsb[0].rv(), BLK, k.bc1, bf=True)
            tt(k, a1.rv(), a1.rv(), k.bc1.rv(0, 128, 0, BLK), ALU.mult,
               tag="l1rms2")
            ts(k, attn_fm.rv(0, 128, h * TOK + s * BLK,
                             h * TOK + s * BLK + BLK),
               a1.rv(), (g1l.ap(), g1l.rect()), ALU.mult, tag="l1gam")
    A.release()
    wst2 = [A.alloc(128, NCH * 128, BF16) for _ in range(3)]
    for oc in range(NCH):
        dma(k, "qpool", wst2[oc % 3].rv(), W["l1wo"][oc],
            tag="l1wodma")
        for b in range(2):
            v = oc * 2 + b
            po = k.pb[v % 4]
            for ac_ in range(NCH):
                mm(k, po.rv(),
                   wst2[oc % 3].rv(0, 128, ac_ * 128, (ac_ + 1) * 128),
                   attn_fm.rv(0, 128, ac_ * TOK + b * BLK,
                              ac_ * TOK + b * BLK + BLK),
                   start=(ac_ == 0), stop=(ac_ == NCH - 1), tag="l1wo")
            xc = k.x_res.rv(0, 128, oc * TOK + b * BLK,
                            oc * TOK + b * BLK + BLK)
            tt(k, xc, po.rv(), xc, ALU.add, tag="l1wo_add")
    A.release()
    A.release()



SUBLAYERS = ["moe", "l0attn", "moe", "l1ffn", "l1attn", "l1ffn"]
ARENA_BYTES = 212672


def build_program(schedule, in_dtype):
    nc = bass.Bass("TRN2", target_bir_lowering=False, debug=False)
    need = set(schedule)

    x_in = nc.dram_tensor("x", [D, TOK], in_dtype, kind="ExternalInput")
    y_out = nc.dram_tensor("y", [D, TOK], F32, kind="ExternalOutput")
    k_dbg = (nc.dram_tensor("dbg", [128, 32768], F32, kind="ExternalOutput")
             if DBG is not None else None)

    k = Ctx()
    k.P = Prog()
    k.dbg = k_dbg
    k.dbg_used = [0]
    din = {}

    def inp(name, shape, dt):
        din[name] = nc.dram_tensor(name, shape, dt, kind="ExternalInput")
        return din[name]

    inp("ident", [128, 128], F32)
    if "kv0" in need:
        inp("rot0", [128, 128], BF16)
        for n in ("r0kc", "r0ks"):
            inp(n, [128, BLK], BF16)
        inp("wk", [D, 512], BF16)
        inp("wv", [D, 512], BF16)
        inp("bk", [1, 512], BF16)
        inp("bv", [1, 512], BF16)
        k.kv0_out = nc.dram_tensor("kv0", [2048, 512], BF16,
                                   kind="ExternalOutput")
        k.kmax0_out = nc.dram_tensor("kmx0o", [1, 1], F32,
                                     kind="ExternalOutput")
    if "kv1" in need:
        if "rot1" not in din:
            inp("rot1", [128, 128], BF16)
        for n in ("r1kc", "r1ks"):
            inp(n, [128, BLK], BF16)
        inp("l1wk", [D, D], F32)
        inp("l1wv", [D, D], F32)
        k.kv1_out = nc.dram_tensor("kv1", [8192, 512], BF16,
                                   kind="ExternalOutput")
        k.kmax1_out = nc.dram_tensor("kmx1o", [1, 1], F32,
                                     kind="ExternalOutput")
    if "l0attn" in need:
        inp("qpos", [1, BLK], F32)
        inp("kpos", [128, 32], F32)
        if "rot0" not in din:
            inp("rot0", [128, 128], BF16)
        for n in ("r0qc", "r0qs"):
            inp(n, [128, BLK], BF16)
        inp("wq", [D, D], BF16)
        inp("wo", [D, D], BF16)
        inp("bq", [1, D], BF16)
        inp("kmx0", [1, 8], F32)
        k.kv0_g = nc.dram_tensor("kv0_g", [16384, 512], BF16,
                                 kind="ExternalInput")
    if "moe" in need:
        inp("gw", [D, 8], F32)
        inp("sel8", [8, 1024], BF16)
        inp("eb", [1, 8], F32)
        inp("w1", [8, D, 1024], F32)
        inp("w3", [8, D, 1024], F32)
        inp("w2", [8, 1024, D], F32)
        inp("sw1", [D, 1024], F32)
        inp("sw3", [D, 1024], F32)
        inp("sw2", [1024, D], F32)
    if "l1ffn" in need:
        inp("wgt", [D, 8192], F32)
        inp("wut", [D, 8192], F32)
        inp("wdt", [8192, D], F32)
    if "l1attn" in need:
        if "rot1" not in din:
            inp("rot1", [128, 128], BF16)
        for n in ("r1qc", "r1qs"):
            inp(n, [128, BLK], BF16)
        inp("l1wq", [D, D], F32)
        inp("l1wo", [D, D], F32)
        inp("kmx1", [1, 8], F32)
        for n in ("lq1", "lk1", "lq2", "lk2"):
            inp(n, [1, 64], F32)
        inp("gamc", [128, 1], F32)
        k.kv1_g = nc.dram_tensor("kv1_g", [65536, 512], BF16,
                                 kind="ExternalInput")

    with (
        nc.sbuf_tensor("arena", [128, ARENA_BYTES // 4], F32) as arena_h,
        nc.psum_tensor("pb0", [128, 512], F32) as pb0,
        nc.psum_tensor("pb1", [128, 512], F32) as pb1,
        nc.psum_tensor("pb2", [128, 512], F32) as pb2,
        nc.psum_tensor("pb3", [128, 512], F32) as pb3,
        nc.psum_tensor("pb4", [128, 512], F32) as pb4,
        nc.psum_tensor("pb5", [128, 512], F32) as pb5,
        nc.psum_tensor("pb6", [128, 512], F32) as pb6,
        nc.psum_tensor("pb7", [128, 512], F32) as pb7,
        nc.semaphore("s_pe") as s_pe,
        nc.semaphore("s_dve") as s_dve,
        nc.semaphore("s_act") as s_act,
        nc.semaphore("s_pool") as s_pool,
        nc.semaphore("s_sp") as s_sp,
        nc.semaphore("q_sp") as q_sp,
        nc.semaphore("q_act") as q_act,
        nc.semaphore("q_pool") as q_pool,
        nc.semaphore("q_cc") as q_cc,
        nc.Block() as block,
    ):
        sems = {"pe": s_pe, "dve": s_dve, "act": s_act, "pool": s_pool,
                "sp": s_sp, "qsp": q_sp, "qact": q_act, "qpool": q_pool,
                "qcc": q_cc}
        A = Arena(arena_h, "S:arena", ARENA_BYTES)
        k.arena = A
        k.pb = [Tile(h, f"P:pb{i}", True, 0, 128, 0, 512, F32)
                for i, h in enumerate([pb0, pb1, pb2, pb3, pb4, pb5, pb6,
                                       pb7])]
        k.x_res = A.alloc(128, NCH * TOK, F32)
        k.scr = [A.alloc(128, BLK, F32) for _ in range(3)]
        k.scrb = [A.alloc(128, BLK, BF16) for _ in range(4)]
        k.rows = [A.alloc(1, BLK, F32) for _ in range(5)]
        k.rowsb = [A.alloc(1, BLK, BF16) for _ in range(2)]
        k.bc1 = A.alloc(128, BLK, F32)
        k.bc2 = A.alloc(128, BLK, F32)
        k.ropes = [(A.alloc(128, BLK, BF16), A.alloc(128, BLK, BF16),
                    A.alloc(128, BLK, BF16)) for _ in range(2)]
        k.nb_col = A.alloc(128, 1, F32)
        k.eps_col = A.alloc(128, 1, F32)
        k.ones_col_f32 = A.alloc(128, 1, F32)
        k.ones_col_bf = A.alloc(128, 1, BF16)
        k.ones_row_f32 = A.alloc(1, 128, F32)
        k.ones_row_bf = A.alloc(1, 128, BF16)
        k.ones512_bf = A.alloc(1, BLK, BF16)
        k.ident = A.alloc(128, 128, F32)

        memset(k, k.eps_col.rv(), EPS)
        memset(k, k.ones_col_f32.rv(), 1.0)
        memset(k, k.ones_col_bf.rv(), 1.0)
        memset(k, k.ones_row_f32.rv(), 1.0)
        memset(k, k.ones_row_bf.rv(), 1.0)
        memset(k, k.ones512_bf.rv(), 1.0)
        dma(k, "qsp", k.ident.rv(), (din["ident"].ap(), drect("ident")))

        x_src = (x_in.ap().rearrange("(c p) t -> p c t", p=128),
                 drect("x"))
        if in_dtype == F32:
            dma(k, "qsp", k.x_res.rv(), x_src, tag="xload")
        else:
            dma(k, "qpool", k.x_res.rv(), x_src, tag="xload")

        W = {}
        for n in din:
            pass
        if "moe" in need:
            W["gw"] = (din["gw"].ap().rearrange("(c p) e -> p c e", p=128),
                       drect("gw"))
            W["eb"] = (din["eb"].ap(), drect("eb"))
            W["sel8"] = (din["sel8"].ap(), drect("sel8"))
            W["w1"] = [[(din["w1"][e][:, ht * 128:(ht + 1) * 128]
                         .rearrange("(c p) h -> p c h", p=128), drect("w1"))
                        for ht in range(8)] for e in range(8)]
            W["w3"] = [[(din["w3"][e][:, ht * 128:(ht + 1) * 128]
                         .rearrange("(c p) h -> p c h", p=128), drect("w3"))
                        for ht in range(8)] for e in range(8)]
            W["w2"] = [[(din["w2"][e][:, ot * 128:(ot + 1) * 128]
                         .rearrange("(c p) o -> p c o", p=128), drect("w2"))
                        for ot in range(NCH)] for e in range(8)]
            W["sw1"] = [(din["sw1"][:, ht * 128:(ht + 1) * 128]
                         .rearrange("(c p) h -> p c h", p=128), drect("sw1"))
                        for ht in range(8)]
            W["sw3"] = [(din["sw3"][:, ht * 128:(ht + 1) * 128]
                         .rearrange("(c p) h -> p c h", p=128), drect("sw3"))
                        for ht in range(8)]
            W["sw2"] = [(din["sw2"][:, ot * 128:(ot + 1) * 128]
                         .rearrange("(c p) o -> p c o", p=128), drect("sw2"))
                        for ot in range(NCH)]
        if "kv0" in need:
            for n in ("rot0", "r0kc", "r0ks", "bk", "bv"):
                W[n] = (din[n].ap(), drect(n))
            W["wk"] = (din["wk"].ap().rearrange("(c p) o -> p c o", p=128),
                       drect("wk"))
            W["wv"] = (din["wv"].ap().rearrange("(c p) o -> p c o", p=128),
                       drect("wv"))
        if "kv1" in need:
            for n in ("rot1", "r1kc", "r1ks"):
                W[n] = (din[n].ap(), drect(n))
            W["l1wk_h"] = [(din["l1wk"][h * 1024:(h + 1) * 1024, :]
                            .rearrange("(c p) o -> p c o", p=128),
                            drect("l1wk")) for h in range(2)]
            W["l1wv_h"] = [(din["l1wv"][h * 1024:(h + 1) * 1024, :]
                            .rearrange("(c p) o -> p c o", p=128),
                            drect("l1wv")) for h in range(2)]
        if "l0attn" in need:
            for n in ("qpos", "kpos", "rot0", "r0qc", "r0qs", "bq",
                      "kmx0"):
                W[n] = (din[n].ap(), drect(n))
            W["wq"] = [(din["wq"][:, oc * 128:(oc + 1) * 128]
                        .rearrange("(c p) o -> p c o", p=128), drect("wq"))
                       for oc in range(NCH)]
            W["wo"] = [(din["wo"][:, oc * 128:(oc + 1) * 128]
                        .rearrange("(c p) o -> p c o", p=128), drect("wo"))
                       for oc in range(NCH)]
        if "l1ffn" in need:
            W["wg"] = [(din["wgt"][:, sl * 128:(sl + 1) * 128]
                        .rearrange("(c p) h -> p c h", p=128),
                        drect("wgt")) for sl in range(64)]
            W["wu"] = [(din["wut"][:, sl * 128:(sl + 1) * 128]
                        .rearrange("(c p) h -> p c h", p=128),
                        drect("wut")) for sl in range(64)]
            W["wd"] = [[(din["wdt"][qtr * 2048:(qtr + 1) * 2048,
                                    ot * 128:(ot + 1) * 128]
                         .rearrange("(c p) o -> p c o", p=128),
                         drect("wdt")) for ot in range(NCH)]
                       for qtr in range(4)]
        if "l1attn" in need:
            for n in ("rot1", "r1qc", "r1qs", "lq1",
                      "lk1", "lq2", "lk2", "gamc", "kmx1"):
                W[n] = (din[n].ap(), drect(n))
            W["l1wq"] = [(din["l1wq"][:, oc * 128:(oc + 1) * 128]
                          .rearrange("(c p) o -> p c o", p=128),
                          drect("l1wq")) for oc in range(NCH)]
            W["l1wo"] = [(din["l1wo"][:, oc * 128:(oc + 1) * 128]
                          .rearrange("(c p) o -> p c o", p=128),
                          drect("l1wo")) for oc in range(NCH)]

        for name in schedule:
            if name == "moe":
                emit_moe(k, W)
            elif name == "kv0":
                emit_kv0(k, W)
            elif name == "kv1":
                emit_kv1(k, W)
            elif name == "l0attn":
                emit_l0attn(k, W)
            elif name == "l1ffn":
                emit_l1ffn2(k, W)
            elif name == "l1attn":
                emit_l1attn(k, W)

        dma(k, "qsp", (y_out.ap().rearrange("(c p) t -> p c t", p=128),
                       drect("y")), k.x_res.rv(), tag="ystore")

        final = k.P.finalize()
        nq = dict(k.P.qcount)

        def emit_engine(eng_obj, ename, tail=False):
            for op, ws in final[ename]:
                for sem, val in ws:
                    eng_obj.wait_ge(sems[sem], val)
                instr = op.fn(eng_obj)
                if op.queue is not None:
                    instr.then_inc(sems[op.queue], 16)
                elif op.need_inc:
                    instr.then_inc(sems[op.engine], 1)
            if tail:
                eng_obj.wait_ge(sems["qsp"], 16 * nq["qsp"])

        @block.sync
        def _(eng):
            emit_engine(eng, "sp", tail=True)

        @block.tensor
        def _(eng):
            emit_engine(eng, "pe")

        @block.vector
        def _(eng):
            emit_engine(eng, "dve")

        @block.scalar
        def _(eng):
            emit_engine(eng, "act")

        @block.gpsimd
        def _(eng):
            emit_engine(eng, "pool")

    return nc



def _rot_matrices():
    R0 = np.zeros((128, 128), dtype=np.float32)
    for d in range(64):
        R0[d, d + 64] = -1.0
        R0[d + 64, d] = 1.0
    R1 = np.zeros((128, 128), dtype=np.float32)
    for base in (0, 64):
        for d in range(32):
            R1[base + d, base + 32 + d] = -1.0
            R1[base + 32 + d, base + d] = 1.0
    return (np.ascontiguousarray(R0.T).astype(BF),
            np.ascontiguousarray(R1.T).astype(BF))


def _l0_cossin(pos):
    inv = 1.0 / (1e6 ** (np.arange(64, dtype=np.float64) / 64.0))
    ang = np.outer(pos, inv)
    ct = np.cos(ang).T
    st = np.sin(ang).T
    return np.vstack([ct, ct]), np.vstack([st, st])


def _rope_tables_l0(core):
    cos, sin = _l0_cossin(core * BLK + np.arange(BLK, dtype=np.float64))
    s = 1.0 / math.sqrt(128.0)
    return {"r0qc": (cos * s).astype(BF), "r0qs": (sin * s).astype(BF)}


def _ktables_l0(core):
    cos, sin = _l0_cossin(core * BLK + np.arange(BLK, dtype=np.float64))
    return {"r0kc": cos.astype(BF), "r0ks": sin.astype(BF)}


def _ktables_l1(core):
    cos, sin = _l1_cossin(core * BLK + np.arange(BLK, dtype=np.float64))
    return {"r1kc": cos.astype(BF), "r1ks": sin.astype(BF)}


def _l1_cossin(pos):
    angle = 1.0 / (10000.0 ** np.linspace(0.0, 1.0, 32))
    fr = pos[:, None] * angle[None, :]
    return np.tile(np.cos(fr).T, (4, 1)), np.tile(np.sin(fr).T, (4, 1))


def _rope_tables_l1(core):
    cos, sin = _l1_cossin(core * BLK + np.arange(BLK, dtype=np.float64))
    s = 1.0 / 8.0
    return {"r1qc": (cos * s).astype(BF), "r1qs": (sin * s).astype(BF)}


_PERM64 = np.concatenate([np.arange(0, 64, 2), np.arange(1, 64, 2)])


def _perm_cols(wt):
    out = wt.reshape(wt.shape[0], 32, 64)[:, :, _PERM64]
    return np.ascontiguousarray(out.reshape(wt.shape[0], 2048))


def _prep_weights(inputs):
    f32 = np.float32
    Wall = {}
    Wall["gw"] = np.ascontiguousarray(
        np.asarray(inputs["L0_ffn_gate_w"], f32).T)
    Wall["eb"] = np.asarray(inputs["L0_ffn_expert_bias"], f32)[None, :]
    Wall["w1"] = np.ascontiguousarray(
        np.asarray(inputs["L0_ffn_w1"], f32).transpose(0, 2, 1))
    Wall["w3"] = np.ascontiguousarray(
        np.asarray(inputs["L0_ffn_w3"], f32).transpose(0, 2, 1))
    Wall["w2"] = np.ascontiguousarray(
        np.asarray(inputs["L0_ffn_w2"], f32).transpose(0, 2, 1))
    Wall["sw1"] = np.ascontiguousarray(np.asarray(inputs["L0_ffn_sw1"],
                                                  f32).T)
    Wall["sw3"] = np.ascontiguousarray(np.asarray(inputs["L0_ffn_sw3"],
                                                  f32).T)
    Wall["sw2"] = np.ascontiguousarray(np.asarray(inputs["L0_ffn_sw2"],
                                                  f32).T)
    Wall["wq"] = np.ascontiguousarray(np.asarray(inputs["L0_attn_Wq"], BF))
    Wall["wk"] = np.ascontiguousarray(np.asarray(inputs["L0_attn_Wk"], BF))
    Wall["wv"] = np.ascontiguousarray(np.asarray(inputs["L0_attn_Wv"], BF))
    Wall["wo"] = np.ascontiguousarray(np.asarray(inputs["L0_attn_Wo"], BF))
    Wall["bq"] = np.asarray(inputs["L0_attn_bq"], BF)[None, :]
    Wall["bk"] = np.asarray(inputs["L0_attn_bk"], BF)[None, :]
    Wall["bv"] = np.asarray(inputs["L0_attn_bv"], BF)[None, :]
    Wall["wgt"] = np.ascontiguousarray(np.asarray(inputs["L1_ffn_w_gate"],
                                                  f32).T)
    Wall["wut"] = np.ascontiguousarray(np.asarray(inputs["L1_ffn_w_up"],
                                                  f32).T)
    Wall["wdt"] = np.ascontiguousarray(np.asarray(inputs["L1_ffn_w_down"],
                                                  f32).T)
    Wall["l1wq"] = _perm_cols(np.asarray(inputs["L1_attn_Wq"], f32).T)
    Wall["l1wk"] = _perm_cols(np.asarray(inputs["L1_attn_Wk"], f32).T)
    Wall["l1wv"] = np.ascontiguousarray(np.asarray(inputs["L1_attn_Wv"],
                                                   f32).T)
    Wall["l1wo"] = np.ascontiguousarray(np.asarray(inputs["L1_attn_Wo"],
                                                   f32).T)
    Wall["lq1"] = np.asarray(inputs["L1_attn_lambda_q1"], f32)[None, :]
    Wall["lk1"] = np.asarray(inputs["L1_attn_lambda_k1"], f32)[None, :]
    Wall["lq2"] = np.asarray(inputs["L1_attn_lambda_q2"], f32)[None, :]
    Wall["lk2"] = np.asarray(inputs["L1_attn_lambda_k2"], f32)[None, :]
    Wall["gamc"] = np.ascontiguousarray(
        np.asarray(inputs["L1_attn_gamma"], f32)[:, None])
    rot0, rot1 = _rot_matrices()
    Wall["rot0"] = rot0
    Wall["rot1"] = rot1
    Wall["ident"] = np.eye(128, dtype=np.float32)
    Wall["kpos"] = (np.arange(32, dtype=np.float32)[None, :] * 128
                    + np.arange(128, dtype=np.float32)[:, None])
    sel8 = np.zeros((8, 1024), dtype=np.float32)
    for e in range(8):
        sel8[e, e * 128:(e + 1) * 128] = 1.0
    Wall["sel8"] = sel8.astype(BF)
    return Wall


_WNAMES = {
    "moe": ("gw", "eb", "w1", "w3", "w2", "sw1", "sw3", "sw2", "sel8"),
    "kv0": ("wk", "wv", "bk", "bv", "rot0"),
    "kv1": ("l1wk", "l1wv", "rot1"),
    "l0attn": ("wq", "wo", "bq", "rot0", "kpos"),
    "l1ffn": ("wgt", "wut", "wdt"),
    "l1attn": ("l1wq", "l1wo", "lq1", "lk1", "lq2", "lk2",
               "gamc", "rot1"),
}


_PROG_CACHE = {}


def _get_prog(group, dt_tag):
    key = (group, dt_tag)
    if key not in _PROG_CACHE:
        _PROG_CACHE[key] = build_program(list(group),
                                         BF16 if dt_tag == "bf16" else F32)
    return _PROG_CACHE[key]


def kernel(**inputs):
    x = np.asarray(inputs["L0_attn_x"], BF)
    Wall = _prep_weights(inputs)

    x_fm = []
    for c in range(NCORES):
        xc = np.concatenate([x[0, c * BLK:(c + 1) * BLK, :],
                             x[1, c * BLK:(c + 1) * BLK, :]], axis=0)
        x_fm.append(np.ascontiguousarray(xc.T))

    def in_map_for(core, schedule, x_arr, extra=None):
        m = {"x": x_arr, "ident": Wall["ident"]}
        need = set(schedule)
        if "l0attn" in need:
            m["qpos"] = (core * BLK
                         + np.arange(BLK, dtype=np.float32))[None, :]
            m.update(_rope_tables_l0(core))
        if "l1attn" in need:
            m.update(_rope_tables_l1(core))
        if "kv0" in need:
            m.update(_ktables_l0(core))
        if "kv1" in need:
            m.update(_ktables_l1(core))
        if extra:
            m.update(extra)
        for s in need:
            for n in _WNAMES[s]:
                m[n] = Wall[n]
        return m

    GROUPS = [["moe", "kv0"], ["l0attn", "moe", "l1ffn", "kv1"],
              ["l1attn", "l1ffn"]]
    cur = x_fm
    extra = None
    for gi, group in enumerate(GROUPS):
        nc = _get_prog(tuple(group), "bf16" if gi == 0 else "f32")
        in_maps = [in_map_for(c, group, cur[c], extra)
                   for c in range(NCORES)]
        res = run_bass_kernel_spmd(nc, in_maps, core_ids=list(range(NCORES)))
        cur = [np.ascontiguousarray(r["y"]) for r in res.results]
        if "kv0" in group:
            extra = {"kv0_g": np.concatenate(
                [np.asarray(r["kv0"]) for r in res.results], axis=0),
                "kmx0": np.concatenate(
                [np.asarray(r["kmx0o"]) for r in res.results], axis=1)}
        elif "kv1" in group:
            extra = {"kv1_g": np.concatenate(
                [np.asarray(r["kv1"]) for r in res.results], axis=0),
                "kmx1": np.concatenate(
                [np.asarray(r["kmx1o"]) for r in res.results], axis=1)}
        else:
            extra = None
    outs = cur

    out = np.empty((2, 4096, 2048), dtype=np.float32)
    for c in range(NCORES):
        yt = outs[c].T
        out[0, c * BLK:(c + 1) * BLK, :] = yt[:BLK]
        out[1, c * BLK:(c + 1) * BLK, :] = yt[BLK:]
    return out


# version 06
# speedup vs baseline: 1.9081x; 1.1438x over previous
import math
import numpy as np
import ml_dtypes

import concourse.bass as bass
import concourse.mybir as mybir
from concourse.bass_utils import run_bass_kernel_spmd

F32 = mybir.dt.float32
BF16 = mybir.dt.bfloat16
AF = mybir.ActivationFunctionType
ALU = mybir.AluOpType
AXL = mybir.AxisListType

BF = np.dtype(ml_dtypes.bfloat16)

D = 2048
NCH = 16
TOK = 1024
BLK = 512
NCORES = 8
EPS = 1e-5
LAMBDA_INIT = 0.8 - 0.6 * math.exp(-0.3 * 12)
L_GUARD = 1e-30

FUSED = False
DBG = None


ENGINES = ("pe", "dve", "act", "pool", "sp")
QUEUES = ("qsp", "qact", "qpool", "qcc")
QUEUE_ENGINE = {"qsp": "sp", "qact": "act", "qpool": "pool", "qcc": "pool"}


class Rect:
    __slots__ = ("key", "p0", "p1", "b0", "b1")

    def __init__(self, key, p0, p1, b0, b1):
        self.key, self.p0, self.p1, self.b0, self.b1 = key, p0, p1, b0, b1

    def overlaps(self, o):
        return (self.key == o.key and self.p0 < o.p1 and o.p0 < self.p1
                and self.b0 < o.b1 and o.b0 < self.b1)


class Op:
    __slots__ = ("engine", "fn", "reads", "writes", "queue", "eidx", "qidx",
                 "need_inc", "inc_count", "waits", "tag", "deps")

    def __init__(self, engine, fn, reads, writes, queue, tag=""):
        self.engine, self.fn = engine, fn
        self.reads, self.writes = reads, writes
        self.queue, self.tag = queue, tag
        self.eidx = -1
        self.qidx = -1
        self.need_inc = False
        self.inc_count = -1
        self.waits = {}
        self.deps = []


class Tile:
    __slots__ = ("handle", "key", "is_psum", "p0", "parts", "b0", "free",
                 "dt", "esz")

    def __init__(self, handle, key, is_psum, p0, parts, b0, free, dt):
        self.handle, self.key, self.is_psum = handle, key, is_psum
        self.p0, self.parts, self.b0, self.free, self.dt = p0, parts, b0, free, dt
        self.esz = 4 if dt == F32 else 2

    def sub(self, f0, f1):
        return Tile(self.handle, self.key, self.is_psum, self.p0, self.parts,
                    self.b0 + f0 * self.esz, f1 - f0, self.dt)

    def ap(self, p0=0, p1=None, f0=0, f1=None):
        if p1 is None:
            p1 = self.parts
        if f1 is None:
            f1 = self.free
        if self.is_psum:
            return self.handle[self.p0 + p0:self.p0 + p1, f0:f1]
        e0 = self.b0 // self.esz
        base = self.handle.ap()
        if self.dt != F32:
            base = base.bitcast(self.dt)
        return base[self.p0 + p0:self.p0 + p1, e0 + f0:e0 + f1]

    def rect(self, p0=0, p1=None, f0=0, f1=None):
        if p1 is None:
            p1 = self.parts
        if f1 is None:
            f1 = self.free
        return Rect(self.key, self.p0 + p0, self.p0 + p1,
                    self.b0 + f0 * self.esz, self.b0 + f1 * self.esz)

    def rv(self, p0=0, p1=None, f0=0, f1=None):
        return (self.ap(p0, p1, f0, f1), self.rect(p0, p1, f0, f1))


class Arena:
    def __init__(self, handle, key, nbytes):
        self.handle, self.key, self.nbytes = handle, key, nbytes
        self.off = 0
        self.marks = []
        self.peak = 0

    def alloc(self, parts, free, dt, align=64):
        esz = 4 if dt == F32 else 2
        sz = free * esz
        self.off = (self.off + align - 1) // align * align
        assert self.off + sz <= self.nbytes, \
            f"arena overflow: {self.off}+{sz} > {self.nbytes}"
        t = Tile(self.handle, self.key, False, 0, parts, self.off, free, dt)
        self.off += sz
        self.peak = max(self.peak, self.off)
        return t

    def mark(self):
        self.marks.append(self.off)

    def release(self):
        self.off = self.marks.pop()


def drect(name, r0=0, r1=1 << 30):
    return Rect("D:" + name, 0, 1, r0, r1)


class Prog:
    def __init__(self):
        self.ops = {e: [] for e in ENGINES}
        self.qcount = {q: 0 for q in QUEUES}
        self.order = []
        self.regions = {}
        self.last_q = {}

    def _dep_scan(self, op):
        deps = []
        for r in op.reads:
            for ent in self.regions.get(r.key, ()):
                if ent[1] is not None and ent[0].overlaps(r):
                    deps.append(ent[1])
        for w in op.writes:
            for ent in self.regions.get(w.key, ()):
                if ent[0].overlaps(w):
                    if ent[1] is not None:
                        deps.append(ent[1])
                    deps.extend(ent[2])
        return deps

    def _record(self, op):
        for w in op.writes:
            lst = self.regions.setdefault(w.key, [])
            keep = [ent for ent in lst
                    if not (ent[0].p0 >= w.p0 and ent[0].p1 <= w.p1
                            and ent[0].b0 >= w.b0 and ent[0].b1 <= w.b1)]
            keep.append((w, op, []))
            self.regions[w.key] = keep
        for r in op.reads:
            lst = self.regions.setdefault(r.key, [])
            for ent in lst:
                if ent[0].overlaps(r):
                    ent[2].append(op)

    def add(self, engine, fn, reads=(), writes=(), queue=None, tag=""):
        op = Op(engine, fn, list(reads), list(writes), queue, tag)
        op.deps = self._dep_scan(op)
        self._record(op)
        op.eidx = len(self.ops[engine])
        self.ops[engine].append(op)
        if queue is not None:
            op.qidx = self.qcount[queue]
            self.qcount[queue] += 1
            prev = self.last_q.get(queue)
            if prev is not None:
                op.deps.append(prev)
            self.last_q[queue] = op
        self.order.append(op)
        return op

    def finalize(self):
        for op in self.order:
            for d in op.deps:
                if d is op or d.queue is not None:
                    continue
                if d.engine == "pe" and op.engine == "pe":
                    continue
                d.need_inc = True
        counts = {e: 0 for e in ENGINES}
        for e in ENGINES:
            for op in self.ops[e]:
                if op.queue is None and op.need_inc:
                    counts[e] += 1
                    op.inc_count = counts[e]
        for op in self.order:
            waits = {}
            for d in op.deps:
                if d is op:
                    continue
                if d.queue is not None:
                    sem, val = d.queue, 16 * (d.qidx + 1)
                elif d.engine == "pe" and op.engine == "pe":
                    continue
                else:
                    if d.inc_count < 0:
                        continue
                    sem, val = d.engine, d.inc_count
                if waits.get(sem, 0) < val:
                    waits[sem] = val
            op.waits = waits
        final = {}
        for e in ENGINES:
            seen = {}
            lst = []
            for op in self.ops[e]:
                ws = []
                for sem, val in sorted(op.waits.items()):
                    if seen.get(sem, 0) >= val:
                        continue
                    seen[sem] = val
                    ws.append((sem, val))
                lst.append((op, ws))
            final[e] = lst
        return final



class Ctx:
    pass


def mm(k, out_rv, lhs_rv, rhs_rv, start=True, stop=True, tag=""):
    (oap, orect), (lap, lrect), (rap, rrect) = out_rv, lhs_rv, rhs_rv
    k.P.add("pe", lambda e: e.matmul(oap, lap, rap, start=start, stop=stop),
            reads=[lrect, rrect], writes=[orect], tag=tag)


def transpose(k, out_rv, in_rv, id_rv, tag=""):
    (oap, orect), (iap, irect), (dap, dr) = out_rv, in_rv, id_rv
    k.P.add("pe", lambda e: e.transpose(oap, iap, dap),
            reads=[irect, dr], writes=[orect], tag=tag)


def act(k, out_rv, in_rv, func, bias=None, scale=None, tag=""):
    (oap, orect), (iap, irect) = out_rv, in_rv
    reads = [irect]
    kw = {}
    if bias is not None:
        if isinstance(bias, tuple):
            kw["bias"] = bias[0]
            reads.append(bias[1])
        else:
            kw["bias"] = bias
    if scale is not None:
        kw["scale"] = scale
    k.P.add("act", lambda e: e.activation(oap, iap, func, **kw),
            reads=reads, writes=[orect], tag=tag)


def tt(k, out_rv, a_rv, b_rv, op, tag=""):
    (oap, orect), (aap, ar), (bap, br) = out_rv, a_rv, b_rv
    k.P.add("dve", lambda e: e.tensor_tensor(oap, aap, bap, op),
            reads=[ar, br], writes=[orect], tag=tag)


def ts(k, out_rv, in_rv, s1, op0, s2=None, op1=None, tag=""):
    (oap, orect), (iap, irect) = out_rv, in_rv
    reads = [irect]
    if isinstance(s1, tuple):
        s1v = s1[0]
        reads.append(s1[1])
    else:
        s1v = s1
    s2v = None
    if s2 is not None:
        if isinstance(s2, tuple):
            s2v = s2[0]
            reads.append(s2[1])
        else:
            s2v = s2

    def fn(e):
        if op1 is None:
            return e.tensor_scalar(oap, iap, s1v, None, op0)
        return e.tensor_scalar(oap, iap, s1v, s2v, op0, op1)
    k.P.add("dve", fn, reads=reads, writes=[orect], tag=tag)


def copy(k, out_rv, in_rv, engine="dve", tag=""):
    (oap, orect), (iap, irect) = out_rv, in_rv
    if engine == "dve":
        k.P.add("dve", lambda e: e.tensor_copy(oap, iap),
                reads=[irect], writes=[orect], tag=tag)
    else:
        k.P.add("act", lambda e: e.activation(oap, iap, AF.Copy),
                reads=[irect], writes=[orect], tag=tag)


def reduce(k, out_rv, in_rv, op, negate=False, tag=""):
    (oap, orect), (iap, irect) = out_rv, in_rv
    k.P.add("dve",
            lambda e: e.tensor_reduce(oap, iap, axis=AXL.X, op=op,
                                      negate=negate),
            reads=[irect], writes=[orect], tag=tag)


def recip(k, out_rv, in_rv, tag=""):
    (oap, orect), (iap, irect) = in_rv if False else (out_rv, in_rv)
    k.P.add("dve", lambda e: e.reciprocal(oap, iap),
            reads=[irect], writes=[orect], tag=tag)


def memset(k, out_rv, val, tag=""):
    (oap, orect) = out_rv
    k.P.add("dve", lambda e: e.memset(oap, val), reads=[], writes=[orect],
            tag=tag)


def dma(k, queue, out_rv, in_rv, tag=""):
    (oap, orect), (iap, irect) = out_rv, in_rv
    if iap.ndim == 3 and oap.ndim == 2:
        oap = oap.rearrange("p (c t) -> p c t", c=iap.shape[1])
    elif oap.ndim == 3 and iap.ndim == 2:
        iap = iap.rearrange("p (c t) -> p c t", c=oap.shape[1])
    k.P.add(QUEUE_ENGINE[queue], lambda e: e.dma_start(out=oap, in_=iap),
            reads=[irect], writes=[orect], queue=queue, tag=tag)


def collective(k, kind, op, in_rv, out_rv, tag=""):
    (iap, irect), (oap, orect) = in_rv, out_rv
    rg = [list(range(NCORES))]
    k.P.add("pool",
            lambda e: e.collective_compute(kind, op, replica_groups=rg,
                                           ins=[iap.opt()], outs=[oap.opt()]),
            reads=[irect], writes=[orect], queue="qcc", tag=tag)


def bcast_row(k, psum_tile, row_rv, n, sb_out, tag="", bf=False):
    ones = k.ones_row_bf if bf else k.ones_row_f32
    mm(k, psum_tile.rv(0, 128, 0, n), ones.rv(), row_rv, tag=tag)
    copy(k, sb_out.rv(0, 128, 0, n), psum_tile.rv(0, 128, 0, n), tag=tag)
    return sb_out



def emit_ln(k, gate=False):
    for b in range(2):
        c0 = b * BLK
        p_sx, p_sq = k.pb[4], k.pb[5]
        for c in range(NCH):
            xc = k.x_res.rv(0, 128, c * TOK + c0, c * TOK + c0 + BLK)
            if gate:
                sq = k.scr[c % 2]
                act(k, sq.rv(0, 128, 0, BLK), xc, AF.Square, tag="ln_sq")
                mm(k, p_sx.rv(0, 1, 0, BLK), k.ones_col_f32.rv(), xc,
                   start=(c == 0), stop=(c == NCH - 1), tag="ln_sx")
                mm(k, p_sq.rv(0, 1, 0, BLK), k.ones_col_f32.rv(),
                   sq.rv(0, 128, 0, BLK),
                   start=(c == 0), stop=(c == NCH - 1), tag="ln_sq2")
            else:
                xb = k.scrb[(2 * c) % 4]
                sqb = k.scrb[(2 * c + 1) % 4]
                act(k, xb.rv(), xc, AF.Copy, tag="ln_cpb")
                act(k, sqb.rv(), xc, AF.Square, tag="ln_sqb")
                mm(k, p_sx.rv(0, 1, 0, BLK), k.ones_col_bf.rv(), xb.rv(),
                   start=(c == 0), stop=(c == NCH - 1), tag="ln_sx")
                mm(k, p_sq.rv(0, 1, 0, BLK), k.ones_col_bf.rv(), sqb.rv(),
                   start=(c == 0), stop=(c == NCH - 1), tag="ln_sq2")
        mu, ssn, var, rstd = k.rows[0], k.rows[1], k.rows[2], k.rows[3]
        act(k, mu.rv(0, 1, 0, BLK), p_sx.rv(0, 1, 0, BLK), AF.Copy,
            scale=1.0 / D, tag="ln")
        act(k, ssn.rv(0, 1, 0, BLK), p_sq.rv(0, 1, 0, BLK), AF.Copy,
            scale=1.0 / D, tag="ln")
        act(k, var.rv(0, 1, 0, BLK), mu.rv(0, 1, 0, BLK), AF.Square,
            tag="ln")
        tt(k, var.rv(0, 1, 0, BLK), ssn.rv(0, 1, 0, BLK),
           var.rv(0, 1, 0, BLK), ALU.subtract, tag="ln")
        act(k, rstd.rv(0, 1, 0, BLK), var.rv(0, 1, 0, BLK), AF.Sqrt,
            bias=(k.eps_col.ap(0, 1, 0, 1), k.eps_col.rect(0, 1, 0, 1)),
            tag="ln")
        recip(k, rstd.rv(0, 1, 0, BLK), rstd.rv(0, 1, 0, BLK), tag="ln")
        tt(k, mu.rv(0, 1, 0, BLK), mu.rv(0, 1, 0, BLK),
           rstd.rv(0, 1, 0, BLK), ALU.mult, tag="ln")
        if gate:
            bc_r = bcast_row(k, k.pb[4], rstd.rv(0, 1, 0, BLK), BLK, k.bc1)
            bc_m = bcast_row(k, k.pb[5], mu.rv(0, 1, 0, BLK), BLK, k.bc2)
        else:
            copy(k, k.rowsb[0].rv(), rstd.rv(0, 1, 0, BLK), tag="ln_rb")
            copy(k, k.rowsb[1].rv(), mu.rv(0, 1, 0, BLK), tag="ln_mb")
            bc_r = bcast_row(k, k.pb[4], k.rowsb[0].rv(), BLK, k.bc1,
                             bf=True)
            bc_m = bcast_row(k, k.pb[5], k.rowsb[1].rv(), BLK, k.bc2,
                             bf=True)
        for c in range(NCH):
            xc = k.x_res.rv(0, 128, c * TOK + c0, c * TOK + c0 + BLK)
            t = k.scr[c % 2]
            tt(k, t.rv(0, 128, 0, BLK), xc, bc_r.rv(0, 128, 0, BLK),
               ALU.mult, tag="ln_ap1")
            lnc = k.ln_x.rv(0, 128, c * TOK + c0, c * TOK + c0 + BLK)
            if not gate:
                tt(k, lnc, t.rv(0, 128, 0, BLK), bc_m.rv(0, 128, 0, BLK),
                   ALU.subtract, tag="ln_ap2")
            else:
                lf = k.scr[2]
                tt(k, lf.rv(0, 128, 0, BLK), t.rv(0, 128, 0, BLK),
                   bc_m.rv(0, 128, 0, BLK), ALU.subtract, tag="ln_ap2")
                copy(k, lnc, lf.rv(0, 128, 0, BLK), tag="ln_cp")
                mm(k, k.pb[6].rv(0, 8, 0, BLK),
                   k.gate_sb.rv(0, 128, c * 8, (c + 1) * 8),
                   lf.rv(0, 128, 0, BLK),
                   start=(c == 0), stop=(c == NCH - 1), tag="gate_mm")
        if gate:
            act(k, k.sc_fm[b].rv(0, 8, 0, BLK), k.pb[6].rv(0, 8, 0, BLK),
                AF.Sigmoid, tag="gate_sig")


def emit_moe(k, W):
    A = k.arena
    A.mark()
    k.gate_sb = A.alloc(128, NCH * 8, F32)
    dma(k, "qsp", k.gate_sb.rv(), W["gw"])
    ebr = A.alloc(1, 8, F32)
    dma(k, "qsp", ebr.rv(), W["eb"])
    k.sel8 = A.alloc(8, 1024, BF16)
    dma(k, "qsp", k.sel8.rv(), W["sel8"])
    k.ebias_bc = A.alloc(128, 8, F32)
    bcast_row(k, k.pb[6], ebr.rv(0, 1, 0, 8), 8, k.ebias_bc)
    k.ln_x = A.alloc(128, NCH * TOK, BF16)
    k.sc_fm = [A.alloc(8, BLK, F32) for _ in range(2)]
    emit_ln(k, gate=True)

    scale_fm = [A.alloc(8, BLK, BF16) for _ in range(2)]
    sc_tm = A.alloc(128, 4 * 8, F32)
    bt = A.alloc(128, 8, F32)
    selt = A.alloc(128, 8, F32)
    m1 = A.alloc(128, 1, F32)
    m2 = A.alloc(128, 1, F32)
    for b in range(2):
        for j in range(4):
            transpose(k, k.pb[6].rv(0, 128, j * 8, (j + 1) * 8),
                      k.sc_fm[b].rv(0, 8, j * 128, (j + 1) * 128),
                      k.ident.rv(0, 8, 0, 8), tag="sc_T")
        copy(k, sc_tm.rv(), k.pb[6].rv(0, 128, 0, 32), tag="sc_T_ev")
        for j in range(4):
            stj = sc_tm.rv(0, 128, j * 8, (j + 1) * 8)
            tt(k, bt.rv(), stj, k.ebias_bc.rv(), ALU.add, tag="top2")
            reduce(k, m1.rv(), bt.rv(), ALU.max, tag="top2")
            ts(k, selt.rv(), bt.rv(), (m1.ap(), m1.rect()), ALU.is_ge,
               1e30, ALU.mult, tag="top2")
            tt(k, selt.rv(), bt.rv(), selt.rv(), ALU.subtract, tag="top2")
            reduce(k, m2.rv(), selt.rv(), ALU.max, tag="top2")
            ts(k, selt.rv(), bt.rv(), (m2.ap(), m2.rect()), ALU.is_ge,
               tag="top2")
            tt(k, selt.rv(), selt.rv(), stj, ALU.mult, tag="top2")
            transpose(k, k.pb[6].rv(0, 8, j * 128, (j + 1) * 128),
                      selt.rv(), k.ident.rv(), tag="sc_Tb")
        copy(k, scale_fm[b].rv(), k.pb[6].rv(0, 8, 0, BLK), tag="sc_Tb_ev")

    he = A.alloc(128, 8 * TOK, BF16)
    w1t = [A.alloc(128, NCH * 128, BF16) for _ in range(3)]
    w3t = [A.alloc(128, NCH * 128, BF16) for _ in range(3)]
    w2t = [A.alloc(128, 8 * 128, BF16) for _ in range(3)]
    sbc = [A.alloc(128, BLK, F32) for _ in range(4)]
    t1 = [A.alloc(128, BLK, BF16) for _ in range(2)]
    t3 = [A.alloc(128, BLK, BF16) for _ in range(2)]
    s1t = [A.alloc(128, BLK, BF16) for _ in range(2)]

    def expert(e, w1_rv, w3_rv, w2_rv, routed):
        if routed:
            for b in range(2):
                mm(k, k.pb[6].rv(0, 128, 0, BLK),
                   k.sel8.rv(0, 8, e * 128, (e + 1) * 128),
                   scale_fm[b].rv(0, 8, 0, BLK), tag="sbc_mm")
                copy(k, sbc[(e % 2) * 2 + b].rv(),
                     k.pb[6].rv(0, 128, 0, BLK),
                     tag="sbc_ev")
        for ht in range(8):
            dma(k, "qpool", w1t[ht % 3].rv(), w1_rv[ht], tag="w1dma")
            dma(k, "qpool", w3t[ht % 3].rv(), w3_rv[ht], tag="w3dma")
            for b in range(2):
                c0 = b * BLK
                u = ht * 2 + b
                ph1 = k.pb[u % 3]
                ph3 = k.pb[3 + u % 3]
                for dc in range(NCH):
                    rhs = k.ln_x.rv(0, 128, dc * TOK + c0,
                                    dc * TOK + c0 + BLK)
                    mm(k, ph1.rv(),
                       w1t[ht % 3].rv(0, 128, dc * 128, (dc + 1) * 128),
                       rhs, start=(dc == 0), stop=(dc == NCH - 1), tag="h1")
                    mm(k, ph3.rv(),
                       w3t[ht % 3].rv(0, 128, dc * 128, (dc + 1) * 128),
                       rhs, start=(dc == 0), stop=(dc == NCH - 1), tag="h3")
                hslot = he.rv(0, 128, ht * TOK + c0, ht * TOK + c0 + BLK)
                if routed:
                    tt(k, t1[u % 2].rv(), ph1.rv(),
                       sbc[(e % 2) * 2 + b].rv(), ALU.mult,
                       tag="h1ev")
                    tt(k, t3[u % 2].rv(), ph3.rv(),
                       sbc[(e % 2) * 2 + b].rv(), ALU.mult,
                       tag="h3ev")
                    act(k, s1t[u % 2].rv(), t1[u % 2].rv(), AF.Silu,
                        tag="silu")
                    tt(k, hslot, s1t[u % 2].rv(), t3[u % 2].rv(), ALU.mult,
                       tag="he")
                else:
                    act(k, s1t[u % 2].rv(), ph1.rv(), AF.Silu, tag="silu")
                    copy(k, t3[u % 2].rv(), ph3.rv(), tag="h3ev")
                    tt(k, hslot, s1t[u % 2].rv(), t3[u % 2].rv(), ALU.mult,
                       tag="he")
        for ot in range(NCH):
            dma(k, "qpool", w2t[ot % 3].rv(), w2_rv[ot], tag="w2dma")
            for b in range(2):
                c0 = b * BLK
                v = ot * 2 + b
                po = k.pb[4 + v % 4]
                for hc in range(8):
                    mm(k, po.rv(),
                       w2t[ot % 3].rv(0, 128, hc * 128, (hc + 1) * 128),
                       he.rv(0, 128, hc * TOK + c0, hc * TOK + c0 + BLK),
                       start=(hc == 0), stop=(hc == 7), tag="w2")
                tmp = k.scr[v % 3]
                act(k, tmp.rv(0, 128, 0, BLK), po.rv(), AF.Copy, scale=0.5,
                    tag="oev")
                xc = k.x_res.rv(0, 128, ot * TOK + c0, ot * TOK + c0 + BLK)
                tt(k, xc, tmp.rv(0, 128, 0, BLK), xc, ALU.add, tag="oadd")

    for e in range(8):
        expert(e, W["w1"][e], W["w3"][e], W["w2"][e], True)
    expert(1, W["sw1"], W["sw3"], W["sw2"], False)
    A.release()


def _rope(k, dst_rv, psum, rot, ctab, stab, par=0):
    sbqr, sbta, sbtb = k.ropes[par % 2]
    pbr = k.pb[6 + (par % 2)]
    act(k, sbqr.rv(), psum.rv(), AF.Copy, tag="rope_ev")
    mm(k, pbr.rv(), rot.rv(), sbqr.rv(), tag="rope_mm")
    tt(k, sbta.rv(), sbqr.rv(), ctab.rv(0, 128, 0, BLK), ALU.mult,
       tag="rope_c")
    tt(k, sbtb.rv(), pbr.rv(), stab.rv(0, 128, 0, BLK), ALU.mult,
       tag="rope_s")
    tt(k, dst_rv, sbta.rv(), sbtb.rv(), ALU.add, tag="rope_a")


def emit_kv0(k, W):
    A = k.arena
    A.mark()
    rot0 = A.alloc(128, 128, BF16)
    bk_sb = A.alloc(1, 512, BF16)
    bv_sb = A.alloc(1, 512, BF16)
    dma(k, "qsp", rot0.rv(), W["rot0"])
    dma(k, "qsp", bk_sb.rv(), W["bk"])
    dma(k, "qsp", bv_sb.rv(), W["bv"])
    kc = A.alloc(128, BLK, BF16)
    ks = A.alloc(128, BLK, BF16)
    dma(k, "qsp", kc.rv(), W["r0kc"])
    dma(k, "qsp", ks.rv(), W["r0ks"])
    k.ln_x = A.alloc(128, NCH * TOK, BF16)
    emit_ln(k)
    wkv = [A.alloc(128, NCH * 512, BF16) for _ in range(2)]
    cst = [A.alloc(128, BLK, BF16) for _ in range(2)]
    dma(k, "qpool", wkv[0].rv(), W["wk"], tag="wk")
    dma(k, "qact", wkv[1].rv(), W["wv"], tag="wv")
    u = 0
    for s in range(2):
        c0 = s * BLK
        for kvh in range(4):
            pq = k.pb[u % 3]
            for dc in range(NCH):
                mm(k, pq.rv(),
                   wkv[0].rv(0, 128, dc * 512 + kvh * 128,
                              dc * 512 + (kvh + 1) * 128),
                   k.ln_x.rv(0, 128, dc * TOK + c0, dc * TOK + c0 + BLK),
                   start=(dc == 0), stop=False, tag="kproj")
            mm(k, pq.rv(), bk_sb.rv(0, 1, kvh * 128, (kvh + 1) * 128),
               k.ones512_bf.rv(), start=False, stop=True, tag="kbias")
            _rope(k, cst[u % 2].rv(), pq, rot0, kc, ks, par=u)
            _kmax_tile(k, cst[u % 2].rv(), u)
            rr = s * 1024 + kvh * 128
            dma(k, "qsp", (k.kv0_out[rr:rr + 128, :],
                           drect("kv0o", rr, rr + 128)),
                cst[u % 2].rv(), tag="kstore")
            u += 1
    _kmax_out(k, 8, k.kmax0_out, "kmx0o")
    u = 0
    for s in range(2):
        c0 = s * BLK
        for tt_i in range(4):
            pv = k.pb[u % 3]
            for dc in range(NCH):
                mm(k, pv.rv(),
                   k.ln_x.rv(0, 128, dc * TOK + c0 + tt_i * 128,
                             dc * TOK + c0 + (tt_i + 1) * 128),
                   wkv[1].rv(0, 128, dc * 512, (dc + 1) * 512),
                   start=(dc == 0), stop=False, tag="vproj")
            mm(k, pv.rv(), k.ones_row_bf.rv(), bv_sb.rv(0, 1, 0, 512),
               start=False, stop=True, tag="vbias")
            act(k, cst[u % 2].rv(), pv.rv(), AF.Copy, tag="vev")
            rr = s * 1024 + 512 + tt_i * 128
            dma(k, "qsp", (k.kv0_out[rr:rr + 128, :],
                           drect("kv0o", rr, rr + 128)),
                cst[u % 2].rv(), tag="vstore")
            u += 1
    A.release()


def emit_kv1(k, W):
    A = k.arena
    A.mark()
    rot1 = A.alloc(128, 128, BF16)
    dma(k, "qsp", rot1.rv(), W["rot1"])
    kc = A.alloc(128, BLK, BF16)
    ks = A.alloc(128, BLK, BF16)
    dma(k, "qsp", kc.rv(), W["r1kc"])
    dma(k, "qsp", ks.rv(), W["r1ks"])
    k.ln_x = A.alloc(128, NCH * TOK, BF16)
    emit_ln(k)
    wfull = A.alloc(128, NCH * D, BF16)
    cst = [A.alloc(128, BLK, BF16) for _ in range(2)]
    dma(k, "qpool", wfull.rv(0, 128, 0, NCH * D // 2),
        W["l1wk_h"][0], tag="l1wk")
    dma(k, "qpool", wfull.rv(0, 128, NCH * D // 2, NCH * D),
        W["l1wk_h"][1], tag="l1wk")
    u = 0
    for s in range(2):
        c0 = s * BLK
        for oc in range(NCH):
            pq = k.pb[u % 3]
            for dc in range(NCH):
                mm(k, pq.rv(),
                   wfull.rv(0, 128, dc * D + oc * 128,
                            dc * D + (oc + 1) * 128),
                   k.ln_x.rv(0, 128, dc * TOK + c0, dc * TOK + c0 + BLK),
                   start=(dc == 0), stop=(dc == NCH - 1), tag="l1k")
            _rope(k, cst[u % 2].rv(), pq, rot1, kc, ks, par=u)
            _kmax_tile(k, cst[u % 2].rv(), u)
            rr = s * 4096 + oc * 128
            dma(k, "qsp", (k.kv1_out[rr:rr + 128, :],
                           drect("kv1o", rr, rr + 128)),
                cst[u % 2].rv(), tag="l1kstore")
            u += 1
    _kmax_out(k, 32, k.kmax1_out, "kmx1o")
    dma(k, "qpool", wfull.rv(0, 128, 0, NCH * D // 2),
        W["l1wv_h"][0], tag="l1wv")
    dma(k, "qpool", wfull.rv(0, 128, NCH * D // 2, NCH * D),
        W["l1wv_h"][1], tag="l1wv")
    u = 0
    for s in range(2):
        c0 = s * BLK
        for vg in range(4):
            for tt_i in range(4):
                pv = k.pb[u % 3]
                for dc in range(NCH):
                    mm(k, pv.rv(),
                       k.ln_x.rv(0, 128, dc * TOK + c0 + tt_i * 128,
                                 dc * TOK + c0 + (tt_i + 1) * 128),
                       wfull.rv(0, 128, dc * D + vg * 512,
                                dc * D + (vg + 1) * 512),
                       start=(dc == 0), stop=(dc == NCH - 1), tag="l1v")
                act(k, cst[u % 2].rv(), pv.rv(), AF.Copy, tag="l1vev")
                rr = s * 4096 + 2048 + vg * 512 + tt_i * 128
                dma(k, "qsp", (k.kv1_out[rr:rr + 128, :],
                               drect("kv1o", rr, rr + 128)),
                    cst[u % 2].rv(), tag="l1vstore")
                u += 1
    A.release()


def emit_l0attn(k, W):
    A = k.arena
    A.mark()
    q_sb = A.alloc(128, NCH * TOK, BF16)
    qpos = A.alloc(1, BLK, F32)
    kpos = A.alloc(128, 32, F32)
    dma(k, "qsp", qpos.rv(), W["qpos"])
    dma(k, "qsp", kpos.rv(), W["kpos"])
    A.mark()
    rot0 = A.alloc(128, 128, BF16)
    dma(k, "qsp", rot0.rv(), W["rot0"])
    r0 = {}
    k.ln_x = A.alloc(128, NCH * TOK, BF16)
    emit_ln(k)
    ln_own = k.ln_x
    bq_sb = A.alloc(1, D, BF16)
    dma(k, "qsp", bq_sb.rv(), W["bq"])
    for n in ("r0qc", "r0qs"):
        r0[n] = A.alloc(128, BLK, BF16)
        dma(k, "qsp", r0[n].rv(), W[n])
    wst = [A.alloc(128, NCH * 128, BF16) for _ in range(3)]
    for oc in range(NCH):
        dma(k, "qpool" if oc % 2 == 0 else "qact",
            wst[oc % 3].rv(), W["wq"][oc], tag="wqdma")
        for s in range(2):
            u = oc * 2 + s
            pq = k.pb[u % 3]
            for dc in range(NCH):
                mm(k, pq.rv(),
                   wst[oc % 3].rv(0, 128, dc * 128, (dc + 1) * 128),
                   ln_own.rv(0, 128, dc * TOK + s * BLK,
                             dc * TOK + s * BLK + BLK),
                   start=(dc == 0), stop=False, tag="qproj")
            mm(k, pq.rv(), bq_sb.rv(0, 1, oc * 128, (oc + 1) * 128),
               k.ones512_bf.rv(), start=False, stop=True, tag="qbias")
            _rope(k, q_sb.rv(0, 128, oc * TOK + s * BLK,
                             oc * TOK + s * BLK + BLK),
                  pq, rot0, r0["r0qc"], r0["r0qs"], par=u)
    A.release()

    A.mark()
    kmx = A.alloc(1, 8, F32)
    dma(k, "qsp", kmx.rv(), W["kmx0"])
    emit_qbound(k, q_sb, kmx)
    masks = A.alloc(128, 32 * BLK, BF16)
    qpb = k.scr[2]
    bcast_row(k, k.pb[6], qpos.rv(0, 1, 0, BLK), BLK, qpb, tag="qp_bc")
    for kt in range(32):
        ts(k, masks.rv(0, 128, kt * BLK, (kt + 1) * BLK),
           qpb.rv(0, 128, 0, BLK),
           (kpos.ap(0, 128, kt, kt + 1), kpos.rect(0, 128, kt, kt + 1)),
           ALU.is_ge, tag="mask")
    attn_s = A.alloc(128, NCH * BLK, BF16)
    ksb = A.alloc(128, 4096, BF16)
    vsb = A.alloc(128, 4096, BF16)
    pt = [A.alloc(128, BLK, BF16) for _ in range(3)]
    wst2 = [A.alloc(128, NCH * 128, BF16) for _ in range(2)]
    sbanks = [k.pb[0], k.pb[1], k.pb[2]]
    spar = 0
    for s in range(2):
        for kvh in range(4):
            for r in range(NCORES):
                rr = r * 2048 + s * 1024 + kvh * 128
                dma(k, "qsp", ksb.rv(0, 128, r * 512, (r + 1) * 512),
                    (k.kv0_g[rr:rr + 128, :], drect("kv0_g", rr, rr + 128)),
                    tag="kload")
                r1 = r * 2048 + s * 1024 + 512
                dma(k, "qsp", vsb.rv(0, 128, r * 512, (r + 1) * 512),
                    (k.kv0_g[r1:r1 + 512, kvh * 128:(kvh + 1) * 128]
                     .rearrange("(a p) v -> p a v", p=128),
                     drect("kv0_g", r1, r1 + 512)),
                    tag="vload")
            for g in range(4):
                qh = kvh * 4 + g
                qrv = q_sb.rv(0, 128, qh * TOK + s * BLK,
                              qh * TOK + s * BLK + BLK)
                p_o = k.pb[3] if spar % 2 == 0 else k.pb[5]
                p_l = k.pb[4] if spar % 2 == 0 else k.pb[6]
                _strip(k, qrv, ksb, vsb, 0, 128, masks, pt,
                       attn_s.rv(0, 128, qh * BLK, (qh + 1) * BLK),
                       p_o, p_l, sbanks, par=spar)
                spar += 1
        for oc in range(NCH):
            dma(k, "qpool" if oc % 2 == 0 else "qact",
                wst2[oc % 2].rv(), W["wo"][oc], tag="wodma")
            po = k.pb[oc % 4]
            for ac_ in range(NCH):
                mm(k, po.rv(),
                   wst2[oc % 2].rv(0, 128, ac_ * 128, (ac_ + 1) * 128),
                   attn_s.rv(0, 128, ac_ * BLK, (ac_ + 1) * BLK),
                   start=(ac_ == 0), stop=(ac_ == NCH - 1), tag="wo")
            xc = k.x_res.rv(0, 128, oc * TOK + s * BLK,
                            oc * TOK + s * BLK + BLK)
            tt(k, xc, po.rv(), xc, ALU.add, tag="wo_add")
    A.release()
    A.release()


def _kmax_tile(k, src_rv, slot):
    sqb = k.scrb[slot % 4]
    act(k, sqb.rv(), src_rv, AF.Square, tag="km_sq")
    pbk = k.pb[3 + slot % 3]
    mm(k, pbk.rv(0, 1, 0, BLK), k.ones_col_bf.rv(), sqb.rv(), tag="km_mm")
    reduce(k, k.rows[3].rv(0, 1, slot, slot + 1), pbk.rv(0, 1, 0, BLK),
           ALU.max, tag="km_red")


def _kmax_out(k, nslots, handle, name):
    reduce(k, k.rows[0].rv(0, 1, 0, 1), k.rows[3].rv(0, 1, 0, nslots),
           ALU.max, tag="km_fin")
    dma(k, "qsp", (handle[0:1, :], drect(name, 0, 1)),
        k.rows[0].rv(0, 1, 0, 1), tag="km_st")


def emit_qbound(k, q_sb, kmax_in):
    for c in range(32):
        sqb = k.scrb[c % 4]
        act(k, sqb.rv(), q_sb.rv(0, 128, c * BLK, (c + 1) * BLK),
            AF.Square, tag="qb_sq")
        pbq = k.pb[6 + c % 2]
        mm(k, pbq.rv(0, 1, 0, BLK), k.ones_col_bf.rv(), sqb.rv(),
           tag="qb_mm")
        reduce(k, k.rows[3].rv(0, 1, c, c + 1), pbq.rv(0, 1, 0, BLK),
               ALU.max, tag="qb_red")
    qm = k.rows[0]
    reduce(k, qm.rv(0, 1, 0, 1), k.rows[3].rv(0, 1, 0, 32), ALU.max,
           tag="qb_fin")
    km = k.rows[2]
    reduce(k, km.rv(0, 1, 0, 1), kmax_in.rv(0, 1, 0, 8), ALU.max,
           tag="qb_km")
    tt(k, qm.rv(0, 1, 0, 1), qm.rv(0, 1, 0, 1), km.rv(0, 1, 0, 1),
       ALU.mult, tag="qb_b")
    act(k, qm.rv(0, 1, 0, 1), qm.rv(0, 1, 0, 1), AF.Sqrt, tag="qb_b")
    ts(k, qm.rv(0, 1, 0, 1), qm.rv(0, 1, 0, 1), -1.0, ALU.mult, tag="qb_b")
    mm(k, k.pb[6].rv(0, 128, 0, 1), k.ones_row_f32.rv(), qm.rv(0, 1, 0, 1),
       tag="qb_bc")
    copy(k, k.nb_col.rv(), k.pb[6].rv(0, 128, 0, 1), tag="qb_ev")


def _strip(k, q_rv, ksb, vsb, p_off, kdim, masks, pt, out_rv, p_o, p_l,
           sbanks, par=0):
    p1 = p_off + kdim
    LA = len(sbanks)

    def s_mm(kt):
        mm(k, sbanks[kt % LA].rv(),
           ksb.rv(p_off, p1, kt * 128, (kt + 1) * 128), q_rv, tag="s_mm")

    for kt in range(LA - 1):
        s_mm(kt)
    for kt in range(32):
        if kt + LA - 1 < 32:
            s_mm(kt + LA - 1)
        p_s = sbanks[kt % LA]
        ptile = pt[kt % len(pt)]
        act(k, ptile.rv(), p_s.rv(), AF.Exp,
            bias=(k.nb_col.ap(), k.nb_col.rect()), tag="exp")
        if masks is not None:
            tt(k, ptile.rv(), ptile.rv(),
               masks.rv(0, 128, kt * BLK, (kt + 1) * BLK), ALU.mult,
               tag="pmask")
        mm(k, p_l.rv(0, 1, 0, BLK), k.ones_col_bf.rv(), ptile.rv(),
           start=(kt == 0), stop=(kt == 31), tag="l_mm")
        mm(k, p_o.rv(), vsb.rv(0, 128, kt * 128, (kt + 1) * 128),
           ptile.rv(), start=(kt == 0), stop=(kt == 31), tag="pv")
    if out_rv is None:
        return
    rl = k.rows[2 + (par % 2)]
    ts(k, rl.rv(0, 1, 0, BLK), p_l.rv(0, 1, 0, BLK), L_GUARD, ALU.add,
       tag="lg")
    recip(k, rl.rv(0, 1, 0, BLK), rl.rv(0, 1, 0, BLK), tag="rl")
    copy(k, k.rowsb[par % 2].rv(), rl.rv(0, 1, 0, BLK), tag="rl_b")
    bc = k.bc1 if par % 2 == 0 else k.bc2
    bcast_row(k, k.pb[7], k.rowsb[par % 2].rv(), BLK, bc, bf=True,
              tag="rl_bc")
    tt(k, out_rv, p_o.rv(), bc.rv(0, 128, 0, BLK), ALU.mult, tag="onorm")


def emit_l1ffn2(k, W):
    A = k.arena
    A.mark()
    k.ln_x = A.alloc(128, NCH * TOK, BF16)
    emit_ln(k)
    g1 = [k.scrb[0], k.scrb[1]]
    u1 = [k.scrb[2], k.scrb[3]]
    for half in range(2):
        A.mark()
        gated = A.alloc(128, 32 * TOK, BF16)
        A.mark()
        wg = [A.alloc(128, NCH * 128, BF16) for _ in range(2)]
        wu = [A.alloc(128, NCH * 128, BF16) for _ in range(2)]
        for sl in range(32):
            gsl = half * 32 + sl
            dma(k, "qpool", wg[sl % 2].rv(), W["wg"][gsl], tag="wgdma")
            dma(k, "qpool", wu[sl % 2].rv(), W["wu"][gsl], tag="wudma")
            for b in range(2):
                u = sl * 2 + b
                pg = k.pb[u % 3]
                pu = k.pb[3 + u % 3]
                for dc in range(NCH):
                    rhs = k.ln_x.rv(0, 128, dc * TOK + b * BLK,
                                    dc * TOK + b * BLK + BLK)
                    mm(k, pg.rv(),
                       wg[sl % 2].rv(0, 128, dc * 128, (dc + 1) * 128),
                       rhs, start=(dc == 0), stop=(dc == NCH - 1), tag="g")
                    mm(k, pu.rv(),
                       wu[sl % 2].rv(0, 128, dc * 128, (dc + 1) * 128),
                       rhs, start=(dc == 0), stop=(dc == NCH - 1), tag="u")
                act(k, g1[u % 2].rv(), pg.rv(), AF.Silu, tag="silu")
                copy(k, u1[u % 2].rv(), pu.rv(), tag="uev")
                tt(k, gated.rv(0, 128, sl * TOK + b * BLK,
                               sl * TOK + b * BLK + BLK),
                   g1[u % 2].rv(), u1[u % 2].rv(), ALU.mult, tag="gu")
        A.release()
        wd = [A.alloc(128, 32 * 128, BF16) for _ in range(2)]
        for ot in range(NCH):
            dma(k, "qpool", wd[ot % 2].rv(), W["wd"][half][ot],
                tag="wddma")
            for b in range(2):
                v = ot * 2 + b
                po = k.pb[4 + v % 4]
                for hc in range(32):
                    mm(k, po.rv(),
                       wd[ot % 2].rv(0, 128, hc * 128, (hc + 1) * 128),
                       gated.rv(0, 128, hc * TOK + b * BLK,
                                hc * TOK + b * BLK + BLK),
                       start=(hc == 0), stop=(hc == 31), tag="wd")
                tmp = k.scr[v % 3]
                act(k, tmp.rv(0, 128, 0, BLK), po.rv(), AF.Copy, scale=0.5,
                    tag="dev")
                xc = k.x_res.rv(0, 128, ot * TOK + b * BLK,
                                ot * TOK + b * BLK + BLK)
                tt(k, xc, tmp.rv(0, 128, 0, BLK), xc, ALU.add, tag="dadd")
        A.release()
    A.release()


def emit_l1attn(k, W):
    A = k.arena
    A.mark()
    rot1 = A.alloc(128, 128, BF16)
    dma(k, "qsp", rot1.rv(), W["rot1"])
    lamv = {}
    for n in ("lq1", "lk1", "lq2", "lk2"):
        lamv[n] = A.alloc(1, 64, F32)
        dma(k, "qsp", lamv[n].rv(), W[n])
    gam = A.alloc(128, 1, F32)
    dma(k, "qsp", gam.rv(), W["gamc"])
    lam = A.alloc(1, 1, F32)
    g1l = A.alloc(128, 1, F32)
    r = k.rows
    tt(k, r[0].rv(0, 1, 0, 64), lamv["lq1"].rv(), lamv["lk1"].rv(),
       ALU.mult, tag="lam")
    reduce(k, r[1].rv(0, 1, 0, 1), r[0].rv(0, 1, 0, 64), ALU.add, tag="lam")
    act(k, r[2].rv(0, 1, 0, 1), r[1].rv(0, 1, 0, 1), AF.Exp, tag="lam")
    tt(k, r[0].rv(0, 1, 0, 64), lamv["lq2"].rv(), lamv["lk2"].rv(),
       ALU.mult, tag="lam")
    reduce(k, r[1].rv(0, 1, 0, 1), r[0].rv(0, 1, 0, 64), ALU.add, tag="lam")
    act(k, r[3].rv(0, 1, 0, 1), r[1].rv(0, 1, 0, 1), AF.Exp, tag="lam")
    tt(k, lam.rv(0, 1, 0, 1), r[2].rv(0, 1, 0, 1), r[3].rv(0, 1, 0, 1),
       ALU.subtract, tag="lam")
    ts(k, lam.rv(0, 1, 0, 1), lam.rv(0, 1, 0, 1), LAMBDA_INIT, ALU.add,
       tag="lam")
    ts(k, g1l.rv(), gam.rv(), 1.0 - LAMBDA_INIT, ALU.mult, tag="gam")

    q_sb = A.alloc(128, NCH * TOK, BF16)
    A.mark()
    k.ln_x = A.alloc(128, NCH * TOK, BF16)
    emit_ln(k)
    ln_own = k.ln_x
    r1q = {}
    for n in ("r1qc", "r1qs"):
        r1q[n] = A.alloc(128, BLK, BF16)
        dma(k, "qsp", r1q[n].rv(), W[n])
    wst = [A.alloc(128, NCH * 128, BF16) for _ in range(3)]
    for oc in range(NCH):
        dma(k, "qpool", wst[oc % 3].rv(), W["l1wq"][oc], tag="l1wqdma")
        for s in range(2):
            u = oc * 2 + s
            pq = k.pb[u % 3]
            for dc in range(NCH):
                mm(k, pq.rv(),
                   wst[oc % 3].rv(0, 128, dc * 128, (dc + 1) * 128),
                   ln_own.rv(0, 128, dc * TOK + s * BLK,
                             dc * TOK + s * BLK + BLK),
                   start=(dc == 0), stop=(dc == NCH - 1), tag="l1q")
            _rope(k, q_sb.rv(0, 128, oc * TOK + s * BLK,
                             oc * TOK + s * BLK + BLK),
                  pq, rot1, r1q["r1qc"], r1q["r1qs"], par=u)
    A.release()

    A.mark()
    attn_fm = A.alloc(128, NCH * TOK, BF16)
    A.mark()
    kmx = A.alloc(1, 8, F32)
    dma(k, "qsp", kmx.rv(), W["kmx1"])
    emit_qbound(k, q_sb, kmx)
    ksb = A.alloc(128, 4096, BF16)
    vsb = A.alloc(128, 4096, BF16)
    pt = [A.alloc(128, BLK, BF16) for _ in range(3)]
    a1 = A.alloc(128, BLK, F32)
    a2 = A.alloc(128, BLK, F32)
    o1sb = A.alloc(128, BLK, BF16)
    o2sb = A.alloc(128, BLK, BF16)
    sbanks = [k.pb[0], k.pb[1], k.pb[2]]
    for s in range(2):
        for h in range(16):
            for rr in range(NCORES):
                r0_ = rr * 8192 + s * 4096 + h * 128
                dma(k, "qsp", ksb.rv(0, 128, rr * 512, (rr + 1) * 512),
                    (k.kv1_g[r0_:r0_ + 128, :],
                     drect("kv1_g", r0_, r0_ + 128)), tag="l1kload")
                g = h // 4
                co = (h % 4) * 128
                r1_ = rr * 8192 + s * 4096 + 2048 + g * 512
                dma(k, "qsp", vsb.rv(0, 128, rr * 512, (rr + 1) * 512),
                    (k.kv1_g[r1_:r1_ + 512, co:co + 128]
                     .rearrange("(a p) v -> p a v", p=128),
                     drect("kv1_g", r1_, r1_ + 512)), tag="l1vload")
            for comp in range(2):
                pof = comp * 64
                qrv = q_sb.rv(pof, pof + 64, h * TOK + s * BLK,
                              h * TOK + s * BLK + BLK)
                p_o = k.pb[3] if comp == 0 else k.pb[5]
                p_l = k.pb[4] if comp == 0 else k.pb[6]
                _strip(k, qrv, ksb, vsb, pof, 64, None, pt, None, p_o, p_l,
                       sbanks)
            rl1, rl2, sd = k.rows[0], k.rows[1], k.rows[2]
            act(k, o1sb.rv(), k.pb[3].rv(), AF.Copy, tag="l1oev")
            act(k, o2sb.rv(), k.pb[5].rv(), AF.Copy, tag="l1oev")
            ts(k, rl1.rv(0, 1, 0, BLK), k.pb[4].rv(0, 1, 0, BLK), L_GUARD,
               ALU.add, tag="lg1")
            recip(k, rl1.rv(0, 1, 0, BLK), rl1.rv(0, 1, 0, BLK), tag="l1rl")
            ts(k, rl2.rv(0, 1, 0, BLK), k.pb[6].rv(0, 1, 0, BLK), L_GUARD,
               ALU.add, tag="lg2")
            recip(k, rl2.rv(0, 1, 0, BLK), rl2.rv(0, 1, 0, BLK), tag="l1rl")
            ts(k, rl2.rv(0, 1, 0, BLK), rl2.rv(0, 1, 0, BLK),
               (lam.ap(0, 1, 0, 1), lam.rect(0, 1, 0, 1)), ALU.mult,
               tag="l1rl2")
            copy(k, k.rowsb[0].rv(), rl1.rv(0, 1, 0, BLK), tag="l1rb1")
            copy(k, k.rowsb[1].rv(), rl2.rv(0, 1, 0, BLK), tag="l1rb2")
            bcast_row(k, k.pb[7], k.rowsb[0].rv(), BLK, k.bc1, bf=True)
            bcast_row(k, k.pb[7], k.rowsb[1].rv(), BLK, k.bc2, bf=True)
            tt(k, a1.rv(), o1sb.rv(), k.bc1.rv(0, 128, 0, BLK), ALU.mult,
               tag="l1o1")
            tt(k, a2.rv(), o2sb.rv(), k.bc2.rv(0, 128, 0, BLK), ALU.mult,
               tag="l1o2")
            tt(k, a1.rv(), a1.rv(), a2.rv(), ALU.subtract, tag="l1sub")
            sqa = k.scrb[0]
            act(k, sqa.rv(), a1.rv(), AF.Square, tag="l1rms")
            mm(k, k.pb[7].rv(0, 1, 0, BLK), k.ones_col_bf.rv(), sqa.rv(),
               tag="l1rms")
            act(k, sd.rv(0, 1, 0, BLK), k.pb[7].rv(0, 1, 0, BLK), AF.Sqrt,
                scale=1.0 / 128.0,
                bias=(k.eps_col.ap(0, 1, 0, 1),
                      k.eps_col.rect(0, 1, 0, 1)), tag="l1rms")
            recip(k, sd.rv(0, 1, 0, BLK), sd.rv(0, 1, 0, BLK), tag="l1rms")
            copy(k, k.rowsb[0].rv(), sd.rv(0, 1, 0, BLK), tag="l1rb3")
            bcast_row(k, k.pb[7], k.rowsb[0].rv(), BLK, k.bc1, bf=True)
            tt(k, a1.rv(), a1.rv(), k.bc1.rv(0, 128, 0, BLK), ALU.mult,
               tag="l1rms2")
            ts(k, attn_fm.rv(0, 128, h * TOK + s * BLK,
                             h * TOK + s * BLK + BLK),
               a1.rv(), (g1l.ap(), g1l.rect()), ALU.mult, tag="l1gam")
    A.release()
    wst2 = [A.alloc(128, NCH * 128, BF16) for _ in range(3)]
    for oc in range(NCH):
        dma(k, "qpool", wst2[oc % 3].rv(), W["l1wo"][oc],
            tag="l1wodma")
        for b in range(2):
            v = oc * 2 + b
            po = k.pb[v % 4]
            for ac_ in range(NCH):
                mm(k, po.rv(),
                   wst2[oc % 3].rv(0, 128, ac_ * 128, (ac_ + 1) * 128),
                   attn_fm.rv(0, 128, ac_ * TOK + b * BLK,
                              ac_ * TOK + b * BLK + BLK),
                   start=(ac_ == 0), stop=(ac_ == NCH - 1), tag="l1wo")
            xc = k.x_res.rv(0, 128, oc * TOK + b * BLK,
                            oc * TOK + b * BLK + BLK)
            tt(k, xc, po.rv(), xc, ALU.add, tag="l1wo_add")
    A.release()
    A.release()



SUBLAYERS = ["moe", "l0attn", "moe", "l1ffn", "l1attn", "l1ffn"]
ARENA_BYTES = 212672


def build_program(schedule, in_dtype):
    nc = bass.Bass("TRN2", target_bir_lowering=False, debug=False)
    need = set(schedule)

    x_in = nc.dram_tensor("x", [D, TOK], in_dtype, kind="ExternalInput")
    y_out = nc.dram_tensor("y", [D, TOK], F32, kind="ExternalOutput")
    k_dbg = (nc.dram_tensor("dbg", [128, 32768], F32, kind="ExternalOutput")
             if DBG is not None else None)

    k = Ctx()
    k.P = Prog()
    k.dbg = k_dbg
    k.dbg_used = [0]
    din = {}

    def inp(name, shape, dt):
        din[name] = nc.dram_tensor(name, shape, dt, kind="ExternalInput")
        return din[name]

    inp("ident", [128, 128], F32)
    if "kv0" in need:
        inp("rot0", [128, 128], BF16)
        for n in ("r0kc", "r0ks"):
            inp(n, [128, BLK], BF16)
        inp("wk", [128, 8192], BF16)
        inp("wv", [128, 8192], BF16)
        inp("bk", [1, 512], BF16)
        inp("bv", [1, 512], BF16)
        k.kv0_out = nc.dram_tensor("kv0", [2048, 512], BF16,
                                   kind="ExternalOutput")
        k.kmax0_out = nc.dram_tensor("kmx0o", [1, 1], F32,
                                     kind="ExternalOutput")
    if "kv1" in need:
        if "rot1" not in din:
            inp("rot1", [128, 128], BF16)
        for n in ("r1kc", "r1ks"):
            inp(n, [128, BLK], BF16)
        inp("l1wk", [128, 32768], F32)
        inp("l1wv", [128, 32768], F32)
        k.kv1_out = nc.dram_tensor("kv1", [8192, 512], BF16,
                                   kind="ExternalOutput")
        k.kmax1_out = nc.dram_tensor("kmx1o", [1, 1], F32,
                                     kind="ExternalOutput")
    if "l0attn" in need:
        inp("qpos", [1, BLK], F32)
        inp("kpos", [128, 32], F32)
        if "rot0" not in din:
            inp("rot0", [128, 128], BF16)
        for n in ("r0qc", "r0qs"):
            inp(n, [128, BLK], BF16)
        inp("wq", [D, D], BF16)
        inp("wo", [D, D], BF16)
        inp("bq", [1, D], BF16)
        inp("kmx0", [1, 8], F32)
        k.kv0_g = nc.dram_tensor("kv0_g", [16384, 512], BF16,
                                 kind="ExternalInput")
    if "moe" in need:
        inp("gw", [128, 128], F32)
        inp("sel8", [8, 1024], BF16)
        inp("eb", [1, 8], F32)
        inp("w1", [8192, 2048], F32)
        inp("w3", [8192, 2048], F32)
        inp("w2", [16384, 1024], F32)
        inp("sw1", [1024, 2048], F32)
        inp("sw3", [1024, 2048], F32)
        inp("sw2", [2048, 1024], F32)
    if "l1ffn" in need:
        inp("wgt", [8192, 2048], F32)
        inp("wut", [8192, 2048], F32)
        inp("wdt", [2048, 8192], F32)
    if "l1attn" in need:
        if "rot1" not in din:
            inp("rot1", [128, 128], BF16)
        for n in ("r1qc", "r1qs"):
            inp(n, [128, BLK], BF16)
        inp("l1wq", [D, D], F32)
        inp("l1wo", [D, D], F32)
        inp("kmx1", [1, 8], F32)
        for n in ("lq1", "lk1", "lq2", "lk2"):
            inp(n, [1, 64], F32)
        inp("gamc", [128, 1], F32)
        k.kv1_g = nc.dram_tensor("kv1_g", [65536, 512], BF16,
                                 kind="ExternalInput")

    with (
        nc.sbuf_tensor("arena", [128, ARENA_BYTES // 4], F32) as arena_h,
        nc.psum_tensor("pb0", [128, 512], F32) as pb0,
        nc.psum_tensor("pb1", [128, 512], F32) as pb1,
        nc.psum_tensor("pb2", [128, 512], F32) as pb2,
        nc.psum_tensor("pb3", [128, 512], F32) as pb3,
        nc.psum_tensor("pb4", [128, 512], F32) as pb4,
        nc.psum_tensor("pb5", [128, 512], F32) as pb5,
        nc.psum_tensor("pb6", [128, 512], F32) as pb6,
        nc.psum_tensor("pb7", [128, 512], F32) as pb7,
        nc.semaphore("s_pe") as s_pe,
        nc.semaphore("s_dve") as s_dve,
        nc.semaphore("s_act") as s_act,
        nc.semaphore("s_pool") as s_pool,
        nc.semaphore("s_sp") as s_sp,
        nc.semaphore("q_sp") as q_sp,
        nc.semaphore("q_act") as q_act,
        nc.semaphore("q_pool") as q_pool,
        nc.semaphore("q_cc") as q_cc,
        nc.Block() as block,
    ):
        sems = {"pe": s_pe, "dve": s_dve, "act": s_act, "pool": s_pool,
                "sp": s_sp, "qsp": q_sp, "qact": q_act, "qpool": q_pool,
                "qcc": q_cc}
        A = Arena(arena_h, "S:arena", ARENA_BYTES)
        k.arena = A
        k.pb = [Tile(h, f"P:pb{i}", True, 0, 128, 0, 512, F32)
                for i, h in enumerate([pb0, pb1, pb2, pb3, pb4, pb5, pb6,
                                       pb7])]
        k.x_res = A.alloc(128, NCH * TOK, F32)
        k.scr = [A.alloc(128, BLK, F32) for _ in range(3)]
        k.scrb = [A.alloc(128, BLK, BF16) for _ in range(4)]
        k.rows = [A.alloc(1, BLK, F32) for _ in range(4)]
        k.rowsb = [A.alloc(1, BLK, BF16) for _ in range(2)]
        k.bc1 = A.alloc(128, BLK, F32)
        k.bc2 = A.alloc(128, BLK, F32)
        _sbta = A.alloc(128, BLK, BF16)
        _sbtb = A.alloc(128, BLK, BF16)
        k.ropes = [(A.alloc(128, BLK, BF16), _sbta, _sbtb)
                   for _ in range(2)]
        k.nb_col = A.alloc(128, 1, F32)
        k.eps_col = A.alloc(128, 1, F32)
        k.ones_col_f32 = A.alloc(128, 1, F32)
        k.ones_col_bf = A.alloc(128, 1, BF16)
        k.ones_row_f32 = A.alloc(1, 128, F32)
        k.ones_row_bf = A.alloc(1, 128, BF16)
        k.ones512_bf = A.alloc(1, BLK, BF16)
        k.ident = A.alloc(128, 128, F32)

        memset(k, k.eps_col.rv(), EPS)
        memset(k, k.ones_col_f32.rv(), 1.0)
        memset(k, k.ones_col_bf.rv(), 1.0)
        memset(k, k.ones_row_f32.rv(), 1.0)
        memset(k, k.ones_row_bf.rv(), 1.0)
        memset(k, k.ones512_bf.rv(), 1.0)
        dma(k, "qsp", k.ident.rv(), (din["ident"].ap(), drect("ident")))

        x_src = (x_in.ap().rearrange("(c p) t -> p c t", p=128),
                 drect("x"))
        if in_dtype == F32:
            dma(k, "qsp", k.x_res.rv(), x_src, tag="xload")
        else:
            dma(k, "qpool", k.x_res.rv(), x_src, tag="xload")

        W = {}
        for n in din:
            pass
        if "moe" in need:
            W["gw"] = (din["gw"].ap(), drect("gw"))
            W["eb"] = (din["eb"].ap(), drect("eb"))
            W["sel8"] = (din["sel8"].ap(), drect("sel8"))
            W["w1"] = [[(din["w1"][(e * 8 + ht) * 128:
                                   (e * 8 + ht + 1) * 128, :],
                         drect("w1", (e * 8 + ht) * 128,
                               (e * 8 + ht + 1) * 128))
                        for ht in range(8)] for e in range(8)]
            W["w3"] = [[(din["w3"][(e * 8 + ht) * 128:
                                   (e * 8 + ht + 1) * 128, :],
                         drect("w3", (e * 8 + ht) * 128,
                               (e * 8 + ht + 1) * 128))
                        for ht in range(8)] for e in range(8)]
            W["w2"] = [[(din["w2"][(e * 16 + ot) * 128:
                                   (e * 16 + ot + 1) * 128, :],
                         drect("w2", (e * 16 + ot) * 128,
                               (e * 16 + ot + 1) * 128))
                        for ot in range(NCH)] for e in range(8)]
            W["sw1"] = [(din["sw1"][ht * 128:(ht + 1) * 128, :],
                         drect("sw1", ht * 128, (ht + 1) * 128))
                        for ht in range(8)]
            W["sw3"] = [(din["sw3"][ht * 128:(ht + 1) * 128, :],
                         drect("sw3", ht * 128, (ht + 1) * 128))
                        for ht in range(8)]
            W["sw2"] = [(din["sw2"][ot * 128:(ot + 1) * 128, :],
                         drect("sw2", ot * 128, (ot + 1) * 128))
                        for ot in range(NCH)]
        if "kv0" in need:
            for n in ("rot0", "r0kc", "r0ks", "bk", "bv"):
                W[n] = (din[n].ap(), drect(n))
            W["wk"] = (din["wk"].ap(), drect("wk"))
            W["wv"] = (din["wv"].ap(), drect("wv"))
        if "kv1" in need:
            for n in ("rot1", "r1kc", "r1ks"):
                W[n] = (din[n].ap(), drect(n))
            W["l1wk_h"] = [(din["l1wk"][:, h * 16384:(h + 1) * 16384],
                            drect("l1wk")) for h in range(2)]
            W["l1wv_h"] = [(din["l1wv"][:, h * 16384:(h + 1) * 16384],
                            drect("l1wv")) for h in range(2)]
        if "l0attn" in need:
            for n in ("qpos", "kpos", "rot0", "r0qc", "r0qs", "bq",
                      "kmx0"):
                W[n] = (din[n].ap(), drect(n))
            W["wq"] = [(din["wq"][oc * 128:(oc + 1) * 128, :],
                        drect("wq", oc * 128, (oc + 1) * 128))
                       for oc in range(NCH)]
            W["wo"] = [(din["wo"][oc * 128:(oc + 1) * 128, :],
                        drect("wo", oc * 128, (oc + 1) * 128))
                       for oc in range(NCH)]
        if "l1ffn" in need:
            W["wg"] = [(din["wgt"][sl * 128:(sl + 1) * 128, :],
                        drect("wgt", sl * 128, (sl + 1) * 128))
                       for sl in range(64)]
            W["wu"] = [(din["wut"][sl * 128:(sl + 1) * 128, :],
                        drect("wut", sl * 128, (sl + 1) * 128))
                       for sl in range(64)]
            W["wd"] = [[(din["wdt"][ot * 128:(ot + 1) * 128,
                                    half * 4096:(half + 1) * 4096],
                         drect("wdt", ot * 128, (ot + 1) * 128))
                        for ot in range(NCH)]
                       for half in range(2)]
        if "l1attn" in need:
            for n in ("rot1", "r1qc", "r1qs", "lq1",
                      "lk1", "lq2", "lk2", "gamc", "kmx1"):
                W[n] = (din[n].ap(), drect(n))
            W["l1wq"] = [(din["l1wq"][oc * 128:(oc + 1) * 128, :],
                          drect("l1wq", oc * 128, (oc + 1) * 128))
                         for oc in range(NCH)]
            W["l1wo"] = [(din["l1wo"][oc * 128:(oc + 1) * 128, :],
                          drect("l1wo", oc * 128, (oc + 1) * 128))
                         for oc in range(NCH)]

        for name in schedule:
            if name == "moe":
                emit_moe(k, W)
            elif name == "kv0":
                emit_kv0(k, W)
            elif name == "kv1":
                emit_kv1(k, W)
            elif name == "l0attn":
                emit_l0attn(k, W)
            elif name == "l1ffn":
                emit_l1ffn2(k, W)
            elif name == "l1attn":
                emit_l1attn(k, W)

        dma(k, "qsp", (y_out.ap().rearrange("(c p) t -> p c t", p=128),
                       drect("y")), k.x_res.rv(), tag="ystore")

        final = k.P.finalize()
        nq = dict(k.P.qcount)

        def emit_engine(eng_obj, ename, tail=False):
            for op, ws in final[ename]:
                for sem, val in ws:
                    eng_obj.wait_ge(sems[sem], val)
                instr = op.fn(eng_obj)
                if op.queue is not None:
                    instr.then_inc(sems[op.queue], 16)
                elif op.need_inc:
                    instr.then_inc(sems[op.engine], 1)
            if tail:
                eng_obj.wait_ge(sems["qsp"], 16 * nq["qsp"])

        @block.sync
        def _(eng):
            emit_engine(eng, "sp", tail=True)

        @block.tensor
        def _(eng):
            emit_engine(eng, "pe")

        @block.vector
        def _(eng):
            emit_engine(eng, "dve")

        @block.scalar
        def _(eng):
            emit_engine(eng, "act")

        @block.gpsimd
        def _(eng):
            emit_engine(eng, "pool")

    return nc



def _rot_matrices():
    R0 = np.zeros((128, 128), dtype=np.float32)
    for d in range(64):
        R0[d, d + 64] = -1.0
        R0[d + 64, d] = 1.0
    R1 = np.zeros((128, 128), dtype=np.float32)
    for base in (0, 64):
        for d in range(32):
            R1[base + d, base + 32 + d] = -1.0
            R1[base + 32 + d, base + d] = 1.0
    return (np.ascontiguousarray(R0.T).astype(BF),
            np.ascontiguousarray(R1.T).astype(BF))


def _l0_cossin(pos):
    inv = 1.0 / (1e6 ** (np.arange(64, dtype=np.float64) / 64.0))
    ang = np.outer(pos, inv)
    ct = np.cos(ang).T
    st = np.sin(ang).T
    return np.vstack([ct, ct]), np.vstack([st, st])


def _rope_tables_l0(core):
    cos, sin = _l0_cossin(core * BLK + np.arange(BLK, dtype=np.float64))
    s = 1.0 / math.sqrt(128.0)
    return {"r0qc": (cos * s).astype(BF), "r0qs": (sin * s).astype(BF)}


def _ktables_l0(core):
    cos, sin = _l0_cossin(core * BLK + np.arange(BLK, dtype=np.float64))
    return {"r0kc": cos.astype(BF), "r0ks": sin.astype(BF)}


def _ktables_l1(core):
    cos, sin = _l1_cossin(core * BLK + np.arange(BLK, dtype=np.float64))
    return {"r1kc": cos.astype(BF), "r1ks": sin.astype(BF)}


def _l1_cossin(pos):
    angle = 1.0 / (10000.0 ** np.linspace(0.0, 1.0, 32))
    fr = pos[:, None] * angle[None, :]
    return np.tile(np.cos(fr).T, (4, 1)), np.tile(np.sin(fr).T, (4, 1))


def _rope_tables_l1(core):
    cos, sin = _l1_cossin(core * BLK + np.arange(BLK, dtype=np.float64))
    s = 1.0 / 8.0
    return {"r1qc": (cos * s).astype(BF), "r1qs": (sin * s).astype(BF)}


_PERM64 = np.concatenate([np.arange(0, 64, 2), np.arange(1, 64, 2)])


def _perm_cols(wt):
    out = wt.reshape(wt.shape[0], 32, 64)[:, :, _PERM64]
    return np.ascontiguousarray(out.reshape(wt.shape[0], 2048))


def _prep_weights(inputs):
    f32 = np.float32
    Wall = {}
    Wall["gw"] = np.ascontiguousarray(
        np.asarray(inputs["L0_ffn_gate_w"], f32).T)
    Wall["eb"] = np.asarray(inputs["L0_ffn_expert_bias"], f32)[None, :]
    Wall["w1"] = np.ascontiguousarray(
        np.asarray(inputs["L0_ffn_w1"], f32).transpose(0, 2, 1))
    Wall["w3"] = np.ascontiguousarray(
        np.asarray(inputs["L0_ffn_w3"], f32).transpose(0, 2, 1))
    Wall["w2"] = np.ascontiguousarray(
        np.asarray(inputs["L0_ffn_w2"], f32).transpose(0, 2, 1))
    Wall["sw1"] = np.ascontiguousarray(np.asarray(inputs["L0_ffn_sw1"],
                                                  f32).T)
    Wall["sw3"] = np.ascontiguousarray(np.asarray(inputs["L0_ffn_sw3"],
                                                  f32).T)
    Wall["sw2"] = np.ascontiguousarray(np.asarray(inputs["L0_ffn_sw2"],
                                                  f32).T)
    Wall["wq"] = np.ascontiguousarray(np.asarray(inputs["L0_attn_Wq"], BF))
    Wall["wk"] = np.ascontiguousarray(np.asarray(inputs["L0_attn_Wk"], BF))
    Wall["wv"] = np.ascontiguousarray(np.asarray(inputs["L0_attn_Wv"], BF))
    Wall["wo"] = np.ascontiguousarray(np.asarray(inputs["L0_attn_Wo"], BF))
    Wall["bq"] = np.asarray(inputs["L0_attn_bq"], BF)[None, :]
    Wall["bk"] = np.asarray(inputs["L0_attn_bk"], BF)[None, :]
    Wall["bv"] = np.asarray(inputs["L0_attn_bv"], BF)[None, :]
    Wall["wgt"] = np.ascontiguousarray(np.asarray(inputs["L1_ffn_w_gate"],
                                                  f32).T)
    Wall["wut"] = np.ascontiguousarray(np.asarray(inputs["L1_ffn_w_up"],
                                                  f32).T)
    Wall["wdt"] = np.ascontiguousarray(np.asarray(inputs["L1_ffn_w_down"],
                                                  f32).T)
    Wall["l1wq"] = _perm_cols(np.asarray(inputs["L1_attn_Wq"], f32).T)
    Wall["l1wk"] = _perm_cols(np.asarray(inputs["L1_attn_Wk"], f32).T)
    Wall["l1wv"] = np.ascontiguousarray(np.asarray(inputs["L1_attn_Wv"],
                                                   f32).T)
    Wall["l1wo"] = np.ascontiguousarray(np.asarray(inputs["L1_attn_Wo"],
                                                   f32).T)
    Wall["lq1"] = np.asarray(inputs["L1_attn_lambda_q1"], f32)[None, :]
    Wall["lk1"] = np.asarray(inputs["L1_attn_lambda_k1"], f32)[None, :]
    Wall["lq2"] = np.asarray(inputs["L1_attn_lambda_q2"], f32)[None, :]
    Wall["lk2"] = np.asarray(inputs["L1_attn_lambda_k2"], f32)[None, :]
    Wall["gamc"] = np.ascontiguousarray(
        np.asarray(inputs["L1_attn_gamma"], f32)[:, None])
    def _t4(a, C, P, O, J):
        return np.ascontiguousarray(
            a.reshape(C, P, O, J).transpose(2, 1, 0, 3)
            .reshape(O * P, C * J))

    def _t5(a, E, C, P, O, J):
        return np.ascontiguousarray(
            a.reshape(E, C, P, O, J).transpose(0, 3, 2, 1, 4)
            .reshape(E * O * P, C * J))

    def _tp(a):
        C = a.shape[0] // 128
        return np.ascontiguousarray(
            a.reshape(C, 128, a.shape[1]).transpose(1, 0, 2)
            .reshape(128, C * a.shape[1]))

    Wall["w1"] = _t5(Wall["w1"], 8, 16, 128, 8, 128)
    Wall["w3"] = _t5(Wall["w3"], 8, 16, 128, 8, 128)
    Wall["w2"] = _t5(Wall["w2"], 8, 8, 128, 16, 128)
    Wall["sw1"] = _t4(Wall["sw1"], 16, 128, 8, 128)
    Wall["sw3"] = _t4(Wall["sw3"], 16, 128, 8, 128)
    Wall["sw2"] = _t4(Wall["sw2"], 8, 128, 16, 128)
    Wall["wgt"] = _t4(Wall["wgt"], 16, 128, 64, 128)
    Wall["wut"] = _t4(Wall["wut"], 16, 128, 64, 128)
    Wall["wdt"] = _t4(Wall["wdt"], 64, 128, 16, 128)
    Wall["wq"] = _t4(Wall["wq"], 16, 128, 16, 128)
    Wall["wo"] = _t4(Wall["wo"], 16, 128, 16, 128)
    Wall["l1wq"] = _t4(Wall["l1wq"], 16, 128, 16, 128)
    Wall["l1wo"] = _t4(Wall["l1wo"], 16, 128, 16, 128)
    Wall["l1wk"] = _tp(Wall["l1wk"])
    Wall["l1wv"] = _tp(Wall["l1wv"])
    Wall["wk"] = _tp(Wall["wk"])
    Wall["wv"] = _tp(Wall["wv"])
    Wall["gw"] = _tp(Wall["gw"])
    rot0, rot1 = _rot_matrices()
    Wall["rot0"] = rot0
    Wall["rot1"] = rot1
    Wall["ident"] = np.eye(128, dtype=np.float32)
    Wall["kpos"] = (np.arange(32, dtype=np.float32)[None, :] * 128
                    + np.arange(128, dtype=np.float32)[:, None])
    sel8 = np.zeros((8, 1024), dtype=np.float32)
    for e in range(8):
        sel8[e, e * 128:(e + 1) * 128] = 1.0
    Wall["sel8"] = sel8.astype(BF)
    return Wall


_WNAMES = {
    "moe": ("gw", "eb", "w1", "w3", "w2", "sw1", "sw3", "sw2", "sel8"),
    "kv0": ("wk", "wv", "bk", "bv", "rot0"),
    "kv1": ("l1wk", "l1wv", "rot1"),
    "l0attn": ("wq", "wo", "bq", "rot0", "kpos"),
    "l1ffn": ("wgt", "wut", "wdt"),
    "l1attn": ("l1wq", "l1wo", "lq1", "lk1", "lq2", "lk2",
               "gamc", "rot1"),
}


_PROG_CACHE = {}


def _get_prog(group, dt_tag):
    key = (group, dt_tag)
    if key not in _PROG_CACHE:
        _PROG_CACHE[key] = build_program(list(group),
                                         BF16 if dt_tag == "bf16" else F32)
    return _PROG_CACHE[key]


def kernel(**inputs):
    x = np.asarray(inputs["L0_attn_x"], BF)
    Wall = _prep_weights(inputs)

    x_fm = []
    for c in range(NCORES):
        xc = np.concatenate([x[0, c * BLK:(c + 1) * BLK, :],
                             x[1, c * BLK:(c + 1) * BLK, :]], axis=0)
        x_fm.append(np.ascontiguousarray(xc.T))

    def in_map_for(core, schedule, x_arr, extra=None):
        m = {"x": x_arr, "ident": Wall["ident"]}
        need = set(schedule)
        if "l0attn" in need:
            m["qpos"] = (core * BLK
                         + np.arange(BLK, dtype=np.float32))[None, :]
            m.update(_rope_tables_l0(core))
        if "l1attn" in need:
            m.update(_rope_tables_l1(core))
        if "kv0" in need:
            m.update(_ktables_l0(core))
        if "kv1" in need:
            m.update(_ktables_l1(core))
        if extra:
            m.update(extra)
        for s in need:
            for n in _WNAMES[s]:
                m[n] = Wall[n]
        return m

    GROUPS = [["moe", "kv0"], ["l0attn", "moe", "l1ffn", "kv1"],
              ["l1attn", "l1ffn"]]
    cur = x_fm
    extra = None
    for gi, group in enumerate(GROUPS):
        nc = _get_prog(tuple(group), "bf16" if gi == 0 else "f32")
        in_maps = [in_map_for(c, group, cur[c], extra)
                   for c in range(NCORES)]
        res = run_bass_kernel_spmd(nc, in_maps, core_ids=list(range(NCORES)))
        cur = [np.ascontiguousarray(r["y"]) for r in res.results]
        if "kv0" in group:
            extra = {"kv0_g": np.concatenate(
                [np.asarray(r["kv0"]) for r in res.results], axis=0),
                "kmx0": np.concatenate(
                [np.asarray(r["kmx0o"]) for r in res.results], axis=1)}
        elif "kv1" in group:
            extra = {"kv1_g": np.concatenate(
                [np.asarray(r["kv1"]) for r in res.results], axis=0),
                "kmx1": np.concatenate(
                [np.asarray(r["kmx1o"]) for r in res.results], axis=1)}
        else:
            extra = None
    outs = cur

    out = np.empty((2, 4096, 2048), dtype=np.float32)
    for c in range(NCORES):
        yt = outs[c].T
        out[0, c * BLK:(c + 1) * BLK, :] = yt[:BLK]
        out[1, c * BLK:(c + 1) * BLK, :] = yt[BLK:]
    return out
